# version 32
# baseline (speedup 1.0000x reference)
import contextlib
import numpy as np
import ml_dtypes
import concourse.bass as bass
import concourse.mybir as mybir
from concourse.bass_utils import run_bass_kernel_spmd

F32 = mybir.dt.float32
BF16 = mybir.dt.bfloat16
AF = mybir.ActivationFunctionType
ALU = mybir.AluOpType
BF = ml_dtypes.bfloat16

N_CORES = 8
SEQS = 2
L = 2048
D = 1024
H = 16
DH = 64
NCK = L // 128
NTOK = SEQS * L
DF = 4096

DEBUG = ()

ENGINE_NAMES = ("tensor", "vector", "scalar", "gpsimd", "sync")


class Buf:
    __slots__ = ("name", "group", "last_writer", "readers", "semkey")

    def __init__(self, name, group=None):
        self.name = name
        self.group = group
        self.last_writer = None
        self.readers = []
        self.semkey = None


class Node:
    __slots__ = ("idx", "eng", "key", "fn", "deps", "is_dma", "needs_inc",
                 "inc_value", "vclock", "tag")

    def __init__(self, idx, eng, fn, tag):
        self.idx = idx
        self.eng = eng
        self.key = eng
        self.fn = fn
        self.deps = set()
        self.is_dma = False
        self.needs_inc = False
        self.inc_value = None
        self.vclock = None
        self.tag = tag


class Prog:
    def __init__(self):
        self.nodes = []
        self.per_engine = {e: [] for e in ENGINE_NAMES}
        self.group_keys = set()
        self.last_by_key = {}
        self._pending = {e: [] for e in ENGINE_NAMES}
        self.n_waits = 0
        self.n_incs = 0

    def buf(self, name, group=None):
        return Buf(name, group)

    def emit(self, eng, fn, reads=(), writes=(), extra_deps=(), tag=""):
        n = Node(len(self.nodes), eng, fn, tag)
        if self._pending[eng]:
            for d in self._pending[eng]:
                if d is not None and d is not n:
                    n.deps.add(d)
            self._pending[eng] = []
        for b in reads:
            if b.last_writer is not None:
                n.deps.add(b.last_writer)
            b.readers.append(n)
        for b in writes:
            for r in b.readers:
                if r is not n:
                    n.deps.add(r)
            if b.last_writer is not None and b.last_writer is not n:
                n.deps.add(b.last_writer)
            b.last_writer = n
            b.readers = [n] if n in b.readers else []
        for d in extra_deps:
            if d is not None and d is not n:
                n.deps.add(d)
        self.nodes.append(n)
        self.per_engine[eng].append(n)
        self.last_by_key[n.key] = n
        return n

    def dma(self, eng, fn, reads=(), writes=(), extra_deps=(), tag="dma"):
        n = self.emit(eng, fn, reads, writes, extra_deps, tag)
        n.is_dma = True
        b = writes[0]
        if b.semkey is None:
            b.semkey = ("g:" + b.group) if b.group else ("b:" + b.name)
            if b.group:
                self.group_keys.add(b.semkey)
        n.key = b.semkey
        self.last_by_key[n.key] = n
        return n

    def barrier(self):
        deps = list(self.last_by_key.values())
        for e in ENGINE_NAMES:
            self._pending[e] = list(deps)

    def build(self, nc, verbose=False):
        for n in self.nodes:
            for d in n.deps:
                if d.key != n.key or d.is_dma:
                    d.needs_inc = True
                elif d.eng == n.eng and n.eng != "tensor":
                    d.needs_inc = True
        for n in self.nodes:
            if n.is_dma:
                n.needs_inc = True
        sem_total = {}
        for n in self.nodes:
            if n.needs_inc:
                step = 16 if n.is_dma else 1
                sem_total[n.key] = sem_total.get(n.key, 0) + step
                n.inc_value = sem_total[n.key]
        last_dma_idx = {}
        for n in self.nodes:
            if n.is_dma and n.key in self.group_keys:
                last_dma_idx[n.key] = n.idx
        clocks = {e: {} for e in ENGINE_NAMES}
        waits = {}
        predrain = set()
        drained_upto = {e: 0 for e in ENGINE_NAMES}
        for n in self.nodes:
            ww = {}
            myclock = clocks[n.eng]
            for d in n.deps:
                if (n.is_dma and d.is_dma and d.key == n.key
                        and d.key in self.group_keys):
                    continue
                if d.eng == n.eng and not d.is_dma and n.eng == "tensor":
                    continue
                v = d.inc_value
                if d.key in self.group_keys:
                    v = sem_total[d.key]
                    assert last_dma_idx[d.key] < n.idx, \
                        f"group {d.key} DMA after waiter {n.tag}"
                if myclock.get(d.key, 0) >= v:
                    continue
                if ww.get(d.key, 0) < v:
                    ww[d.key] = v
            if ww:
                for pk, v in ww.items():
                    myclock[pk] = max(myclock.get(pk, 0), v)
                for d in n.deps:
                    if d.vclock and ww.get(d.key, 0) >= (d.inc_value or 0):
                        for k, kv in d.vclock.items():
                            if myclock.get(k, 0) < kv:
                                myclock[k] = kv
                waits[n] = sorted(ww.items())
                self.n_waits += len(ww)
            if n.needs_inc:
                vc = dict(clocks[n.eng])
                vc[n.key] = n.inc_value
                n.vclock = vc
                self.n_incs += 1
        self._wmap = waits
        self._predrain = predrain

        used_keys = sorted(set(n.key for n in self.nodes if n.needs_inc))
        with contextlib.ExitStack() as stack:
            sem_handles = {k: stack.enter_context(nc.semaphore(f"s_{k}"))
                           for k in used_keys}
            block = stack.enter_context(nc.Block())
            prog = self

            def mk_runner(ename):
                def run(engobj):
                    for n in prog.per_engine[ename]:
                        if n in prog._predrain:
                            engobj.drain()
                        w = prog._wmap.get(n)
                        if w:
                            for pk, v in w:
                                engobj.wait_ge(sem_handles[pk], v)
                        inst = n.fn(engobj)
                        if n.needs_inc:
                            inst.then_inc(sem_handles[n.key],
                                          16 if n.is_dma else 1)
                return run

            for e in ENGINE_NAMES:
                if self.per_engine[e]:
                    getattr(block, e)(mk_runner(e))
        if verbose:
            print(f"[fw] nodes={len(self.nodes)} incs={self.n_incs} "
                  f"waits={self.n_waits} drains={len(predrain)} "
                  f"sems={len(used_keys)}")
        return sem_total


def build_program(debug=(), verbose=False):
    nc = bass.Bass("TRN2", target_bir_lowering=False, debug=False)
    p = Prog()

    xin = nc.dram_tensor("xin", [NTOK, D], F32, kind="ExternalInput").ap()
    wqt = nc.dram_tensor("wqt", [D, D], F32, kind="ExternalInput").ap()
    wkt = nc.dram_tensor("wkt", [D, D], F32, kind="ExternalInput").ap()
    wvt = nc.dram_tensor("wvt", [D, D], F32, kind="ExternalInput").ap()
    wot = nc.dram_tensor("wot", [D, D], F32, kind="ExternalInput").ap()
    wwt = nc.dram_tensor("wwt", [D, H], F32, kind="ExternalInput").ap()
    wet = nc.dram_tensor("wet", [D, H], F32, kind="ExternalInput").ap()
    bwc = nc.dram_tensor("bwc", [H, 1], F32, kind="ExternalInput").ap()
    bec = nc.dram_tensor("bec", [H, 1], F32, kind="ExternalInput").ap()
    fct = nc.dram_tensor("fct", [D, DF], F32, kind="ExternalInput").ap()
    pjt = nc.dram_tensor("pjt", [DF, D], F32, kind="ExternalInput").ap()
    i128bf = nc.dram_tensor("i128bf", [128, 128], BF16, kind="ExternalInput").ap()
    i128f = nc.dram_tensor("i128f", [128, 128], F32, kind="ExternalInput").ap()
    i16f = nc.dram_tensor("i16f", [16, 16], F32, kind="ExternalInput").ap()
    mpre = nc.dram_tensor("mpre", [128, 128], F32, kind="ExternalInput").ap()
    ones16 = nc.dram_tensor("ones16", [16, 128], F32, kind="ExternalInput").ap()
    galg = nc.dram_tensor("galg", [16, 2], F32, kind="ExternalInput").ap()
    out_t = nc.dram_tensor("out", [NTOK, D], F32, kind="ExternalOutput").ap()
    x1scr = nc.dram_tensor("x1scr", [NTOK, D], BF16, kind="Internal").ap()

    dbg = {}
    for name, shape, dt in [("xn", [NTOK, D], BF16), ("qt", [NTOK, D], BF16),
                            ("kt", [NTOK, D], BF16), ("vaug", [NTOK, 1040], BF16),
                            ("cc", [SEQS * NCK * 16, 128], F32),
                            ("bb", [SEQS * NCK * 16, 128], F32),
                            ("attn", [NTOK, D], BF16), ("x1", [NTOK, D], F32)]:
        if name in debug:
            dbg[name] = nc.dram_tensor("dbg_" + name, shape, dt,
                                       kind="ExternalOutput").ap()

    stack = contextlib.ExitStack()
    sb = lambda name, shape, dt: stack.enter_context(nc.sbuf_tensor(name, shape, dt))
    psalloc = lambda name: stack.enter_context(nc.psum_tensor(name, [128, 512], F32))

    c_i128bf = sb("c_i128bf", [128, 128], BF16)
    c_i128f = sb("c_i128f", [128, 128], F32)
    c_i16f = sb("c_i16f", [16, 16], F32)
    c_mpre = sb("c_mpre", [128, 128], F32)
    c_ones16 = sb("c_ones16", [16, 128], F32)
    c_galg = sb("c_galg", [16, 2], F32)
    c_bwc = sb("c_bwc", [16, 1], F32)
    c_nbw = sb("c_nbw", [16, 1], F32)
    c_bec = sb("c_bec", [16, 1], F32)
    pb = [psalloc(f"pb{i}") for i in range(8)]

    B = {}

    def buf(name, group=None):
        if name not in B:
            B[name] = p.buf(name, group)
        return B[name]

    pbb = [buf(f"pb{i}") for i in range(8)]

    c_m1 = sb("c_m1", [128, 1], F32)
    p.emit("vector", lambda e: e.memset(c_m1[:], -1.0),
           writes=[buf("m1w")], tag="m1set")
    m1b = buf("m1w")

    ln_cols = [sb(f"ln_cols{i}", [128, 8], F32) for i in range(2)]
    ln_cols_b = [buf("ln_cols0"), buf("ln_cols1")]
    l2inv = sb("l2inv", [128, 32], F32)
    l2nmu = sb("l2nmu", [128, 32], F32)
    l2invb = buf("l2inv")
    l2nmub = buf("l2nmu")

    def emit_ln(idx, x_ap, xbuf, out_ap, outbuf, tagp):
        par = idx % 2
        cl, clb = ln_cols[par], ln_cols_b[par]
        p.emit("scalar", lambda e, o=out_ap, i=x_ap, a=cl[:, 1:2]:
               e.activation(o, i, AF.Square, accum_out=a),
               reads=[xbuf], writes=[outbuf, clb], tag=f"{tagp}sq")
        p.emit("vector", lambda e, o=cl[:, 0:1], i=x_ap:
               e.reduce_sum(o, i, axis=mybir.AxisListType.X),
               reads=[xbuf], writes=[clb], tag=f"{tagp}sum")
        p.emit("vector", lambda e, o=cl[:, 2:3], i=cl[:, 0:1]:
               e.tensor_single_scalar(o, i, 1.0 / D, ALU.mult),
               reads=[clb], writes=[clb], tag=f"{tagp}mean")
        p.emit("vector", lambda e, o=cl[:, 3:4], i=cl[:, 2:3]:
               e.tensor_tensor(o, i, i, ALU.mult),
               reads=[clb], writes=[clb], tag=f"{tagp}msq")
        p.emit("vector", lambda e, o=cl[:, 4:5], i=cl[:, 1:2], m=cl[:, 3:4]:
               e.scalar_tensor_tensor(o, i, 1.0 / D, m, ALU.mult, ALU.subtract),
               reads=[clb], writes=[clb], tag=f"{tagp}var")
        p.emit("vector", lambda e, o=cl[:, 4:5], i=cl[:, 4:5]:
               e.tensor_single_scalar(o, i, 1e-5, ALU.add),
               reads=[clb], writes=[clb], tag=f"{tagp}vev")
        p.emit("scalar", lambda e, o=cl[:, 4:5], i=cl[:, 4:5]:
               e.activation(o, i, AF.Ln),
               reads=[clb], writes=[clb], tag=f"{tagp}ln")
        p.emit("vector", lambda e, o=cl[:, 4:5], i=cl[:, 4:5]:
               e.tensor_single_scalar(o, i, -0.5, ALU.mult),
               reads=[clb], writes=[clb], tag=f"{tagp}nh")
        p.emit("scalar", lambda e, o=cl[:, 4:5], i=cl[:, 4:5]:
               e.activation(o, i, AF.Exp),
               reads=[clb], writes=[clb], tag=f"{tagp}ex")
        p.emit("vector", lambda e, o=cl[:, 5:6], i=cl[:, 2:3], v=cl[:, 4:5]:
               e.tensor_scalar(o, i, v, -1.0, ALU.mult, ALU.mult),
               reads=[clb], writes=[clb], tag=f"{tagp}nmu")
        p.emit("scalar", lambda e, o=out_ap, i=x_ap, v=cl[:, 4:5],
               b2=cl[:, 5:6]:
               e.activation(o, i, AF.Identity, scale=v, bias=b2),
               reads=[xbuf, clb], writes=[outbuf], tag=f"{tagp}app")

    for cap, src in [(c_i128bf, i128bf), (c_i128f, i128f), (c_i16f, i16f),
                     (c_mpre, mpre), (c_ones16, ones16), (c_bwc, bwc),
                     (c_bec, bec), (c_galg, galg)]:
        p.dma("sync", lambda e, o=cap, s=src: e.dma_start(out=o[:], in_=s),
              writes=[buf("consts", group="consts")])
    cb = buf("consts")
    p.emit("vector", lambda e: e.tensor_scalar(c_nbw[:], c_bwc[:], -1.0, None,
                                               ALU.mult),
           reads=[cb], writes=[buf("nbwd")], tag="negbw")
    nbwb = buf("nbwd")

    with contextlib.ExitStack() as sA:
        sbA = lambda name, shape, dt: sA.enter_context(nc.sbuf_tensor(name, shape, dt))
        stg = [sbA(f"stg{i}", [128, D], F32) for i in range(4)]
        l2sm = sbA("l2sm", [128, 32], F32)
        l2ss = sbA("l2ss", [128, 32], F32)
        l2tmp = sbA("l2tmp", [128, 32], F32)
        l2smb, l2ssb, l2tmpb = buf("l2sm"), buf("l2ss"), buf("l2tmp")
        stgb = [buf(f"stg{i}") for i in range(4)]
        wq_sb = sbA("wq_sb", [128, 8 * D], BF16)
        wk_sb = sbA("wk_sb", [128, 8 * D], BF16)
        wv_sb = sbA("wv_sb", [128, 8 * D], BF16)
        wo_sb = sbA("wo_sb", [128, 8 * D], BF16)
        ww_sb = sbA("ww_sb", [128, 8 * H], BF16)
        we_sb = sbA("we_sb", [128, 8 * H], BF16)
        xn_bf = [sbA(f"xn_bf{i}", [128, D], BF16) for i in range(2)]
        xnT = [sbA(f"xnT{i}", [128, D], BF16) for i in range(2)]
        qt_c = [sbA(f"qt_c{i}", [128, D], BF16) for i in range(2)]
        kt_c = [sbA(f"kt_c{i}", [128, D], BF16) for i in range(2)]
        va_c = [sbA(f"va_c{i}", [128, H * 65], BF16) for i in range(2)]
        s_f32 = sbA("s_f32", [128, 8 * 65], F32)
        s_bf = sbA("s_bf", [128, 8 * 65], BF16)
        g_u = sbA("g_u", [16, 128], F32)
        g_lw1 = sbA("g_lw1", [16, 128], F32)
        g_t = sbA("g_t", [16, 128], F32)
        g_v = sbA("g_v", [16, 128], F32)
        g_t1 = sbA("g_t1", [16, 128], F32)
        g_logg = sbA("g_logg", [16, 128], F32)
        g_c = [sbA(f"g_c{i}", [16, 128], F32) for i in range(2)]
        g_b = sbA("g_b", [16, 128], F32)
        carry = sbA("carry", [16, 1], F32)
        cprev = sbA("cprev", [16, 1], F32)
        dcol = sbA("dcol", [16, 1], F32)
        asrc = sbA("asrc", [16, 128], F32)
        ct_c = [sbA(f"ct_c{i}", [128, 16], F32) for i in range(2)]
        bt_c = [sbA(f"bt_c{i}", [128, 16], F32) for i in range(2)]
        qd_c = [sbA(f"qd_c{i}", [128, 16], F32) for i in range(2)]
        kd_c = [sbA(f"kd_c{i}", [128, 16], F32) for i in range(2)]
        lam_c = [sbA(f"lam_c{i}", [128, 16], F32) for i in range(2)]
        qdt = sbA("qdt", [128, 16], F32)
        kdt = sbA("kdt", [128, 16], F32)
        scr_e = [sbA(f"scr_e{i}", [128, 512], F32) for i in range(2)]
        scr_m = [sbA(f"scr_m{i}", [128, 512], F32) for i in range(2)]
        tmpM = [sbA(f"tmpM{i}", [128, 128], F32) for i in range(8)]
        dec = [sbA(f"dec{i}", [128, 128], F32) for i in range(8)]
        effT = [sbA(f"effT{i}", [128, 128], BF16) for i in range(8)]
        kdb = [sbA(f"kdb{i}", [128, DH], BF16) for i in range(8)]
        r1s = [sbA(f"r1s{i}", [128, 65], F32) for i in range(8)]
        comb = [sbA(f"comb{i}", [128, 65], F32) for i in range(8)]
        recc = [sbA(f"recc{i}", [128, 1], F32) for i in range(8)]
        atm = [sbA(f"atm{i}", [128, D], BF16) for i in range(2)]
        att = [sbA(f"att{i}", [128, D], BF16) for i in range(2)]
        x1s = [sbA(f"x1s{i}", [128, D], F32) for i in range(2)]

        def wload(dst, src, cdim, fdim, name):
            inap = src.rearrange("(c p) f -> p c f", p=128)
            outap = dst[:].rearrange("p (c f) -> p c f", f=fdim)
            p.dma("gpsimd", lambda e, o=outap, i=inap: e.dma_start(out=o, in_=i),
                  writes=[buf(name)], tag="dma_w_" + name)
            return buf(name)

        wqb = wload(wq_sb, wqt, 8, D, "wq")
        wkb = wload(wk_sb, wkt, 8, D, "wk")
        wvb = wload(wv_sb, wvt, 8, D, "wv")
        wob = wload(wo_sb, wot, 8, D, "wo")
        wwb = wload(ww_sb, wwt, 8, H, "ww")
        web = wload(we_sb, wet, 8, H, "we")

        sbuf_bufs = {}

        def tb(obj, name):
            key = name
            if key not in sbuf_bufs:
                sbuf_bufs[key] = buf(key)
            return sbuf_bufs[key]

        stg_b = stgb
        xnbf_b = [buf("xnbf0"), buf("xnbf1")]
        xnT_b = [buf("xnT0"), buf("xnT1")]
        qt_b = [buf("qt0"), buf("qt1")]
        kt_b = [buf("kt0"), buf("kt1")]
        va_b = [buf("va0"), buf("va1")]
        sf_b = [buf(f"sf{h}") for h in range(H)]
        sbf_b = [buf(f"sbf{h}") for h in range(H)]
        gate_b = {nm: buf("g_" + nm) for nm in
                  ["u", "lw1", "t", "v", "t1", "logg", "b", "carry", "cprev",
                   "dcol", "asrc", "qdt", "kdt"]}
        gc_b = [buf("gc0"), buf("gc1")]
        ct_b = [buf("ct0"), buf("ct1")]
        bt_b = [buf("bt0"), buf("bt1")]
        qd_b = [buf("qd0"), buf("qd1")]
        kd_b = [buf("kd0"), buf("kd1")]
        lam_b = [buf("lam0"), buf("lam1")]
        scre_b = [buf("scre0"), buf("scre1")]
        scrm_b = [buf("scrm0"), buf("scrm1")]
        tmpM_b = [buf(f"tmpM{i}") for i in range(8)]
        dec_b = [buf(f"dec{i}") for i in range(8)]
        effT_b = [buf(f"effT{i}") for i in range(8)]
        kdb_b = [buf(f"kdb{i}") for i in range(8)]
        r1_b = [buf(f"r1{i}") for i in range(8)]
        comb_b = [buf(f"comb{i}") for i in range(8)]
        recc_b = [buf(f"recc{i}") for i in range(8)]
        atm_b = [buf("atm0"), buf("atm1")]
        att_b = [buf("att0"), buf("att1")]
        x1s_b = [buf("x1s0"), buf("x1s1")]

        def _ahead(gc):
            return (gc // NCK, gc % NCK, gc % 2, stg[gc % 4], stg_b[gc % 4])

        def stageA1a(gc):
            s, c, r, st, stb = _ahead(gc)
            p.dma("sync", lambda e, o=st, g=gc: e.dma_start(
                out=o[:], in_=xin[g * 128:(g + 1) * 128, :]),
                writes=[stb], tag=f"dma_x{gc}")
            emit_ln(gc, st[:], stb, xn_bf[r][:], xnbf_b[r], f"ln{gc}_")
            if "xn" in dbg:
                p.dma("sync", lambda e, i=xn_bf[r], g=gc: e.dma_start(
                    out=dbg["xn"][g * 128:(g + 1) * 128, :], in_=i[:]),
                    reads=[xnbf_b[r]], writes=[buf(f"dbg_xn_{r}")])
        def stageA1b(gc):
            s, c, r, st, stb = _ahead(gc)
            for d in range(8):
                tp = pb[2 + (d % 2)][:, 0:64].bitcast(BF16)
                tpb = pbb[2 + (d % 2)]
                p.emit("tensor", lambda e, o=tp, i=xn_bf[r][:, d * 128:(d + 1) * 128]:
                       e.transpose(o, i, c_i128bf[:]),
                       reads=[xnbf_b[r], cb], writes=[tpb], tag=f"trx{gc}_{d}")
                p.emit("vector", lambda e, o=xnT[r][:, d * 128:(d + 1) * 128], i=tp:
                       e.tensor_copy(o, i),
                       reads=[tpb], writes=[xnT_b[r]], tag=f"trxe{gc}_{d}")

        def stageA2(gc):
            s, c, r, st, stb = _ahead(gc)
            for (wsb_, wb_, bank, bankb, nm) in [
                    (ww_sb, wwb, pb[4], pbb[4], "zw"),
                    (we_sb, web, pb[5], pbb[5], "ze")]:
                for d in range(8):
                    p.emit("tensor", lambda e, o=bank[0:16, 0:128],
                           l=wsb_[:, d * H:(d + 1) * H],
                           rr=xnT[r][:, d * 128:(d + 1) * 128],
                           s0=(d == 0), s1=(d == 7):
                           e.matmul(o, l, rr, start=s0, stop=s1),
                           reads=[wb_, xnT_b[r]], writes=[bankb],
                           tag=f"mm{nm}{gc}_{d}")

            for (wsb, wb_, dst, dstb, nm) in [
                    (wq_sb, wqb, qt_c[r], qt_b[r], "q"),
                    (wk_sb, wkb, kt_c[r], kt_b[r], "k")]:
                for quad in range(2):
                    bank = pb[quad]
                    bankb = pbb[quad]
                    for fq in range(4):
                        fc = quad * 4 + fq
                        for d in range(8):
                            p.emit("tensor", lambda e,
                                   o=bank[:, fq * 128:(fq + 1) * 128],
                                   l=wsb[:, d * D + fc * 128: d * D + (fc + 1) * 128],
                                   rr=xnT[r][:, d * 128:(d + 1) * 128],
                                   s0=(d == 0), s1=(d == 7):
                                   e.matmul(o, l, rr, start=s0, stop=s1),
                                   reads=[wb_, xnT_b[r]], writes=[bankb],
                                   tag=f"mm{nm}{gc}_{fc}_{d}")
                    p.emit("scalar", lambda e, o=scr_e[quad], i=bank:
                           e.activation(o[:], i[:], AF.Exp),
                           reads=[bankb], writes=[scre_b[quad]],
                           tag=f"phiexp{nm}{gc}_{quad}")
                    p.emit("scalar", lambda e, o=scr_m[quad], i=bank:
                           e.activation(o[:], i[:], AF.Relu),
                           reads=[bankb], writes=[scrm_b[quad]],
                           tag=f"phirelu{nm}{gc}_{quad}")
                    p.emit("vector", lambda e, o=scr_e[quad], i=scr_e[quad]:
                           e.tensor_single_scalar(o[:], i[:], 1.0, ALU.min),
                           reads=[scre_b[quad]], writes=[scre_b[quad]],
                           tag=f"phimin{nm}{gc}_{quad}")
                    p.emit("gpsimd", lambda e,
                           o=dst[:, quad * 512:(quad + 1) * 512],
                           a=scr_m[quad], bb2=scr_e[quad]:
                           e.tensor_add(o, a[:], bb2[:]),
                           reads=[scrm_b[quad], scre_b[quad]], writes=[dstb],
                           tag=f"phiout{nm}{gc}_{quad}")
            if "qt" in dbg:
                p.dma("sync", lambda e, i=qt_c[r], g=gc: e.dma_start(
                    out=dbg["qt"][g * 128:(g + 1) * 128, :], in_=i[:]),
                    reads=[qt_b[r]], writes=[buf(f"dbg_qt_{r}")])
            if "kt" in dbg:
                p.dma("sync", lambda e, i=kt_c[r], g=gc: e.dma_start(
                    out=dbg["kt"][g * 128:(g + 1) * 128, :], in_=i[:]),
                    reads=[kt_b[r]], writes=[buf(f"dbg_kt_{r}")])

            for half in range(2):
                bank = pb[2 + half]
                bankb = pbb[2 + half]
                for d in range(8):
                    p.emit("tensor", lambda e, o=bank[:, 0:512],
                           l=xnT[r][:, d * 128:(d + 1) * 128],
                           rr=wv_sb[:, d * D + half * 512: d * D + half * 512 + 512],
                           s0=(d == 0), s1=(d == 7):
                           e.matmul(o, l, rr, start=s0, stop=s1),
                           reads=[wvb, xnT_b[r]], writes=[bankb],
                           tag=f"mmv{gc}_{half}_{d}")
                oap = va_c[r][:, half * 8 * 65:(half * 8 + 8) * 65] \
                    .rearrange("p (h v) -> p h v", v=65)[:, :, 0:64]
                p.emit("vector", lambda e, o=oap, i=bank[:, 0:512]:
                       e.tensor_copy(o, i),
                       reads=[bankb], writes=[va_b[r]], tag=f"vev{gc}_{half}")
            msap = va_c[r][:].rearrange("p (h v) -> p h v", v=65)[:, :, 64:65]
            p.emit("vector", lambda e, o=msap: e.memset(o, 1.0),
                   writes=[va_b[r]], tag=f"vones{gc}")
            if "vaug" in dbg:
                p.dma("sync", lambda e, i=va_c[r], g=gc: e.dma_start(
                    out=dbg["vaug"][g * 128:(g + 1) * 128, :], in_=i[:]),
                    reads=[va_b[r]], writes=[buf(f"dbg_va_{r}")])


        def stageA3(gc):
            s, c, r, st, stb = _ahead(gc)
            GB = gate_b
            p.emit("scalar", lambda e, o=g_u, i=pb[4][0:16, 0:128]:
                   e.activation(o[:], i, AF.Exp, bias=c_nbw[:], scale=-1.0),
                   reads=[pbb[4], nbwb], writes=[GB["u"]], tag=f"gu{gc}")
            p.emit("scalar", lambda e, o=g_lw1, i=g_u:
                   e.activation(o[:], i[:], AF.Ln, bias=1.0),
                   reads=[GB["u"]], writes=[GB["lw1"]], tag=f"glw{gc}")
            p.emit("vector", lambda e, o=g_t, i=pb[5][0:16, 0:128]:
                   e.tensor_scalar(o[:], i, c_bec[:], 80.0, ALU.add, ALU.min),
                   reads=[pbb[5], cb], writes=[GB["t"]], tag=f"gt{gc}")
            p.emit("scalar", lambda e, o=g_v, i=g_t:
                   e.activation(o[:], i[:], AF.Exp),
                   reads=[GB["t"]], writes=[GB["v"]], tag=f"gv{gc}")
            p.emit("scalar", lambda e, o=g_t1, i=g_v:
                   e.activation(o[:], i[:], AF.Ln, bias=1.0),
                   reads=[GB["v"]], writes=[GB["t1"]], tag=f"gt1{gc}")
            p.emit("scalar", lambda e, o=g_v, i=g_v:
                   e.activation(o[:], i[:], AF.Ln, bias=c_galg[:, 0:1],
                                scale=c_galg[:, 1:2]),
                   reads=[GB["v"], cb], writes=[GB["v"]], tag=f"gt2{gc}")
            p.emit("vector", lambda e, o=g_logg, a=g_v, b=g_t1:
                   e.tensor_sub(o[:], a[:], b[:]),
                   reads=[GB["v"], GB["t1"]], writes=[GB["logg"]],
                   tag=f"glogg{gc}")
            if c == 0:
                p.emit("vector", lambda e, o=cprev: e.memset(o[:], 0.0),
                       writes=[GB["cprev"]], tag=f"gcp{gc}")
            else:
                p.emit("vector", lambda e, o=cprev, i=carry:
                       e.tensor_copy(o[:], i[:]),
                       reads=[GB["carry"]], writes=[GB["cprev"]],
                       tag=f"gcp{gc}")
            p.emit("vector", lambda e, o=g_c[r], i=g_logg, ini=cprev:
                   e.tensor_tensor_scan(o[:], i[:], i[:], ini[:],
                                        ALU.add, ALU.bypass),
                   reads=[GB["logg"], GB["cprev"]], writes=[gc_b[r]],
                   tag=f"gscan{gc}")
            if "cc" in dbg:
                p.dma("sync", lambda e, i=g_c[r], g=gc: e.dma_start(
                    out=dbg["cc"][g * 16:(g + 1) * 16, :], in_=i[:]),
                    reads=[gc_b[r]], writes=[buf(f"dbg_cc_{r}")])
            p.emit("vector", lambda e, o=carry, i=g_c[r][:, 127:128]:
                   e.tensor_copy(o[:], i),
                   reads=[gc_b[r]], writes=[GB["carry"]], tag=f"gca{gc}")
            p.emit("vector", lambda e, o=g_b, a=g_lw1, bb_=g_c[r]:
                   e.scalar_tensor_tensor(o[:], a[:], -1.0, bb_[:],
                                          ALU.mult, ALU.subtract),
                   reads=[GB["lw1"], gc_b[r]], writes=[GB["b"]],
                   tag=f"gb{gc}")
            if "bb" in dbg:
                p.dma("sync", lambda e, i=g_b, g=gc: e.dma_start(
                    out=dbg["bb"][g * 16:(g + 1) * 16, :], in_=i[:]),
                    reads=[GB["b"]], writes=[buf(f"dbg_bb_{gc % 2}")])
            p.emit("tensor", lambda e, o=pb[4][:, 0:16], i=g_c[r]:
                   e.transpose(o, i[:], c_i16f[:]),
                   reads=[gc_b[r], cb], writes=[pbb[4]], tag=f"trc{gc}")
            p.emit("vector", lambda e, o=ct_c[r], i=pb[4][:, 0:16]:
                   e.tensor_copy(o[:], i),
                   reads=[pbb[4]], writes=[ct_b[r]], tag=f"trce{gc}")
            p.emit("tensor", lambda e, o=pb[5][:, 0:16], i=g_b:
                   e.transpose(o, i[:], c_i16f[:]),
                   reads=[GB["b"], cb], writes=[pbb[5]], tag=f"trb{gc}")
            p.emit("vector", lambda e, o=bt_c[r], i=pb[5][:, 0:16]:
                   e.tensor_copy(o[:], i),
                   reads=[pbb[5]], writes=[bt_b[r]], tag=f"trbe{gc}")
            p.emit("vector", lambda e, o=asrc, s1=cprev:
                   e.tensor_scalar(o[:], c_ones16[:], s1[:], None, ALU.mult),
                   reads=[cb, GB["cprev"]], writes=[GB["asrc"]],
                   tag=f"gas{gc}")
            p.emit("tensor", lambda e, o=pb[6][:, 0:16], i=asrc:
                   e.transpose(o, i[:], c_i16f[:]),
                   reads=[GB["asrc"], cb], writes=[pbb[6]], tag=f"trap{gc}")
            p.emit("vector", lambda e, o=qdt, a=ct_c[r], i=pb[6][:, 0:16]:
                   e.tensor_sub(o[:], a[:], i),
                   reads=[ct_b[r], pbb[6]], writes=[GB["qdt"]],
                   tag=f"qdt{gc}")
            p.emit("scalar", lambda e, o=qd_c[r], i=qdt:
                   e.activation(o[:], i[:], AF.Exp),
                   reads=[GB["qdt"]], writes=[qd_b[r]], tag=f"qde{gc}")
            p.emit("vector", lambda e, o=asrc, s1=carry:
                   e.tensor_scalar(o[:], c_ones16[:], s1[:], None, ALU.mult),
                   reads=[cb, GB["carry"]], writes=[GB["asrc"]],
                   tag=f"gas2{gc}")
            p.emit("tensor", lambda e, o=pb[7][:, 0:16], i=asrc:
                   e.transpose(o, i[:], c_i16f[:]),
                   reads=[GB["asrc"], cb], writes=[pbb[7]], tag=f"tran{gc}")
            p.emit("vector", lambda e, o=kdt, a=bt_c[r], i=pb[7][:, 0:16]:
                   e.tensor_add(o[:], a[:], i),
                   reads=[bt_b[r], pbb[7]], writes=[GB["kdt"]],
                   tag=f"kdt{gc}")
            p.emit("scalar", lambda e, o=kd_c[r], i=kdt:
                   e.activation(o[:], i[:], AF.Exp),
                   reads=[GB["kdt"]], writes=[kd_b[r]], tag=f"kde{gc}")
            p.emit("vector", lambda e, o=dcol, a=carry, bb_=cprev:
                   e.tensor_sub(o[:], a[:], bb_[:]),
                   reads=[GB["carry"], GB["cprev"]], writes=[GB["dcol"]],
                   tag=f"dcol{gc}")
            p.emit("vector", lambda e, o=asrc, s1=dcol:
                   e.tensor_scalar(o[:], c_ones16[:], s1[:], None,
                                   ALU.mult),
                   reads=[cb, GB["dcol"]], writes=[GB["asrc"]],
                   tag=f"lsrc{gc}")
            p.emit("tensor", lambda e, o=pb[6][:, 16:32], i=asrc:
                   e.transpose(o, i[:], c_i16f[:]),
                   reads=[GB["asrc"], cb], writes=[pbb[6]], tag=f"trl{gc}")
            p.emit("scalar", lambda e, o=lam_c[r], i=pb[6][:, 16:32]:
                   e.activation(o[:], i, AF.Exp),
                   reads=[pbb[6]], writes=[lam_b[r]], tag=f"lam{gc}")

        def stageA4(gc):
            s, c, r, st, stb = _ahead(gc)
            GSZ = 4
            KNOINT = False
            KNOU = False
            for hg in range(H // GSZ):
                heads = list(range(hg * GSZ, hg * GSZ + GSZ))

                def HH(fn):
                    for hh, h in enumerate(heads):
                        po = 64 * (h % 2)
                        fc = h // 2
                        fn(hh, h, po, fc,
                           kt_c[r][po:po + 64, fc * 128:(fc + 1) * 128],
                           qt_c[r][po:po + 64, fc * 128:(fc + 1) * 128],
                           va_c[r][:, h * 65:(h + 1) * 65])

                def sc_ap(hh):
                    return pb[hh % 2][:, (hh // 2) * 128:(hh // 2) * 128 + 128]

                def crm_ap(hh):
                    return pb[2 + hh % 2][:, (hh // 2) * 128:(hh // 2) * 128 + 128]

                def ktm_ap(hh):
                    c0 = (hh // 2) * 128
                    return pb[2 + hh % 2][:, c0:c0 + 32].bitcast(BF16)

                def po_ap(hh):
                    return pb[4 + hh % 2][:, (hh // 2) * 65:(hh // 2) * 65 + 65]

                def pi_ap(hh):
                    return pb[6 + hh % 2][:, (hh // 2) * 65:(hh // 2) * 65 + 65]


                def pu_ap(hh, po):
                    c0 = 260 + (hh // 2) * 65
                    return pb[6 + hh % 2][po:po + 64, c0:c0 + 65]

                def pub(hh):
                    return pbb[6 + hh % 2]

                def emit_tmpM(hg2):
                    for hh2, h2 in enumerate(range(hg2 * GSZ,
                                                   hg2 * GSZ + GSZ)):
                        sl2 = hh2 + 4 * (hg2 % 2)
                        p.emit("vector", lambda e, o=tmpM[sl2],
                               s1=ct_c[r][:, h2:h2 + 1]:
                               e.tensor_scalar(o[:], c_mpre[:], s1, None,
                                               ALU.add),
                               reads=[cb, ct_b[r]], writes=[tmpM_b[sl2]],
                               tag=f"tmpM{gc}_{h2}")

                if hg == 0:
                    emit_tmpM(0)
                HH(lambda hh, h, po, fc, ktsl, qtsl, vasl:
                   p.emit("tensor", lambda e, o=sc_ap(hh), l=ktsl, rr=qtsl:
                          e.matmul(o, l, rr, start=True, stop=True),
                          reads=[kt_b[r], qt_b[r]], writes=[pbb[hh % 2]],
                          tag=f"mmsc{gc}_{h}"))
                HH(lambda hh, h, po, fc, ktsl, qtsl, vasl:
                   p.emit("tensor", lambda e, o=crm_ap(hh),
                          i=tmpM[hh + 4 * (hg % 2)]:
                          e.transpose(o, i[:], c_i128f[:]),
                          reads=[tmpM_b[hh + 4 * (hg % 2)], cb],
                          writes=[pbb[2 + hh % 2]],
                          tag=f"trcrm{gc}_{h}"))
                HH(lambda hh, h, po, fc, ktsl, qtsl, vasl:
                   p.emit("scalar", lambda e, o=dec[hh], i=crm_ap(hh),
                          bi=bt_c[r][:, h:h + 1]:
                          e.activation(o[:], i, AF.Exp, bias=bi),
                          reads=[pbb[2 + hh % 2], bt_b[r]], writes=[dec_b[hh]],
                          tag=f"dec{gc}_{h}"))
                HH(lambda hh, h, po, fc, ktsl, qtsl, vasl:
                   p.emit("vector", lambda e, o=effT[hh], a=sc_ap(hh),
                          bb_=dec[hh]: e.tensor_mul(o[:], a, bb_[:]),
                          reads=[pbb[hh % 2], dec_b[hh]], writes=[effT_b[hh]],
                          tag=f"eff{gc}_{h}"))
                if hg + 1 < H // GSZ:
                    emit_tmpM(hg + 1)
                HH(lambda hh, h, po, fc, ktsl, qtsl, vasl:
                   p.emit("tensor", lambda e, o=ktm_ap(hh), i=ktsl,
                          idn=c_i128bf[po:po + 64, po:po + 64]:
                          e.transpose(o, i, idn),
                          reads=[kt_b[r], cb, dec_b[hh]],
                          writes=[pbb[2 + hh % 2]], tag=f"trktm{gc}_{h}"))
                HH(lambda hh, h, po, fc, ktsl, qtsl, vasl:
                   p.emit("scalar", lambda e, o=kdb[hh], i=ktm_ap(hh),
                          sc=kd_c[r][:, h:h + 1]:
                          e.activation(o[:], i, AF.Identity, scale=sc),
                          reads=[pbb[2 + hh % 2], kd_b[r]], writes=[kdb_b[hh]],
                          tag=f"kdscale{gc}_{h}"))
                HH(lambda hh, h, po, fc, ktsl, qtsl, vasl:
                   p.emit("tensor", lambda e, o=po_ap(hh), l=effT[hh], rr=vasl:
                          e.matmul(o, l[:], rr, start=True, stop=True),
                          reads=[effT_b[hh], va_b[r]], writes=[pbb[4 + hh % 2]],
                          tag=f"mmdiag{gc}_{h}"))
                if c > 0 and not KNOINT:
                    HH(lambda hh, h, po, fc, ktsl, qtsl, vasl:
                       p.emit("tensor", lambda e, o=pi_ap(hh), l=qtsl,
                              rr=s_bf[po:po + 64, fc * 65:(fc + 1) * 65]:
                              e.matmul(o, l, rr, start=True, stop=True),
                              reads=[qt_b[r], sbf_b[h]],
                              writes=[pbb[6 + hh // 4]],
                              tag=f"mmint{gc}_{h}"))
                KNOU or HH(lambda hh, h, po, fc, ktsl, qtsl, vasl:
                   p.emit("tensor", lambda e, o=pu_ap(hh, po), l=kdb[hh],
                          rr=vasl: e.matmul(o, l[:], rr, start=True, stop=True),
                          reads=[kdb_b[hh], va_b[r]], writes=[pbb[6 + hh // 4]],
                          tag=f"mmu{gc}_{h}"))
                if c > 0 and not KNOINT:
                    HH(lambda hh, h, po, fc, ktsl, qtsl, vasl:
                       p.emit("scalar", lambda e, o=r1s[hh], i=pi_ap(hh),
                              sc=qd_c[r][:, h:h + 1]:
                              e.activation(o[:], i, AF.Identity, scale=sc),
                              reads=[pbb[6 + hh // 4], qd_b[r]],
                              writes=[r1_b[hh]], tag=f"r1{gc}_{h}"))
                    HH(lambda hh, h, po, fc, ktsl, qtsl, vasl:
                       p.emit("vector", lambda e, o=comb[hh], a=po_ap(hh),
                              bb_=r1s[hh]: e.tensor_add(o[:], a, bb_[:]),
                              reads=[pbb[4 + hh % 2], r1_b[hh]],
                              writes=[comb_b[hh]], tag=f"comb{gc}_{h}"))
                if c == 0 or KNOINT:
                    HH(lambda hh, h, po, fc, ktsl, qtsl, vasl:
                       p.emit("vector", lambda e, o=comb[hh], i=po_ap(hh):
                              e.tensor_copy(o[:], i),
                              reads=[pbb[4 + hh % 2]], writes=[comb_b[hh]],
                              tag=f"comb{gc}_{h}"))
                if KNOU:
                    pass
                elif c == 0:
                    HH(lambda hh, h, po, fc, ktsl, qtsl, vasl:
                       p.emit("vector", lambda e,
                              o=s_f32[po:po + 64, fc * 65:(fc + 1) * 65],
                              i=pu_ap(hh, po): e.tensor_copy(o, i),
                              reads=[pbb[6 + hh // 4]], writes=[sf_b[h]],
                              tag=f"sini{gc}_{h}"))
                else:
                    HH(lambda hh, h, po, fc, ktsl, qtsl, vasl:
                       p.emit("vector", lambda e,
                              o=s_f32[po:po + 64, fc * 65:(fc + 1) * 65],
                              sc=lam_c[r][po:po + 64, h:h + 1]:
                              e.tensor_scalar(o, o, sc, None, ALU.mult),
                              reads=[lam_b[r], sf_b[h]], writes=[sf_b[h]],
                              tag=f"slam{gc}_{h}"))
                    KNOU or \
                    HH(lambda hh, h, po, fc, ktsl, qtsl, vasl:
                       p.emit("vector", lambda e,
                              o=s_f32[po:po + 64, fc * 65:(fc + 1) * 65],
                              u=pu_ap(hh, po): e.tensor_add(o, o, u),
                              reads=[pbb[6 + hh // 4], sf_b[h]],
                              writes=[sf_b[h]], tag=f"supd{gc}_{h}"))
                HH(lambda hh, h, po, fc, ktsl, qtsl, vasl:
                   p.emit("gpsimd", lambda e,
                          o=s_bf[po:po + 64, fc * 65:(fc + 1) * 65],
                          i=s_f32[po:po + 64, fc * 65:(fc + 1) * 65]:
                          e.tensor_copy(o, i),
                          reads=[sf_b[h]], writes=[sbf_b[h]],
                          tag=f"scast{gc}_{h}"))
                HH(lambda hh, h, po, fc, ktsl, qtsl, vasl:
                   p.emit("vector", lambda e, o=recc[hh], i=comb[hh][:, 64:65]:
                          e.tensor_single_scalar(o[:], i, 1e-6, ALU.add),
                          reads=[comb_b[hh]], writes=[recc_b[hh]],
                          tag=f"eps{gc}_{h}"))
                HH(lambda hh, h, po, fc, ktsl, qtsl, vasl:
                   p.emit("vector", lambda e, o=recc[hh]:
                          e.reciprocal(o[:], o[:]),
                          reads=[recc_b[hh]], writes=[recc_b[hh]],
                          tag=f"rec{gc}_{h}"))
                HH(lambda hh, h, po, fc, ktsl, qtsl, vasl:
                   p.emit("scalar", lambda e, o=atm[r][:, h * 64:(h + 1) * 64],
                          i=comb[hh][:, 0:64], sc=recc[hh]:
                          e.activation(o, i, AF.Identity, scale=sc[:]),
                          reads=[comb_b[hh], recc_b[hh]], writes=[atm_b[r]],
                          tag=f"osc{gc}_{h}"))

            if "attn" in dbg:
                p.dma("sync", lambda e, i=atm[r], g=gc: e.dma_start(
                    out=dbg["attn"][g * 128:(g + 1) * 128, :], in_=i[:]),
                    reads=[atm_b[r]], writes=[buf(f"dbg_at_{r}")])

        def stageA5(gc):
            s, c, r, st, stb = _ahead(gc)
            for d in range(8):
                tp = pb[2 + (d % 2)][:, 0:64].bitcast(BF16)
                tpb = pbb[2 + (d % 2)]
                p.emit("tensor", lambda e, o=tp,
                       i=atm[r][:, d * 128:(d + 1) * 128]:
                       e.transpose(o, i, c_i128bf[:]),
                       reads=[atm_b[r], cb], writes=[tpb],
                       tag=f"trat{gc}_{d}")
                p.emit("vector", lambda e, o=att[r][:, d * 128:(d + 1) * 128],
                       i=tp: e.tensor_copy(o, i),
                       reads=[tpb], writes=[att_b[r]], tag=f"trate{gc}_{d}")
            for half in range(2):
                bank = pb[half]
                bankb = pbb[half]
                for d in range(8):
                    p.emit("tensor", lambda e, o=bank[:, 0:512],
                           l=att[r][:, d * 128:(d + 1) * 128],
                           rr=wo_sb[:, d * D + half * 512: d * D + half * 512 + 512],
                           s0=(d == 0), s1=(d == 7):
                           e.matmul(o, l, rr, start=s0, stop=s1),
                           reads=[att_b[r], wob], writes=[bankb],
                           tag=f"mmwo{gc}_{half}_{d}")
                p.emit("vector", lambda e,
                       o=x1s[r][:, half * 512:half * 512 + 512],
                       a=bank[:, 0:512],
                       bb_=st[:, half * 512:half * 512 + 512]:
                       e.tensor_add(o, a, bb_),
                       reads=[bankb, stb], writes=[x1s_b[r]],
                       tag=f"x1add{gc}_{half}")
            p.dma("gpsimd", lambda e, i=x1s[r], g=gc: e.dma_start(
                out=x1scr[g * 128:(g + 1) * 128, :], in_=i[:]),
                reads=[x1s_b[r]], writes=[buf(f"x1w{r}")],
                tag=f"dma_x1w{gc}")
            p.emit("scalar", lambda e, o=atm[r], i=x1s[r],
                   a=l2ss[:, gc:gc + 1]:
                   e.activation(o[:], i[:], AF.Square, accum_out=a),
                   reads=[x1s_b[r]], writes=[atm_b[r], l2ssb],
                   tag=f"l2sq{gc}")
            p.emit("vector", lambda e, o=l2sm[:, gc:gc + 1], i=x1s[r]:
                   e.reduce_sum(o, i[:], axis=mybir.AxisListType.X),
                   reads=[x1s_b[r]], writes=[l2smb], tag=f"l2sum{gc}")
            if "x1" in dbg:
                p.dma("sync", lambda e, i=x1s[r], g=gc: e.dma_start(
                    out=dbg["x1"][g * 128:(g + 1) * 128, :], in_=i[:]),
                    reads=[x1s_b[r]], writes=[buf(f"dbg_x1_{r}")])

        NCA = NCK * SEQS
        for gc in range(NCA):
            if gc == 0:
                stageA1a(0)
                stageA1b(0)
            if gc + 1 < NCA:
                stageA1a(gc + 1)
            stageA2(gc)
            if gc > 0:
                stageA5(gc - 1)
            stageA3(gc)
            if gc + 1 < NCA:
                stageA1b(gc + 1)
            stageA4(gc)
        stageA5(NCA - 1)

        p.emit("vector", lambda e: e.tensor_single_scalar(
               l2sm[:], l2sm[:], 1.0 / D, ALU.mult),
               reads=[l2smb], writes=[l2smb], tag="l2mean")
        p.emit("vector", lambda e: e.tensor_tensor(
               l2tmp[:], l2sm[:], l2sm[:], ALU.mult),
               reads=[l2smb], writes=[l2tmpb], tag="l2msq")
        p.emit("vector", lambda e: e.scalar_tensor_tensor(
               l2ss[:], l2ss[:], 1.0 / D, l2tmp[:], ALU.mult, ALU.subtract),
               reads=[l2ssb, l2tmpb], writes=[l2ssb], tag="l2var")
        p.emit("vector", lambda e: e.tensor_single_scalar(
               l2ss[:], l2ss[:], 1e-5, ALU.add),
               reads=[l2ssb], writes=[l2ssb], tag="l2vev")
        p.emit("scalar", lambda e: e.activation(l2ss[:], l2ss[:], AF.Ln),
               reads=[l2ssb], writes=[l2ssb], tag="l2ln")
        p.emit("vector", lambda e: e.tensor_single_scalar(
               l2ss[:], l2ss[:], -0.5, ALU.mult),
               reads=[l2ssb], writes=[l2ssb], tag="l2nh")
        p.emit("scalar", lambda e: e.activation(l2inv[:], l2ss[:], AF.Exp),
               reads=[l2ssb], writes=[l2invb], tag="l2ex")
        p.emit("vector", lambda e: e.scalar_tensor_tensor(
               l2nmu[:], l2sm[:], -1.0, l2inv[:], ALU.mult, ALU.mult),
               reads=[l2smb, l2invb], writes=[l2nmub], tag="l2nmu")

        p.barrier()
        sA.close()

    fc_sb = sb("fc_sb", [128, 8 * DF], BF16)
    pj_sb = sb("pj_sb", [128, 32 * D], BF16)
    ht_sb = sb("ht_sb", [128, 32 * 256], BF16)
    xn2T = [sb(f"xn2T{i}", [128, 8 * 256], BF16) for i in range(2)]
    x1st = [sb(f"x1st{i}", [128, D], BF16) for i in range(6)]
    xn2b = [sb(f"xn2b{i}", [128, D], BF16) for i in range(4)]
    y_sb = [sb(f"y_sb{i}", [128, 512], F32) for i in range(4)]
    sn_s = [sb(f"sn_s{i}", [128, 512], F32) for i in range(4)]
    sn_a = [sb(f"sn_a{i}", [128, 512], BF16) for i in range(4)]
    sn_m = [sb(f"sn_m{i}", [128, 512], F32) for i in range(4)]
    rh_sb = [sb(f"rh_sb{i}", [128, 256], F32) for i in range(4)]

    fcb, pjb = [], []
    for i in range(8):
        inap = fct.rearrange("(c p) f -> p c f", p=128)[:, :, i * 512:(i + 1) * 512]
        outap = fc_sb[:].rearrange("p (c f) -> p c f", f=DF)[:, :, i * 512:(i + 1) * 512]
        p.dma("gpsimd", lambda e, o=outap, s=inap: e.dma_start(out=o, in_=s),
              writes=[buf(f"fc{i}")], tag=f"dma_fc{i}")
        fcb.append(buf(f"fc{i}"))
    for i in range(8):
        inap = pjt.rearrange("(c p) f -> p c f", p=128)[:, i * 4:(i + 1) * 4, :]
        outap = pj_sb[:].rearrange("p (c f) -> p c f", f=D)[:, i * 4:(i + 1) * 4, :]
        p.dma("gpsimd", lambda e, o=outap, s=inap: e.dma_start(out=o, in_=s),
              writes=[buf(f"pj{i}")], tag=f"dma_pj{i}")
        pjb.append(buf(f"pj{i}"))

    htb = buf("ht")
    xn2T_b = [buf("xn2T0"), buf("xn2T1")]
    x1st_b = [buf(f"x1st{i}") for i in range(6)]
    xn2b_b = [buf(f"xn2b{i}") for i in range(4)]
    y_b = [buf(f"y{i}") for i in range(4)]
    sns_b = [buf(f"sns{i}") for i in range(4)]
    sna_b = [buf(f"sna{i}") for i in range(4)]
    snm_b = [buf(f"snm{i}") for i in range(4)]
    rh_b = [buf(f"rh{i}") for i in range(4)]
    x1r_deps = [buf("x1w0"), buf("x1w1")]

    def stageP1a(grp):
        gr = grp % 2
        for j in range(2):
            cc = grp * 2 + j
            stb_ = x1st_b[cc % 6]
            st_ = x1st[cc % 6]
            p.dma("scalar", lambda e, o=st_, g=cc: e.dma_start(
                out=o[:], in_=x1scr[g * 128:(g + 1) * 128, :]),
                reads=x1r_deps, writes=[stb_], tag=f"dma_x1r{cc}")
            p.emit("scalar", lambda e, o=xn2b[cc % 4][:], i=st_[:],
                   sc=l2inv[:, cc:cc + 1], b2=l2nmu[:, cc:cc + 1]:
                   e.activation(o, i, AF.Identity, scale=sc, bias=b2),
                   reads=[stb_, l2invb, l2nmub], writes=[xn2b_b[cc % 4]],
                   tag=f"ln2app{cc}")

    def stageP1b(grp):
        gr = grp % 2
        for j in range(2):
            cc = grp * 2 + j
            for d in range(8):
                tp = pb[2 + (d % 2)][:, 0:64].bitcast(BF16)
                tpb = pbb[2 + (d % 2)]
                p.emit("tensor", lambda e, o=tp,
                       i=xn2b[cc % 4][:, d * 128:(d + 1) * 128]:
                       e.transpose(o, i, c_i128bf[:]),
                       reads=[xn2b_b[cc % 4], cb], writes=[tpb],
                       tag=f"tr2_{cc}_{d}")
                p.emit("scalar", lambda e,
                       o=xn2T[gr][:, d * 256 + j * 128: d * 256 + j * 128 + 128],
                       i=tp: e.activation(o, i, AF.Copy),
                       reads=[tpb], writes=[xn2T_b[gr]], tag=f"tr2e{cc}_{d}")

    def stageP2(grp):
        gr = grp % 2
        for fcx in range(32):
            bank = pb[fcx % 4]
            bankb = pbb[fcx % 4]
            for d in range(8):
                p.emit("tensor", lambda e, o=bank[:, 0:256],
                       l=fc_sb[:, d * DF + fcx * 128: d * DF + (fcx + 1) * 128],
                       rr=xn2T[gr][:, d * 256:(d + 1) * 256],
                       s0=(d == 0), s1=(d == 7):
                       e.matmul(o, l, rr, start=s0, stop=s1),
                       reads=[fcb[fcx // 4], xn2T_b[gr]], writes=[bankb],
                       tag=f"mmh{grp}_{fcx}_{d}")
            p.emit("scalar", lambda e, o=rh_sb[fcx % 4], i=bank[:, 0:256]:
                   e.activation(o[:], i, AF.Relu),
                   reads=[bankb], writes=[rh_b[fcx % 4]], tag=f"relu{grp}_{fcx}")
            engm = "gpsimd"
            p.emit(engm, lambda e, o=ht_sb[:, fcx * 256:(fcx + 1) * 256],
                   i=rh_sb[fcx % 4]: e.tensor_mul(o, i[:], i[:]),
                   reads=[rh_b[fcx % 4]], writes=[htb], tag=f"sq{grp}_{fcx}")

    def stageP3(grp):
        gr = grp % 2
        tiles = []
        for j in range(2):
            cc = grp * 2 + j
            for half in range(2):
                t = 2 * j + half
                bank = pb[4 + t]
                bankb = pbb[4 + t]
                for fcx in range(32):
                    p.emit("tensor", lambda e, o=bank[:, 0:512],
                           l=ht_sb[:, fcx * 256 + j * 128: fcx * 256 + j * 128 + 128],
                           rr=pj_sb[:, fcx * D + half * 512: fcx * D + half * 512 + 512],
                           s0=(fcx == 0), s1=(fcx == 31):
                           e.matmul(o, l, rr, start=s0, stop=s1),
                           reads=[htb, pjb[fcx // 4]],
                           writes=[bankb], tag=f"mmy{cc}_{half}_{fcx}")
                tiles.append((t, cc, j, half, bank, bankb))

        def each(fn):
            for (tp2, cc, j, half, bank, bankb) in tiles:
                fn(tp2, cc, j, half, bank, bankb)

        each(lambda tp2, cc, j, half, bank, bankb:
             p.emit("vector", lambda e, o=y_sb[tp2], i=bank[:, 0:512]:
                    e.tensor_copy(o[:], i),
                    reads=[bankb], writes=[y_b[tp2]], tag=f"yev{cc}_{half}"))

        def sg(src_list):
            each(lambda tp2, cc, j, half, bank, bankb:
                 p.emit("vector", lambda e, o=sn_s[tp2], i=src_list[tp2]:
                        e.tensor_single_scalar(o[:], i[:], 1.0, ALU.is_gt),
                        reads=[(y_b if src_list is y_sb else snm_b)[tp2]],
                        writes=[sns_b[tp2]], tag=f"sg{cc}_{half}"))

        sg(y_sb)
        each(lambda tp2, cc, j, half, bank, bankb:
             p.emit("vector", lambda e, o=sn_a[tp2], i=sn_s[tp2]:
                    e.tensor_copy(o[:], i[:]),
                    reads=[sns_b[tp2]], writes=[sna_b[tp2]], tag=f"sac{cc}_{half}"))
        each(lambda tp2, cc, j, half, bank, bankb:
             p.emit("vector", lambda e, o=sn_m[tp2], i=y_sb[tp2],
                    s2=sn_s[tp2]:
                    e.scalar_tensor_tensor(o[:], i[:], 1.9, s2[:],
                                           ALU.mult, ALU.subtract),
                    reads=[y_b[tp2], sns_b[tp2]], writes=[snm_b[tp2]],
                    tag=f"sn3{cc}_{half}"))
        for k2 in range(3):
            sg(sn_m)
            each(lambda tp2, cc, j, half, bank, bankb:
                 p.emit("vector", lambda e, o=sn_a[tp2], i=sn_s[tp2]:
                        e.tensor_add(o[:], o[:], i[:]),
                        reads=[sns_b[tp2], sna_b[tp2]], writes=[sna_b[tp2]],
                        tag=f"sad{k2}_{cc}_{half}"))
            each(lambda tp2, cc, j, half, bank, bankb:
                 p.emit("vector", lambda e, o=sn_m[tp2], s2=sn_s[tp2]:
                        e.scalar_tensor_tensor(o[:], o[:], 0.9, s2[:],
                                               ALU.mult, ALU.subtract),
                        reads=[sns_b[tp2], snm_b[tp2]], writes=[snm_b[tp2]],
                        tag=f"sw{k2}_{cc}_{half}"))
            each(lambda tp2, cc, j, half, bank, bankb:
                 p.emit("vector", lambda e, o=sn_m[tp2], y2=y_sb[tp2]:
                        e.tensor_add(o[:], o[:], y2[:]),
                        reads=[y_b[tp2], snm_b[tp2]], writes=[snm_b[tp2]],
                        tag=f"swy{k2}_{cc}_{half}"))
        sg(sn_m)
        each(lambda tp2, cc, j, half, bank, bankb:
             p.emit("vector", lambda e, o=sn_a[tp2], i=sn_s[tp2]:
                    e.tensor_add(o[:], o[:], i[:]),
                    reads=[sns_b[tp2], sna_b[tp2]], writes=[sna_b[tp2]],
                    tag=f"sa9{cc}_{half}"))
        each(lambda tp2, cc, j, half, bank, bankb:
             p.emit("vector", lambda e, o=sn_m[tp2], i=sn_a[tp2]:
                    e.tensor_single_scalar(o[:], i[:], 0.2, ALU.mult),
                    reads=[sna_b[tp2]], writes=[snm_b[tp2]], tag=f"so1{cc}_{half}"))
        def snout(tp2, cc, j, half, bank, bankb):
            p.emit("vector", lambda e, o=y_sb[tp2][:, 0:512], a=sn_m[tp2],
                   x1=x1st[cc % 6][:, half * 512:half * 512 + 512]:
                   e.tensor_add(o, a[:], x1),
                   reads=[snm_b[tp2], x1st_b[cc % 6]],
                   writes=[y_b[tp2]], tag=f"snout{cc}_{half}")
            p.dma("sync", lambda e, i=y_sb[tp2], g=cc, hf=half:
                  e.dma_start(out=out_t[g * 128:(g + 1) * 128,
                                        hf * 512:hf * 512 + 512], in_=i[:]),
                  reads=[y_b[tp2]], writes=[buf(f"outw{tp2}")],
                  tag=f"dma_out{cc}_{half}")
        each(snout)

    NG = NTOK // 256
    for grp in range(NG):
        if grp == 0:
            stageP1a(0)
            stageP1b(0)
        if grp + 1 < NG:
            stageP1a(grp + 1)
            stageP1b(grp + 1)
        stageP2(grp)
        stageP3(grp)

    fin_reads = [buf(f"outw{i}") for i in range(4)]
    fin_reads += [B[k] for k in B if k.startswith("dbg_")]
    p.emit("sync", lambda e: None, reads=fin_reads, tag="final")

    p.build(nc, verbose=verbose)
    stack.close()
    return nc


_CACHE = {}


def _get_program(debug=()):
    key = tuple(sorted(debug))
    if key not in _CACHE:
        _CACHE[key] = build_program(debug=key)
    return _CACHE[key]


def kernel(L0_attn_x, L0_attn_Wq, L0_attn_Wk, L0_attn_Wv, L0_attn_Wo,
           L0_attn_Ww, L0_attn_bw, L0_attn_We, L0_attn_be,
           L0_ffn_W_fc, L0_ffn_W_proj, _debug=(), _return_raw=False):
    x = np.ascontiguousarray(np.asarray(L0_attn_x, dtype=np.float32))
    com = {
        "wqt": np.ascontiguousarray(np.asarray(L0_attn_Wq, np.float32).T),
        "wkt": np.ascontiguousarray(np.asarray(L0_attn_Wk, np.float32).T),
        "wvt": np.ascontiguousarray(np.asarray(L0_attn_Wv, np.float32).T),
        "wot": np.ascontiguousarray(np.asarray(L0_attn_Wo, np.float32).T),
        "wwt": np.ascontiguousarray(np.asarray(L0_attn_Ww, np.float32).T),
        "wet": np.ascontiguousarray(np.asarray(L0_attn_We, np.float32).T),
        "bwc": np.ascontiguousarray(np.asarray(L0_attn_bw, np.float32)
                                    .reshape(H, 1)),
        "bec": np.ascontiguousarray(np.asarray(L0_attn_be, np.float32)
                                    .reshape(H, 1)),
        "fct": np.ascontiguousarray(np.asarray(L0_ffn_W_fc, np.float32).T),
        "pjt": np.ascontiguousarray(np.asarray(L0_ffn_W_proj, np.float32).T),
        "i128bf": np.eye(128, dtype=BF),
        "i128f": np.eye(128, dtype=np.float32),
        "i16f": np.eye(16, dtype=np.float32),
        "mpre": np.triu(np.full((128, 128), -1e30, np.float32), 1),
        "ones16": np.ones((16, 128), np.float32),
        "galg": np.tile(np.array([[np.float32(0.95) + np.float32(1e-8),
                                   1e-8]], np.float32), (16, 1)),
    }
    nc = _get_program(_debug)
    in_maps = []
    for k in range(N_CORES):
        m = dict(com)
        m["xin"] = np.ascontiguousarray(
            x[SEQS * k:SEQS * (k + 1)].reshape(NTOK, D))
        in_maps.append(m)
    res = run_bass_kernel_spmd(nc, in_maps, core_ids=list(range(N_CORES)))
    if _return_raw:
        return res
    out = np.concatenate(
        [r["out"].reshape(SEQS, L, D) for r in res.results], axis=0)
    return out


# version 34
# speedup vs baseline: 1.0880x; 1.0880x over previous
import contextlib
import numpy as np
import ml_dtypes
import concourse.bass as bass
import concourse.mybir as mybir
from concourse.bass_utils import run_bass_kernel_spmd

F32 = mybir.dt.float32
BF16 = mybir.dt.bfloat16
AF = mybir.ActivationFunctionType
ALU = mybir.AluOpType
BF = ml_dtypes.bfloat16

N_CORES = 8
SEQS = 2
L = 2048
D = 1024
H = 16
DH = 64
NCK = L // 128
NTOK = SEQS * L
DF = 4096

DEBUG = ()

ENGINE_NAMES = ("tensor", "vector", "scalar", "gpsimd", "sync")


class Buf:
    __slots__ = ("name", "group", "last_writer", "readers", "semkey")

    def __init__(self, name, group=None):
        self.name = name
        self.group = group
        self.last_writer = None
        self.readers = []
        self.semkey = None


class Node:
    __slots__ = ("idx", "eng", "key", "fn", "deps", "is_dma", "needs_inc",
                 "inc_value", "vclock", "tag")

    def __init__(self, idx, eng, fn, tag):
        self.idx = idx
        self.eng = eng
        self.key = eng
        self.fn = fn
        self.deps = set()
        self.is_dma = False
        self.needs_inc = False
        self.inc_value = None
        self.vclock = None
        self.tag = tag


class Prog:
    def __init__(self):
        self.nodes = []
        self.per_engine = {e: [] for e in ENGINE_NAMES}
        self.group_keys = set()
        self.last_by_key = {}
        self._pending = {e: [] for e in ENGINE_NAMES}
        self.n_waits = 0
        self.n_incs = 0

    def buf(self, name, group=None):
        return Buf(name, group)

    def emit(self, eng, fn, reads=(), writes=(), extra_deps=(), tag=""):
        n = Node(len(self.nodes), eng, fn, tag)
        if self._pending[eng]:
            for d in self._pending[eng]:
                if d is not None and d is not n:
                    n.deps.add(d)
            self._pending[eng] = []
        for b in reads:
            if b.last_writer is not None:
                n.deps.add(b.last_writer)
            b.readers.append(n)
        for b in writes:
            for r in b.readers:
                if r is not n:
                    n.deps.add(r)
            if b.last_writer is not None and b.last_writer is not n:
                n.deps.add(b.last_writer)
            b.last_writer = n
            b.readers = [n] if n in b.readers else []
        for d in extra_deps:
            if d is not None and d is not n:
                n.deps.add(d)
        self.nodes.append(n)
        self.per_engine[eng].append(n)
        self.last_by_key[n.key] = n
        return n

    def dma(self, eng, fn, reads=(), writes=(), extra_deps=(), tag="dma"):
        n = self.emit(eng, fn, reads, writes, extra_deps, tag)
        n.is_dma = True
        b = writes[0]
        if b.semkey is None:
            b.semkey = ("g:" + b.group) if b.group else ("b:" + b.name)
            if b.group:
                self.group_keys.add(b.semkey)
        n.key = b.semkey
        self.last_by_key[n.key] = n
        return n

    def barrier(self):
        deps = list(self.last_by_key.values())
        for e in ENGINE_NAMES:
            self._pending[e] = list(deps)

    def build(self, nc, verbose=False):
        for n in self.nodes:
            for d in n.deps:
                if d.key != n.key or d.is_dma:
                    d.needs_inc = True
                elif d.eng == n.eng and n.eng != "tensor":
                    d.needs_inc = True
        for n in self.nodes:
            if n.is_dma:
                n.needs_inc = True
        sem_total = {}
        for n in self.nodes:
            if n.needs_inc:
                step = 16 if n.is_dma else 1
                sem_total[n.key] = sem_total.get(n.key, 0) + step
                n.inc_value = sem_total[n.key]
        last_dma_idx = {}
        for n in self.nodes:
            if n.is_dma and n.key in self.group_keys:
                last_dma_idx[n.key] = n.idx
        clocks = {e: {} for e in ENGINE_NAMES}
        waits = {}
        predrain = set()
        drained_upto = {e: 0 for e in ENGINE_NAMES}
        for n in self.nodes:
            ww = {}
            myclock = clocks[n.eng]
            for d in n.deps:
                if (n.is_dma and d.is_dma and d.key == n.key
                        and d.key in self.group_keys):
                    continue
                if d.eng == n.eng and not d.is_dma and n.eng == "tensor":
                    continue
                v = d.inc_value
                if d.key in self.group_keys:
                    v = sem_total[d.key]
                    assert last_dma_idx[d.key] < n.idx, \
                        f"group {d.key} DMA after waiter {n.tag}"
                if myclock.get(d.key, 0) >= v:
                    continue
                if ww.get(d.key, 0) < v:
                    ww[d.key] = v
            if ww:
                for pk, v in ww.items():
                    myclock[pk] = max(myclock.get(pk, 0), v)
                for d in n.deps:
                    if d.vclock and ww.get(d.key, 0) >= (d.inc_value or 0):
                        for k, kv in d.vclock.items():
                            if myclock.get(k, 0) < kv:
                                myclock[k] = kv
                waits[n] = sorted(ww.items())
                self.n_waits += len(ww)
            if n.needs_inc:
                vc = dict(clocks[n.eng])
                vc[n.key] = n.inc_value
                n.vclock = vc
                self.n_incs += 1
        self._wmap = waits
        self._predrain = predrain

        used_keys = sorted(set(n.key for n in self.nodes if n.needs_inc))
        with contextlib.ExitStack() as stack:
            sem_handles = {k: stack.enter_context(nc.semaphore(f"s_{k}"))
                           for k in used_keys}
            block = stack.enter_context(nc.Block())
            prog = self

            def mk_runner(ename):
                def run(engobj):
                    for n in prog.per_engine[ename]:
                        if n in prog._predrain:
                            engobj.drain()
                        w = prog._wmap.get(n)
                        if w:
                            for pk, v in w:
                                engobj.wait_ge(sem_handles[pk], v)
                        inst = n.fn(engobj)
                        if n.needs_inc:
                            inst.then_inc(sem_handles[n.key],
                                          16 if n.is_dma else 1)
                return run

            for e in ENGINE_NAMES:
                if self.per_engine[e]:
                    getattr(block, e)(mk_runner(e))
        if verbose:
            print(f"[fw] nodes={len(self.nodes)} incs={self.n_incs} "
                  f"waits={self.n_waits} drains={len(predrain)} "
                  f"sems={len(used_keys)}")
        return sem_total


def build_program(debug=(), verbose=False):
    nc = bass.Bass("TRN2", target_bir_lowering=False, debug=False)
    p = Prog()

    xin = nc.dram_tensor("xin", [NTOK, D], F32, kind="ExternalInput").ap()
    wqt = nc.dram_tensor("wqt", [D, D], F32, kind="ExternalInput").ap()
    wkt = nc.dram_tensor("wkt", [D, D], F32, kind="ExternalInput").ap()
    wvt = nc.dram_tensor("wvt", [D, D], F32, kind="ExternalInput").ap()
    wot = nc.dram_tensor("wot", [D, D], F32, kind="ExternalInput").ap()
    wwt = nc.dram_tensor("wwt", [D, H], F32, kind="ExternalInput").ap()
    wet = nc.dram_tensor("wet", [D, H], F32, kind="ExternalInput").ap()
    bwc = nc.dram_tensor("bwc", [H, 1], F32, kind="ExternalInput").ap()
    bec = nc.dram_tensor("bec", [H, 1], F32, kind="ExternalInput").ap()
    fct = nc.dram_tensor("fct", [D, DF], F32, kind="ExternalInput").ap()
    pjt = nc.dram_tensor("pjt", [DF, D], F32, kind="ExternalInput").ap()
    i128bf = nc.dram_tensor("i128bf", [128, 128], BF16, kind="ExternalInput").ap()
    i128f = nc.dram_tensor("i128f", [128, 128], F32, kind="ExternalInput").ap()
    i16f = nc.dram_tensor("i16f", [16, 16], F32, kind="ExternalInput").ap()
    mpre = nc.dram_tensor("mpre", [128, 128], F32, kind="ExternalInput").ap()
    ones16 = nc.dram_tensor("ones16", [16, 128], F32, kind="ExternalInput").ap()
    galg = nc.dram_tensor("galg", [16, 2], F32, kind="ExternalInput").ap()
    out_t = nc.dram_tensor("out", [NTOK, D], F32, kind="ExternalOutput").ap()
    x1scr = nc.dram_tensor("x1scr", [NTOK, D], BF16, kind="Internal").ap()

    dbg = {}
    for name, shape, dt in [("xn", [NTOK, D], BF16), ("qt", [NTOK, D], BF16),
                            ("kt", [NTOK, D], BF16), ("vaug", [NTOK, 1040], BF16),
                            ("cc", [SEQS * NCK * 16, 128], F32),
                            ("bb", [SEQS * NCK * 16, 128], F32),
                            ("attn", [NTOK, D], BF16), ("x1", [NTOK, D], F32)]:
        if name in debug:
            dbg[name] = nc.dram_tensor("dbg_" + name, shape, dt,
                                       kind="ExternalOutput").ap()

    stack = contextlib.ExitStack()
    sb = lambda name, shape, dt: stack.enter_context(nc.sbuf_tensor(name, shape, dt))
    psalloc = lambda name: stack.enter_context(nc.psum_tensor(name, [128, 512], F32))

    c_i128bf = sb("c_i128bf", [128, 128], BF16)
    c_i128f = sb("c_i128f", [128, 128], F32)
    c_i16f = sb("c_i16f", [16, 16], F32)
    c_mpre = sb("c_mpre", [128, 128], F32)
    c_ones16 = sb("c_ones16", [16, 128], F32)
    c_galg = sb("c_galg", [16, 2], F32)
    c_bwc = sb("c_bwc", [16, 1], F32)
    c_nbw = sb("c_nbw", [16, 1], F32)
    c_bec = sb("c_bec", [16, 1], F32)
    pb = [psalloc(f"pb{i}") for i in range(8)]

    B = {}

    def buf(name, group=None):
        if name not in B:
            B[name] = p.buf(name, group)
        return B[name]

    pbb = [buf(f"pb{i}") for i in range(8)]

    c_m1 = sb("c_m1", [128, 1], F32)
    p.emit("vector", lambda e: e.memset(c_m1[:], -1.0),
           writes=[buf("m1w")], tag="m1set")
    m1b = buf("m1w")

    ln_cols = [sb(f"ln_cols{i}", [128, 8], F32) for i in range(2)]
    ln_cols_b = [buf("ln_cols0"), buf("ln_cols1")]
    l2inv = sb("l2inv", [128, 32], F32)
    l2nmu = sb("l2nmu", [128, 32], F32)
    l2invb = buf("l2inv")
    l2nmub = buf("l2nmu")

    def emit_ln(idx, x_ap, xbuf, out_ap, outbuf, tagp):
        par = idx % 2
        cl, clb = ln_cols[par], ln_cols_b[par]
        p.emit("scalar", lambda e, o=out_ap, i=x_ap, a=cl[:, 1:2]:
               e.activation(o, i, AF.Square, accum_out=a),
               reads=[xbuf], writes=[outbuf, clb], tag=f"{tagp}sq")
        p.emit("vector", lambda e, o=cl[:, 0:1], i=x_ap:
               e.reduce_sum(o, i, axis=mybir.AxisListType.X),
               reads=[xbuf], writes=[clb], tag=f"{tagp}sum")
        p.emit("vector", lambda e, o=cl[:, 2:3], i=cl[:, 0:1]:
               e.tensor_single_scalar(o, i, 1.0 / D, ALU.mult),
               reads=[clb], writes=[clb], tag=f"{tagp}mean")
        p.emit("vector", lambda e, o=cl[:, 3:4], i=cl[:, 2:3]:
               e.tensor_tensor(o, i, i, ALU.mult),
               reads=[clb], writes=[clb], tag=f"{tagp}msq")
        p.emit("vector", lambda e, o=cl[:, 4:5], i=cl[:, 1:2], m=cl[:, 3:4]:
               e.scalar_tensor_tensor(o, i, 1.0 / D, m, ALU.mult, ALU.subtract),
               reads=[clb], writes=[clb], tag=f"{tagp}var")
        p.emit("vector", lambda e, o=cl[:, 4:5], i=cl[:, 4:5]:
               e.tensor_single_scalar(o, i, 1e-5, ALU.add),
               reads=[clb], writes=[clb], tag=f"{tagp}vev")
        p.emit("scalar", lambda e, o=cl[:, 4:5], i=cl[:, 4:5]:
               e.activation(o, i, AF.Ln),
               reads=[clb], writes=[clb], tag=f"{tagp}ln")
        p.emit("vector", lambda e, o=cl[:, 4:5], i=cl[:, 4:5]:
               e.tensor_single_scalar(o, i, -0.5, ALU.mult),
               reads=[clb], writes=[clb], tag=f"{tagp}nh")
        p.emit("scalar", lambda e, o=cl[:, 4:5], i=cl[:, 4:5]:
               e.activation(o, i, AF.Exp),
               reads=[clb], writes=[clb], tag=f"{tagp}ex")
        p.emit("vector", lambda e, o=cl[:, 5:6], i=cl[:, 2:3], v=cl[:, 4:5]:
               e.tensor_scalar(o, i, v, -1.0, ALU.mult, ALU.mult),
               reads=[clb], writes=[clb], tag=f"{tagp}nmu")
        p.emit("scalar", lambda e, o=out_ap, i=x_ap, v=cl[:, 4:5],
               b2=cl[:, 5:6]:
               e.activation(o, i, AF.Identity, scale=v, bias=b2),
               reads=[xbuf, clb], writes=[outbuf], tag=f"{tagp}app")

    for cap, src in [(c_i128bf, i128bf), (c_i128f, i128f), (c_i16f, i16f),
                     (c_mpre, mpre), (c_ones16, ones16), (c_bwc, bwc),
                     (c_bec, bec), (c_galg, galg)]:
        p.dma("sync", lambda e, o=cap, s=src: e.dma_start(out=o[:], in_=s),
              writes=[buf("consts", group="consts")])
    cb = buf("consts")
    p.emit("vector", lambda e: e.tensor_scalar(c_nbw[:], c_bwc[:], -1.0, None,
                                               ALU.mult),
           reads=[cb], writes=[buf("nbwd")], tag="negbw")
    nbwb = buf("nbwd")

    with contextlib.ExitStack() as sA:
        sbA = lambda name, shape, dt: sA.enter_context(nc.sbuf_tensor(name, shape, dt))
        stg = [sbA(f"stg{i}", [128, D], F32) for i in range(4)]
        l2sm = sbA("l2sm", [128, 32], F32)
        l2ss = sbA("l2ss", [128, 32], F32)
        l2tmp = sbA("l2tmp", [128, 32], F32)
        l2smb, l2ssb, l2tmpb = buf("l2sm"), buf("l2ss"), buf("l2tmp")
        stgb = [buf(f"stg{i}") for i in range(4)]
        wq_sb = sbA("wq_sb", [128, 8 * D], BF16)
        wk_sb = sbA("wk_sb", [128, 8 * D], BF16)
        wv_sb = sbA("wv_sb", [128, 8 * D], BF16)
        wo_sb = sbA("wo_sb", [128, 8 * D], BF16)
        ww_sb = sbA("ww_sb", [128, 8 * H], BF16)
        we_sb = sbA("we_sb", [128, 8 * H], BF16)
        xn_bf = [sbA(f"xn_bf{i}", [128, D], BF16) for i in range(2)]
        xnT = [sbA(f"xnT{i}", [128, D], BF16) for i in range(2)]
        qt_c = [sbA(f"qt_c{i}", [128, D], BF16) for i in range(2)]
        kt_c = [sbA(f"kt_c{i}", [128, D], BF16) for i in range(2)]
        va_c = [sbA(f"va_c{i}", [128, H * 65], BF16) for i in range(2)]
        s_f32 = sbA("s_f32", [128, 8 * 65], F32)
        s_bf = sbA("s_bf", [128, 8 * 65], BF16)
        g_u = sbA("g_u", [16, 128], F32)
        g_lw1 = sbA("g_lw1", [16, 128], F32)
        g_t = sbA("g_t", [16, 128], F32)
        g_v = sbA("g_v", [16, 128], F32)
        g_t1 = sbA("g_t1", [16, 128], F32)
        g_logg = sbA("g_logg", [16, 128], F32)
        g_c = [sbA(f"g_c{i}", [16, 128], F32) for i in range(2)]
        g_b = sbA("g_b", [16, 128], F32)
        carry = sbA("carry", [16, 1], F32)
        cprev = sbA("cprev", [16, 1], F32)
        dcol = sbA("dcol", [16, 1], F32)
        asrc = sbA("asrc", [16, 128], F32)
        ct_c = [sbA(f"ct_c{i}", [128, 16], F32) for i in range(2)]
        bt_c = [sbA(f"bt_c{i}", [128, 16], F32) for i in range(2)]
        qd_c = [sbA(f"qd_c{i}", [128, 16], F32) for i in range(2)]
        kd_c = [sbA(f"kd_c{i}", [128, 16], F32) for i in range(2)]
        lam_c = [sbA(f"lam_c{i}", [128, 16], F32) for i in range(2)]
        qdt = sbA("qdt", [128, 16], F32)
        kdt = sbA("kdt", [128, 16], F32)
        scr_e = [sbA(f"scr_e{i}", [128, 512], F32) for i in range(2)]
        scr_m = [sbA(f"scr_m{i}", [128, 512], F32) for i in range(2)]
        tmpM = [sbA(f"tmpM{i}", [128, 128], F32) for i in range(8)]
        dec = [sbA(f"dec{i}", [128, 128], F32) for i in range(8)]
        effT = [sbA(f"effT{i}", [128, 128], BF16) for i in range(8)]
        kdb = [sbA(f"kdb{i}", [128, DH], BF16) for i in range(8)]
        r1s = [sbA(f"r1s{i}", [128, 65], F32) for i in range(8)]
        comb = [sbA(f"comb{i}", [128, 65], F32) for i in range(8)]
        recc = [sbA(f"recc{i}", [128, 1], F32) for i in range(8)]
        atm = [sbA(f"atm{i}", [128, D], BF16) for i in range(2)]
        att = [sbA(f"att{i}", [128, D], BF16) for i in range(2)]
        x1s = [sbA(f"x1s{i}", [128, D], F32) for i in range(2)]

        def wload(dst, src, cdim, fdim, name):
            inap = src.rearrange("(c p) f -> p c f", p=128)
            outap = dst[:].rearrange("p (c f) -> p c f", f=fdim)
            p.dma("gpsimd", lambda e, o=outap, i=inap: e.dma_start(out=o, in_=i),
                  writes=[buf(name)], tag="dma_w_" + name)
            return buf(name)

        wqb = wload(wq_sb, wqt, 8, D, "wq")
        wkb = wload(wk_sb, wkt, 8, D, "wk")
        wvb = wload(wv_sb, wvt, 8, D, "wv")
        wob = wload(wo_sb, wot, 8, D, "wo")
        wwb = wload(ww_sb, wwt, 8, H, "ww")
        web = wload(we_sb, wet, 8, H, "we")

        sbuf_bufs = {}

        def tb(obj, name):
            key = name
            if key not in sbuf_bufs:
                sbuf_bufs[key] = buf(key)
            return sbuf_bufs[key]

        stg_b = stgb
        xnbf_b = [buf("xnbf0"), buf("xnbf1")]
        xnT_b = [buf("xnT0"), buf("xnT1")]
        qt_b = [buf("qt0"), buf("qt1")]
        kt_b = [buf("kt0"), buf("kt1")]
        va_b = [buf("va0"), buf("va1")]
        sf_b = [buf(f"sf{h}") for h in range(H)]
        sbf_b = [buf(f"sbf{h}") for h in range(H)]
        gate_b = {nm: buf("g_" + nm) for nm in
                  ["u", "lw1", "t", "v", "t1", "logg", "b", "carry", "cprev",
                   "dcol", "asrc", "qdt", "kdt"]}
        gc_b = [buf("gc0"), buf("gc1")]
        ct_b = [buf("ct0"), buf("ct1")]
        bt_b = [buf("bt0"), buf("bt1")]
        qd_b = [buf("qd0"), buf("qd1")]
        kd_b = [buf("kd0"), buf("kd1")]
        lam_b = [buf("lam0"), buf("lam1")]
        scre_b = [buf("scre0"), buf("scre1")]
        scrm_b = [buf("scrm0"), buf("scrm1")]
        tmpM_b = [buf(f"tmpM{i}") for i in range(8)]
        dec_b = [buf(f"dec{i}") for i in range(8)]
        effT_b = [buf(f"effT{i}") for i in range(8)]
        kdb_b = [buf(f"kdb{i}") for i in range(8)]
        r1_b = [buf(f"r1{i}") for i in range(8)]
        comb_b = [buf(f"comb{i}") for i in range(8)]
        recc_b = [buf(f"recc{i}") for i in range(8)]
        atm_b = [buf("atm0"), buf("atm1")]
        att_b = [buf("att0"), buf("att1")]
        x1s_b = [buf("x1s0"), buf("x1s1")]

        def _ahead(gc):
            return (gc // NCK, gc % NCK, gc % 2, stg[gc % 4], stg_b[gc % 4])

        def stageA1a(gc):
            s, c, r, st, stb = _ahead(gc)
            p.dma("sync", lambda e, o=st, g=gc: e.dma_start(
                out=o[:], in_=xin[g * 128:(g + 1) * 128, :]),
                writes=[stb], tag=f"dma_x{gc}")
            emit_ln(gc, st[:], stb, xn_bf[r][:], xnbf_b[r], f"ln{gc}_")
            if "xn" in dbg:
                p.dma("sync", lambda e, i=xn_bf[r], g=gc: e.dma_start(
                    out=dbg["xn"][g * 128:(g + 1) * 128, :], in_=i[:]),
                    reads=[xnbf_b[r]], writes=[buf(f"dbg_xn_{r}")])
        def stageA1b(gc):
            s, c, r, st, stb = _ahead(gc)
            for d in range(8):
                tp = pb[2 + (d % 2)][:, 0:64].bitcast(BF16)
                tpb = pbb[2 + (d % 2)]
                p.emit("tensor", lambda e, o=tp, i=xn_bf[r][:, d * 128:(d + 1) * 128]:
                       e.transpose(o, i, c_i128bf[:]),
                       reads=[xnbf_b[r], cb], writes=[tpb], tag=f"trx{gc}_{d}")
                p.emit("vector", lambda e, o=xnT[r][:, d * 128:(d + 1) * 128], i=tp:
                       e.tensor_copy(o, i),
                       reads=[tpb], writes=[xnT_b[r]], tag=f"trxe{gc}_{d}")

        def stageA2(gc):
            s, c, r, st, stb = _ahead(gc)
            for (wsb_, wb_, bank, bankb, nm) in [
                    (ww_sb, wwb, pb[4], pbb[4], "zw"),
                    (we_sb, web, pb[5], pbb[5], "ze")]:
                for d in range(8):
                    p.emit("tensor", lambda e, o=bank[0:16, 0:128],
                           l=wsb_[:, d * H:(d + 1) * H],
                           rr=xnT[r][:, d * 128:(d + 1) * 128],
                           s0=(d == 0), s1=(d == 7):
                           e.matmul(o, l, rr, start=s0, stop=s1),
                           reads=[wb_, xnT_b[r]], writes=[bankb],
                           tag=f"mm{nm}{gc}_{d}")

            for (wsb, wb_, dst, dstb, nm) in [
                    (wq_sb, wqb, qt_c[r], qt_b[r], "q"),
                    (wk_sb, wkb, kt_c[r], kt_b[r], "k")]:
                for quad in range(2):
                    bank = pb[quad]
                    bankb = pbb[quad]
                    for fq in range(4):
                        fc = quad * 4 + fq
                        for d in range(8):
                            p.emit("tensor", lambda e,
                                   o=bank[:, fq * 128:(fq + 1) * 128],
                                   l=wsb[:, d * D + fc * 128: d * D + (fc + 1) * 128],
                                   rr=xnT[r][:, d * 128:(d + 1) * 128],
                                   s0=(d == 0), s1=(d == 7):
                                   e.matmul(o, l, rr, start=s0, stop=s1),
                                   reads=[wb_, xnT_b[r]], writes=[bankb],
                                   tag=f"mm{nm}{gc}_{fc}_{d}")
                    p.emit("scalar", lambda e, o=scr_e[quad], i=bank:
                           e.activation(o[:], i[:], AF.Exp),
                           reads=[bankb], writes=[scre_b[quad]],
                           tag=f"phiexp{nm}{gc}_{quad}")
                    p.emit("scalar", lambda e, o=scr_m[quad], i=bank:
                           e.activation(o[:], i[:], AF.Relu),
                           reads=[bankb], writes=[scrm_b[quad]],
                           tag=f"phirelu{nm}{gc}_{quad}")
                    p.emit("vector", lambda e, o=scr_e[quad], i=scr_e[quad]:
                           e.tensor_single_scalar(o[:], i[:], 1.0, ALU.min),
                           reads=[scre_b[quad]], writes=[scre_b[quad]],
                           tag=f"phimin{nm}{gc}_{quad}")
                    p.emit("gpsimd", lambda e,
                           o=dst[:, quad * 512:(quad + 1) * 512],
                           a=scr_m[quad], bb2=scr_e[quad]:
                           e.tensor_add(o, a[:], bb2[:]),
                           reads=[scrm_b[quad], scre_b[quad]], writes=[dstb],
                           tag=f"phiout{nm}{gc}_{quad}")
            if "qt" in dbg:
                p.dma("sync", lambda e, i=qt_c[r], g=gc: e.dma_start(
                    out=dbg["qt"][g * 128:(g + 1) * 128, :], in_=i[:]),
                    reads=[qt_b[r]], writes=[buf(f"dbg_qt_{r}")])
            if "kt" in dbg:
                p.dma("sync", lambda e, i=kt_c[r], g=gc: e.dma_start(
                    out=dbg["kt"][g * 128:(g + 1) * 128, :], in_=i[:]),
                    reads=[kt_b[r]], writes=[buf(f"dbg_kt_{r}")])

            for half in range(2):
                bank = pb[2 + half]
                bankb = pbb[2 + half]
                for d in range(8):
                    p.emit("tensor", lambda e, o=bank[:, 0:512],
                           l=xnT[r][:, d * 128:(d + 1) * 128],
                           rr=wv_sb[:, d * D + half * 512: d * D + half * 512 + 512],
                           s0=(d == 0), s1=(d == 7):
                           e.matmul(o, l, rr, start=s0, stop=s1),
                           reads=[wvb, xnT_b[r]], writes=[bankb],
                           tag=f"mmv{gc}_{half}_{d}")
                oap = va_c[r][:, half * 8 * 65:(half * 8 + 8) * 65] \
                    .rearrange("p (h v) -> p h v", v=65)[:, :, 0:64]
                p.emit("vector", lambda e, o=oap, i=bank[:, 0:512]:
                       e.tensor_copy(o, i),
                       reads=[bankb], writes=[va_b[r]], tag=f"vev{gc}_{half}")
            msap = va_c[r][:].rearrange("p (h v) -> p h v", v=65)[:, :, 64:65]
            p.emit("vector", lambda e, o=msap: e.memset(o, 1.0),
                   writes=[va_b[r]], tag=f"vones{gc}")
            if "vaug" in dbg:
                p.dma("sync", lambda e, i=va_c[r], g=gc: e.dma_start(
                    out=dbg["vaug"][g * 128:(g + 1) * 128, :], in_=i[:]),
                    reads=[va_b[r]], writes=[buf(f"dbg_va_{r}")])


        def stageA3(gc):
            s, c, r, st, stb = _ahead(gc)
            GB = gate_b
            p.emit("scalar", lambda e, o=g_u, i=pb[4][0:16, 0:128]:
                   e.activation(o[:], i, AF.Exp, bias=c_nbw[:], scale=-1.0),
                   reads=[pbb[4], nbwb], writes=[GB["u"]], tag=f"gu{gc}")
            p.emit("scalar", lambda e, o=g_lw1, i=g_u:
                   e.activation(o[:], i[:], AF.Ln, bias=1.0),
                   reads=[GB["u"]], writes=[GB["lw1"]], tag=f"glw{gc}")
            p.emit("vector", lambda e, o=g_t, i=pb[5][0:16, 0:128]:
                   e.tensor_scalar(o[:], i, c_bec[:], 80.0, ALU.add, ALU.min),
                   reads=[pbb[5], cb], writes=[GB["t"]], tag=f"gt{gc}")
            p.emit("scalar", lambda e, o=g_v, i=g_t:
                   e.activation(o[:], i[:], AF.Exp),
                   reads=[GB["t"]], writes=[GB["v"]], tag=f"gv{gc}")
            p.emit("scalar", lambda e, o=g_t1, i=g_v:
                   e.activation(o[:], i[:], AF.Ln, bias=1.0),
                   reads=[GB["v"]], writes=[GB["t1"]], tag=f"gt1{gc}")
            p.emit("scalar", lambda e, o=g_v, i=g_v:
                   e.activation(o[:], i[:], AF.Ln, bias=c_galg[:, 0:1],
                                scale=c_galg[:, 1:2]),
                   reads=[GB["v"], cb], writes=[GB["v"]], tag=f"gt2{gc}")
            p.emit("vector", lambda e, o=g_logg, a=g_v, b=g_t1:
                   e.tensor_sub(o[:], a[:], b[:]),
                   reads=[GB["v"], GB["t1"]], writes=[GB["logg"]],
                   tag=f"glogg{gc}")
            if c == 0:
                p.emit("vector", lambda e, o=cprev: e.memset(o[:], 0.0),
                       writes=[GB["cprev"]], tag=f"gcp{gc}")
            else:
                p.emit("vector", lambda e, o=cprev, i=carry:
                       e.tensor_copy(o[:], i[:]),
                       reads=[GB["carry"]], writes=[GB["cprev"]],
                       tag=f"gcp{gc}")
            p.emit("vector", lambda e, o=g_c[r], i=g_logg, ini=cprev:
                   e.tensor_tensor_scan(o[:], i[:], i[:], ini[:],
                                        ALU.add, ALU.bypass),
                   reads=[GB["logg"], GB["cprev"]], writes=[gc_b[r]],
                   tag=f"gscan{gc}")
            if "cc" in dbg:
                p.dma("sync", lambda e, i=g_c[r], g=gc: e.dma_start(
                    out=dbg["cc"][g * 16:(g + 1) * 16, :], in_=i[:]),
                    reads=[gc_b[r]], writes=[buf(f"dbg_cc_{r}")])
            p.emit("vector", lambda e, o=carry, i=g_c[r][:, 127:128]:
                   e.tensor_copy(o[:], i),
                   reads=[gc_b[r]], writes=[GB["carry"]], tag=f"gca{gc}")
            p.emit("vector", lambda e, o=g_b, a=g_lw1, bb_=g_c[r]:
                   e.scalar_tensor_tensor(o[:], a[:], -1.0, bb_[:],
                                          ALU.mult, ALU.subtract),
                   reads=[GB["lw1"], gc_b[r]], writes=[GB["b"]],
                   tag=f"gb{gc}")
            if "bb" in dbg:
                p.dma("sync", lambda e, i=g_b, g=gc: e.dma_start(
                    out=dbg["bb"][g * 16:(g + 1) * 16, :], in_=i[:]),
                    reads=[GB["b"]], writes=[buf(f"dbg_bb_{gc % 2}")])
            p.emit("tensor", lambda e, o=pb[4][:, 0:16], i=g_c[r]:
                   e.transpose(o, i[:], c_i16f[:]),
                   reads=[gc_b[r], cb], writes=[pbb[4]], tag=f"trc{gc}")
            p.emit("vector", lambda e, o=ct_c[r], i=pb[4][:, 0:16]:
                   e.tensor_copy(o[:], i),
                   reads=[pbb[4]], writes=[ct_b[r]], tag=f"trce{gc}")
            p.emit("tensor", lambda e, o=pb[5][:, 0:16], i=g_b:
                   e.transpose(o, i[:], c_i16f[:]),
                   reads=[GB["b"], cb], writes=[pbb[5]], tag=f"trb{gc}")
            p.emit("vector", lambda e, o=bt_c[r], i=pb[5][:, 0:16]:
                   e.tensor_copy(o[:], i),
                   reads=[pbb[5]], writes=[bt_b[r]], tag=f"trbe{gc}")
            p.emit("vector", lambda e, o=asrc, s1=cprev:
                   e.tensor_scalar(o[:], c_ones16[:], s1[:], None, ALU.mult),
                   reads=[cb, GB["cprev"]], writes=[GB["asrc"]],
                   tag=f"gas{gc}")
            p.emit("tensor", lambda e, o=pb[6][:, 0:16], i=asrc:
                   e.transpose(o, i[:], c_i16f[:]),
                   reads=[GB["asrc"], cb], writes=[pbb[6]], tag=f"trap{gc}")
            p.emit("vector", lambda e, o=qdt, a=ct_c[r], i=pb[6][:, 0:16]:
                   e.tensor_sub(o[:], a[:], i),
                   reads=[ct_b[r], pbb[6]], writes=[GB["qdt"]],
                   tag=f"qdt{gc}")
            p.emit("scalar", lambda e, o=qd_c[r], i=qdt:
                   e.activation(o[:], i[:], AF.Exp),
                   reads=[GB["qdt"]], writes=[qd_b[r]], tag=f"qde{gc}")
            p.emit("vector", lambda e, o=asrc, s1=carry:
                   e.tensor_scalar(o[:], c_ones16[:], s1[:], None, ALU.mult),
                   reads=[cb, GB["carry"]], writes=[GB["asrc"]],
                   tag=f"gas2{gc}")
            p.emit("tensor", lambda e, o=pb[7][:, 0:16], i=asrc:
                   e.transpose(o, i[:], c_i16f[:]),
                   reads=[GB["asrc"], cb], writes=[pbb[7]], tag=f"tran{gc}")
            p.emit("vector", lambda e, o=kdt, a=bt_c[r], i=pb[7][:, 0:16]:
                   e.tensor_add(o[:], a[:], i),
                   reads=[bt_b[r], pbb[7]], writes=[GB["kdt"]],
                   tag=f"kdt{gc}")
            p.emit("scalar", lambda e, o=kd_c[r], i=kdt:
                   e.activation(o[:], i[:], AF.Exp),
                   reads=[GB["kdt"]], writes=[kd_b[r]], tag=f"kde{gc}")
            p.emit("vector", lambda e, o=dcol, a=carry, bb_=cprev:
                   e.tensor_sub(o[:], a[:], bb_[:]),
                   reads=[GB["carry"], GB["cprev"]], writes=[GB["dcol"]],
                   tag=f"dcol{gc}")
            p.emit("vector", lambda e, o=asrc, s1=dcol:
                   e.tensor_scalar(o[:], c_ones16[:], s1[:], None,
                                   ALU.mult),
                   reads=[cb, GB["dcol"]], writes=[GB["asrc"]],
                   tag=f"lsrc{gc}")
            p.emit("tensor", lambda e, o=pb[6][:, 16:32], i=asrc:
                   e.transpose(o, i[:], c_i16f[:]),
                   reads=[GB["asrc"], cb], writes=[pbb[6]], tag=f"trl{gc}")
            p.emit("scalar", lambda e, o=lam_c[r], i=pb[6][:, 16:32]:
                   e.activation(o[:], i, AF.Exp),
                   reads=[pbb[6]], writes=[lam_b[r]], tag=f"lam{gc}")

        def stageA4(gc):
            s, c, r, st, stb = _ahead(gc)
            GSZ = 4
            KNOINT = False
            KNOU = False
            for hg in range(H // GSZ):
                heads = list(range(hg * GSZ, hg * GSZ + GSZ))

                def HH(fn):
                    for hh, h in enumerate(heads):
                        po = 64 * (h % 2)
                        fc = h // 2
                        fn(hh, h, po, fc,
                           kt_c[r][po:po + 64, fc * 128:(fc + 1) * 128],
                           qt_c[r][po:po + 64, fc * 128:(fc + 1) * 128],
                           va_c[r][:, h * 65:(h + 1) * 65])

                def sc_ap(hh):
                    return pb[hh % 2][:, (hh // 2) * 128:(hh // 2) * 128 + 128]

                def crm_ap(hh):
                    return pb[2 + hh % 2][:, (hh // 2) * 128:(hh // 2) * 128 + 128]

                def ktm_ap(hh):
                    c0 = (hh // 2) * 128
                    return pb[2 + hh % 2][:, c0:c0 + 32].bitcast(BF16)

                def po_ap(hh):
                    return pb[4 + hh % 2][:, (hh // 2) * 65:(hh // 2) * 65 + 65]

                def pi_ap(hh):
                    return pb[6 + hh % 2][:, (hh // 2) * 65:(hh // 2) * 65 + 65]


                def pu_ap(hh, po):
                    c0 = 260 + (hh // 2) * 65
                    return pb[6 + hh % 2][po:po + 64, c0:c0 + 65]

                def pub(hh):
                    return pbb[6 + hh % 2]

                def emit_tmpM(hg2):
                    for hh2, h2 in enumerate(range(hg2 * GSZ,
                                                   hg2 * GSZ + GSZ)):
                        sl2 = hh2 + 4 * (hg2 % 2)
                        p.emit("vector", lambda e, o=tmpM[sl2],
                               s1=ct_c[r][:, h2:h2 + 1]:
                               e.tensor_scalar(o[:], c_mpre[:], s1, None,
                                               ALU.add),
                               reads=[cb, ct_b[r]], writes=[tmpM_b[sl2]],
                               tag=f"tmpM{gc}_{h2}")

                if hg == 0:
                    emit_tmpM(0)
                HH(lambda hh, h, po, fc, ktsl, qtsl, vasl:
                   p.emit("tensor", lambda e, o=sc_ap(hh), l=ktsl, rr=qtsl:
                          e.matmul(o, l, rr, start=True, stop=True),
                          reads=[kt_b[r], qt_b[r]], writes=[pbb[hh % 2]],
                          tag=f"mmsc{gc}_{h}"))
                HH(lambda hh, h, po, fc, ktsl, qtsl, vasl:
                   p.emit("tensor", lambda e, o=crm_ap(hh),
                          i=tmpM[hh + 4 * (hg % 2)]:
                          e.transpose(o, i[:], c_i128f[:]),
                          reads=[tmpM_b[hh + 4 * (hg % 2)], cb],
                          writes=[pbb[2 + hh % 2]],
                          tag=f"trcrm{gc}_{h}"))
                HH(lambda hh, h, po, fc, ktsl, qtsl, vasl:
                   p.emit("scalar", lambda e, o=dec[hh], i=crm_ap(hh),
                          bi=bt_c[r][:, h:h + 1]:
                          e.activation(o[:], i, AF.Exp, bias=bi),
                          reads=[pbb[2 + hh % 2], bt_b[r]], writes=[dec_b[hh]],
                          tag=f"dec{gc}_{h}"))
                HH(lambda hh, h, po, fc, ktsl, qtsl, vasl:
                   p.emit("vector", lambda e, o=effT[hh], a=sc_ap(hh),
                          bb_=dec[hh]: e.tensor_mul(o[:], a, bb_[:]),
                          reads=[pbb[hh % 2], dec_b[hh]], writes=[effT_b[hh]],
                          tag=f"eff{gc}_{h}"))
                if hg + 1 < H // GSZ:
                    emit_tmpM(hg + 1)
                HH(lambda hh, h, po, fc, ktsl, qtsl, vasl:
                   p.emit("tensor", lambda e, o=ktm_ap(hh), i=ktsl,
                          idn=c_i128bf[po:po + 64, po:po + 64]:
                          e.transpose(o, i, idn),
                          reads=[kt_b[r], cb, dec_b[hh]],
                          writes=[pbb[2 + hh % 2]], tag=f"trktm{gc}_{h}"))
                HH(lambda hh, h, po, fc, ktsl, qtsl, vasl:
                   p.emit("vector", lambda e, o=kdb[hh], i=ktm_ap(hh),
                          sc=kd_c[r][:, h:h + 1]:
                          e.tensor_scalar(o[:], i, sc, None, ALU.mult),
                          reads=[pbb[2 + hh % 2], kd_b[r]], writes=[kdb_b[hh]],
                          tag=f"kdscale{gc}_{h}"))
                HH(lambda hh, h, po, fc, ktsl, qtsl, vasl:
                   p.emit("tensor", lambda e, o=po_ap(hh), l=effT[hh], rr=vasl:
                          e.matmul(o, l[:], rr, start=True, stop=True),
                          reads=[effT_b[hh], va_b[r]], writes=[pbb[4 + hh % 2]],
                          tag=f"mmdiag{gc}_{h}"))
                if c > 0 and not KNOINT:
                    HH(lambda hh, h, po, fc, ktsl, qtsl, vasl:
                       p.emit("tensor", lambda e, o=pi_ap(hh), l=qtsl,
                              rr=s_bf[po:po + 64, fc * 65:(fc + 1) * 65]:
                              e.matmul(o, l, rr, start=True, stop=True),
                              reads=[qt_b[r], sbf_b[h]],
                              writes=[pbb[6 + hh // 4]],
                              tag=f"mmint{gc}_{h}"))
                KNOU or HH(lambda hh, h, po, fc, ktsl, qtsl, vasl:
                   p.emit("tensor", lambda e, o=pu_ap(hh, po), l=kdb[hh],
                          rr=vasl: e.matmul(o, l[:], rr, start=True, stop=True),
                          reads=[kdb_b[hh], va_b[r]], writes=[pbb[6 + hh // 4]],
                          tag=f"mmu{gc}_{h}"))
                if c > 0 and not KNOINT:
                    HH(lambda hh, h, po, fc, ktsl, qtsl, vasl:
                       p.emit("vector", lambda e, o=r1s[hh], i=pi_ap(hh),
                              sc=qd_c[r][:, h:h + 1]:
                              e.tensor_scalar(o[:], i, sc, None, ALU.mult),
                              reads=[pbb[6 + hh // 4], qd_b[r]],
                              writes=[r1_b[hh]], tag=f"r1{gc}_{h}"))
                    HH(lambda hh, h, po, fc, ktsl, qtsl, vasl:
                       p.emit("vector", lambda e, o=comb[hh], a=po_ap(hh),
                              bb_=r1s[hh]: e.tensor_add(o[:], a, bb_[:]),
                              reads=[pbb[4 + hh % 2], r1_b[hh]],
                              writes=[comb_b[hh]], tag=f"comb{gc}_{h}"))
                if c == 0 or KNOINT:
                    HH(lambda hh, h, po, fc, ktsl, qtsl, vasl:
                       p.emit("vector", lambda e, o=comb[hh], i=po_ap(hh):
                              e.tensor_copy(o[:], i),
                              reads=[pbb[4 + hh % 2]], writes=[comb_b[hh]],
                              tag=f"comb{gc}_{h}"))
                if KNOU:
                    pass
                elif c == 0:
                    HH(lambda hh, h, po, fc, ktsl, qtsl, vasl:
                       p.emit("vector", lambda e,
                              o=s_f32[po:po + 64, fc * 65:(fc + 1) * 65],
                              i=pu_ap(hh, po): e.tensor_copy(o, i),
                              reads=[pbb[6 + hh // 4]], writes=[sf_b[h]],
                              tag=f"sini{gc}_{h}"))
                else:
                    HH(lambda hh, h, po, fc, ktsl, qtsl, vasl:
                       p.emit("vector", lambda e,
                              o=s_f32[po:po + 64, fc * 65:(fc + 1) * 65],
                              sc=lam_c[r][po:po + 64, h:h + 1]:
                              e.tensor_scalar(o, o, sc, None, ALU.mult),
                              reads=[lam_b[r], sf_b[h]], writes=[sf_b[h]],
                              tag=f"slam{gc}_{h}"))
                    KNOU or \
                    HH(lambda hh, h, po, fc, ktsl, qtsl, vasl:
                       p.emit("vector", lambda e,
                              o=s_f32[po:po + 64, fc * 65:(fc + 1) * 65],
                              u=pu_ap(hh, po): e.tensor_add(o, o, u),
                              reads=[pbb[6 + hh // 4], sf_b[h]],
                              writes=[sf_b[h]], tag=f"supd{gc}_{h}"))
                HH(lambda hh, h, po, fc, ktsl, qtsl, vasl:
                   p.emit("gpsimd", lambda e,
                          o=s_bf[po:po + 64, fc * 65:(fc + 1) * 65],
                          i=s_f32[po:po + 64, fc * 65:(fc + 1) * 65]:
                          e.tensor_copy(o, i),
                          reads=[sf_b[h]], writes=[sbf_b[h]],
                          tag=f"scast{gc}_{h}"))
                HH(lambda hh, h, po, fc, ktsl, qtsl, vasl:
                   p.emit("vector", lambda e, o=recc[hh], i=comb[hh][:, 64:65]:
                          e.tensor_single_scalar(o[:], i, 1e-6, ALU.add),
                          reads=[comb_b[hh]], writes=[recc_b[hh]],
                          tag=f"eps{gc}_{h}"))
                HH(lambda hh, h, po, fc, ktsl, qtsl, vasl:
                   p.emit("vector", lambda e, o=recc[hh]:
                          e.reciprocal(o[:], o[:]),
                          reads=[recc_b[hh]], writes=[recc_b[hh]],
                          tag=f"rec{gc}_{h}"))
                HH(lambda hh, h, po, fc, ktsl, qtsl, vasl:
                   p.emit("scalar", lambda e, o=atm[r][:, h * 64:(h + 1) * 64],
                          i=comb[hh][:, 0:64], sc=recc[hh]:
                          e.activation(o, i, AF.Identity, scale=sc[:]),
                          reads=[comb_b[hh], recc_b[hh]], writes=[atm_b[r]],
                          tag=f"osc{gc}_{h}"))

            if "attn" in dbg:
                p.dma("sync", lambda e, i=atm[r], g=gc: e.dma_start(
                    out=dbg["attn"][g * 128:(g + 1) * 128, :], in_=i[:]),
                    reads=[atm_b[r]], writes=[buf(f"dbg_at_{r}")])

        def stageA5(gc):
            s, c, r, st, stb = _ahead(gc)
            for d in range(8):
                tp = pb[2 + (d % 2)][:, 0:64].bitcast(BF16)
                tpb = pbb[2 + (d % 2)]
                p.emit("tensor", lambda e, o=tp,
                       i=atm[r][:, d * 128:(d + 1) * 128]:
                       e.transpose(o, i, c_i128bf[:]),
                       reads=[atm_b[r], cb], writes=[tpb],
                       tag=f"trat{gc}_{d}")
                p.emit("vector", lambda e, o=att[r][:, d * 128:(d + 1) * 128],
                       i=tp: e.tensor_copy(o, i),
                       reads=[tpb], writes=[att_b[r]], tag=f"trate{gc}_{d}")
            for half in range(2):
                bank = pb[half]
                bankb = pbb[half]
                for d in range(8):
                    p.emit("tensor", lambda e, o=bank[:, 0:512],
                           l=att[r][:, d * 128:(d + 1) * 128],
                           rr=wo_sb[:, d * D + half * 512: d * D + half * 512 + 512],
                           s0=(d == 0), s1=(d == 7):
                           e.matmul(o, l, rr, start=s0, stop=s1),
                           reads=[att_b[r], wob], writes=[bankb],
                           tag=f"mmwo{gc}_{half}_{d}")
                p.emit("vector", lambda e,
                       o=x1s[r][:, half * 512:half * 512 + 512],
                       a=bank[:, 0:512],
                       bb_=st[:, half * 512:half * 512 + 512]:
                       e.tensor_add(o, a, bb_),
                       reads=[bankb, stb], writes=[x1s_b[r]],
                       tag=f"x1add{gc}_{half}")
            p.dma("gpsimd", lambda e, i=x1s[r], g=gc: e.dma_start(
                out=x1scr[g * 128:(g + 1) * 128, :], in_=i[:]),
                reads=[x1s_b[r]], writes=[buf(f"x1w{r}")],
                tag=f"dma_x1w{gc}")
        def stageA5stats(gc):
            s, c, r, st, stb = _ahead(gc)
            p.emit("scalar", lambda e, o=atm[r], i=x1s[r],
                   a=l2ss[:, gc:gc + 1]:
                   e.activation(o[:], i[:], AF.Square, accum_out=a),
                   reads=[x1s_b[r]], writes=[atm_b[r], l2ssb],
                   tag=f"l2sq{gc}")
            p.emit("vector", lambda e, o=l2sm[:, gc:gc + 1], i=x1s[r]:
                   e.reduce_sum(o, i[:], axis=mybir.AxisListType.X),
                   reads=[x1s_b[r]], writes=[l2smb], tag=f"l2sum{gc}")
            if "x1" in dbg:
                p.dma("sync", lambda e, i=x1s[r], g=gc: e.dma_start(
                    out=dbg["x1"][g * 128:(g + 1) * 128, :], in_=i[:]),
                    reads=[x1s_b[r]], writes=[buf(f"dbg_x1_{r}")])

        NCA = NCK * SEQS
        for gc in range(NCA):
            if gc == 0:
                stageA1a(0)
                stageA1b(0)
            if gc + 1 < NCA:
                stageA1a(gc + 1)
            stageA2(gc)
            if gc > 0:
                stageA5(gc - 1)
            stageA3(gc)
            if gc + 1 < NCA:
                stageA1b(gc + 1)
            stageA4(gc)
            if gc > 0:
                stageA5stats(gc - 1)
        stageA5(NCA - 1)
        stageA5stats(NCA - 1)

        p.emit("vector", lambda e: e.tensor_single_scalar(
               l2sm[:], l2sm[:], 1.0 / D, ALU.mult),
               reads=[l2smb], writes=[l2smb], tag="l2mean")
        p.emit("vector", lambda e: e.tensor_tensor(
               l2tmp[:], l2sm[:], l2sm[:], ALU.mult),
               reads=[l2smb], writes=[l2tmpb], tag="l2msq")
        p.emit("vector", lambda e: e.scalar_tensor_tensor(
               l2ss[:], l2ss[:], 1.0 / D, l2tmp[:], ALU.mult, ALU.subtract),
               reads=[l2ssb, l2tmpb], writes=[l2ssb], tag="l2var")
        p.emit("vector", lambda e: e.tensor_single_scalar(
               l2ss[:], l2ss[:], 1e-5, ALU.add),
               reads=[l2ssb], writes=[l2ssb], tag="l2vev")
        p.emit("scalar", lambda e: e.activation(l2ss[:], l2ss[:], AF.Ln),
               reads=[l2ssb], writes=[l2ssb], tag="l2ln")
        p.emit("vector", lambda e: e.tensor_single_scalar(
               l2ss[:], l2ss[:], -0.5, ALU.mult),
               reads=[l2ssb], writes=[l2ssb], tag="l2nh")
        p.emit("scalar", lambda e: e.activation(l2inv[:], l2ss[:], AF.Exp),
               reads=[l2ssb], writes=[l2invb], tag="l2ex")
        p.emit("vector", lambda e: e.scalar_tensor_tensor(
               l2nmu[:], l2sm[:], -1.0, l2inv[:], ALU.mult, ALU.mult),
               reads=[l2smb, l2invb], writes=[l2nmub], tag="l2nmu")

        p.barrier()
        sA.close()

    fc_sb = sb("fc_sb", [128, 8 * DF], BF16)
    pj_sb = sb("pj_sb", [128, 32 * D], BF16)
    ht_sb = sb("ht_sb", [128, 32 * 256], BF16)
    xn2T = [sb(f"xn2T{i}", [128, 8 * 256], BF16) for i in range(2)]
    x1st = [sb(f"x1st{i}", [128, D], BF16) for i in range(6)]
    xn2b = [sb(f"xn2b{i}", [128, D], BF16) for i in range(4)]
    y_sb = [sb(f"y_sb{i}", [128, 512], F32) for i in range(4)]
    sn_s = [sb(f"sn_s{i}", [128, 512], F32) for i in range(4)]
    sn_a = [sb(f"sn_a{i}", [128, 512], BF16) for i in range(4)]
    sn_m = [sb(f"sn_m{i}", [128, 512], F32) for i in range(4)]
    rh_sb = [sb(f"rh_sb{i}", [128, 256], F32) for i in range(4)]

    fcb, pjb = [], []
    for i in range(8):
        inap = fct.rearrange("(c p) f -> p c f", p=128)[:, :, i * 512:(i + 1) * 512]
        outap = fc_sb[:].rearrange("p (c f) -> p c f", f=DF)[:, :, i * 512:(i + 1) * 512]
        p.dma("gpsimd", lambda e, o=outap, s=inap: e.dma_start(out=o, in_=s),
              writes=[buf(f"fc{i}")], tag=f"dma_fc{i}")
        fcb.append(buf(f"fc{i}"))
    for i in range(8):
        inap = pjt.rearrange("(c p) f -> p c f", p=128)[:, i * 4:(i + 1) * 4, :]
        outap = pj_sb[:].rearrange("p (c f) -> p c f", f=D)[:, i * 4:(i + 1) * 4, :]
        p.dma("gpsimd", lambda e, o=outap, s=inap: e.dma_start(out=o, in_=s),
              writes=[buf(f"pj{i}")], tag=f"dma_pj{i}")
        pjb.append(buf(f"pj{i}"))

    htb = buf("ht")
    xn2T_b = [buf("xn2T0"), buf("xn2T1")]
    x1st_b = [buf(f"x1st{i}") for i in range(6)]
    xn2b_b = [buf(f"xn2b{i}") for i in range(4)]
    y_b = [buf(f"y{i}") for i in range(4)]
    sns_b = [buf(f"sns{i}") for i in range(4)]
    sna_b = [buf(f"sna{i}") for i in range(4)]
    snm_b = [buf(f"snm{i}") for i in range(4)]
    rh_b = [buf(f"rh{i}") for i in range(4)]
    x1r_deps = [buf("x1w0"), buf("x1w1")]

    def stageP1a(grp):
        gr = grp % 2
        for j in range(2):
            cc = grp * 2 + j
            stb_ = x1st_b[cc % 6]
            st_ = x1st[cc % 6]
            p.dma("scalar", lambda e, o=st_, g=cc: e.dma_start(
                out=o[:], in_=x1scr[g * 128:(g + 1) * 128, :]),
                reads=x1r_deps, writes=[stb_], tag=f"dma_x1r{cc}")
            p.emit("scalar", lambda e, o=xn2b[cc % 4][:], i=st_[:],
                   sc=l2inv[:, cc:cc + 1], b2=l2nmu[:, cc:cc + 1]:
                   e.activation(o, i, AF.Identity, scale=sc, bias=b2),
                   reads=[stb_, l2invb, l2nmub], writes=[xn2b_b[cc % 4]],
                   tag=f"ln2app{cc}")

    def stageP1b(grp):
        gr = grp % 2
        for j in range(2):
            cc = grp * 2 + j
            for d in range(8):
                tp = pb[2 + (d % 2)][:, 0:64].bitcast(BF16)
                tpb = pbb[2 + (d % 2)]
                p.emit("tensor", lambda e, o=tp,
                       i=xn2b[cc % 4][:, d * 128:(d + 1) * 128]:
                       e.transpose(o, i, c_i128bf[:]),
                       reads=[xn2b_b[cc % 4], cb], writes=[tpb],
                       tag=f"tr2_{cc}_{d}")
                p.emit("scalar", lambda e,
                       o=xn2T[gr][:, d * 256 + j * 128: d * 256 + j * 128 + 128],
                       i=tp: e.activation(o, i, AF.Copy),
                       reads=[tpb], writes=[xn2T_b[gr]], tag=f"tr2e{cc}_{d}")

    def stageP2(grp):
        gr = grp % 2
        for fcx in range(32):
            bank = pb[fcx % 4]
            bankb = pbb[fcx % 4]
            for d in range(8):
                p.emit("tensor", lambda e, o=bank[:, 0:256],
                       l=fc_sb[:, d * DF + fcx * 128: d * DF + (fcx + 1) * 128],
                       rr=xn2T[gr][:, d * 256:(d + 1) * 256],
                       s0=(d == 0), s1=(d == 7):
                       e.matmul(o, l, rr, start=s0, stop=s1),
                       reads=[fcb[fcx // 4], xn2T_b[gr]], writes=[bankb],
                       tag=f"mmh{grp}_{fcx}_{d}")
            p.emit("scalar", lambda e, o=rh_sb[fcx % 4], i=bank[:, 0:256]:
                   e.activation(o[:], i, AF.Relu),
                   reads=[bankb], writes=[rh_b[fcx % 4]], tag=f"relu{grp}_{fcx}")
            engm = "gpsimd"
            p.emit(engm, lambda e, o=ht_sb[:, fcx * 256:(fcx + 1) * 256],
                   i=rh_sb[fcx % 4]: e.tensor_mul(o, i[:], i[:]),
                   reads=[rh_b[fcx % 4]], writes=[htb], tag=f"sq{grp}_{fcx}")

    def stageP3(grp):
        gr = grp % 2
        tiles = []
        for j in range(2):
            cc = grp * 2 + j
            for half in range(2):
                t = 2 * j + half
                bank = pb[4 + t]
                bankb = pbb[4 + t]
                for fcx in range(32):
                    p.emit("tensor", lambda e, o=bank[:, 0:512],
                           l=ht_sb[:, fcx * 256 + j * 128: fcx * 256 + j * 128 + 128],
                           rr=pj_sb[:, fcx * D + half * 512: fcx * D + half * 512 + 512],
                           s0=(fcx == 0), s1=(fcx == 31):
                           e.matmul(o, l, rr, start=s0, stop=s1),
                           reads=[htb, pjb[fcx // 4]],
                           writes=[bankb], tag=f"mmy{cc}_{half}_{fcx}")
                tiles.append((t, cc, j, half, bank, bankb))

        def each(fn):
            for (tp2, cc, j, half, bank, bankb) in tiles:
                fn(tp2, cc, j, half, bank, bankb)

        each(lambda tp2, cc, j, half, bank, bankb:
             p.emit("vector", lambda e, o=y_sb[tp2], i=bank[:, 0:512]:
                    e.tensor_copy(o[:], i),
                    reads=[bankb], writes=[y_b[tp2]], tag=f"yev{cc}_{half}"))

        def sg(src_list):
            each(lambda tp2, cc, j, half, bank, bankb:
                 p.emit("vector", lambda e, o=sn_s[tp2], i=src_list[tp2]:
                        e.tensor_single_scalar(o[:], i[:], 1.0, ALU.is_gt),
                        reads=[(y_b if src_list is y_sb else snm_b)[tp2]],
                        writes=[sns_b[tp2]], tag=f"sg{cc}_{half}"))

        sg(y_sb)
        each(lambda tp2, cc, j, half, bank, bankb:
             p.emit("vector", lambda e, o=sn_a[tp2], i=sn_s[tp2]:
                    e.tensor_copy(o[:], i[:]),
                    reads=[sns_b[tp2]], writes=[sna_b[tp2]], tag=f"sac{cc}_{half}"))
        each(lambda tp2, cc, j, half, bank, bankb:
             p.emit("vector", lambda e, o=sn_m[tp2], i=y_sb[tp2],
                    s2=sn_s[tp2]:
                    e.scalar_tensor_tensor(o[:], i[:], 1.9, s2[:],
                                           ALU.mult, ALU.subtract),
                    reads=[y_b[tp2], sns_b[tp2]], writes=[snm_b[tp2]],
                    tag=f"sn3{cc}_{half}"))
        for k2 in range(3):
            sg(sn_m)
            each(lambda tp2, cc, j, half, bank, bankb:
                 p.emit("vector", lambda e, o=sn_a[tp2], i=sn_s[tp2]:
                        e.tensor_add(o[:], o[:], i[:]),
                        reads=[sns_b[tp2], sna_b[tp2]], writes=[sna_b[tp2]],
                        tag=f"sad{k2}_{cc}_{half}"))
            each(lambda tp2, cc, j, half, bank, bankb:
                 p.emit("vector", lambda e, o=sn_m[tp2], s2=sn_s[tp2]:
                        e.scalar_tensor_tensor(o[:], o[:], 0.9, s2[:],
                                               ALU.mult, ALU.subtract),
                        reads=[sns_b[tp2], snm_b[tp2]], writes=[snm_b[tp2]],
                        tag=f"sw{k2}_{cc}_{half}"))
            each(lambda tp2, cc, j, half, bank, bankb:
                 p.emit("vector", lambda e, o=sn_m[tp2], y2=y_sb[tp2]:
                        e.tensor_add(o[:], o[:], y2[:]),
                        reads=[y_b[tp2], snm_b[tp2]], writes=[snm_b[tp2]],
                        tag=f"swy{k2}_{cc}_{half}"))
        sg(sn_m)
        each(lambda tp2, cc, j, half, bank, bankb:
             p.emit("vector", lambda e, o=sn_a[tp2], i=sn_s[tp2]:
                    e.tensor_add(o[:], o[:], i[:]),
                    reads=[sns_b[tp2], sna_b[tp2]], writes=[sna_b[tp2]],
                    tag=f"sa9{cc}_{half}"))
        each(lambda tp2, cc, j, half, bank, bankb:
             p.emit("vector", lambda e, o=sn_m[tp2], i=sn_a[tp2]:
                    e.tensor_single_scalar(o[:], i[:], 0.2, ALU.mult),
                    reads=[sna_b[tp2]], writes=[snm_b[tp2]], tag=f"so1{cc}_{half}"))
        def snout(tp2, cc, j, half, bank, bankb):
            p.emit("vector", lambda e, o=y_sb[tp2][:, 0:512], a=sn_m[tp2],
                   x1=x1st[cc % 6][:, half * 512:half * 512 + 512]:
                   e.tensor_add(o, a[:], x1),
                   reads=[snm_b[tp2], x1st_b[cc % 6]],
                   writes=[y_b[tp2]], tag=f"snout{cc}_{half}")
            p.dma("sync", lambda e, i=y_sb[tp2], g=cc, hf=half:
                  e.dma_start(out=out_t[g * 128:(g + 1) * 128,
                                        hf * 512:hf * 512 + 512], in_=i[:]),
                  reads=[y_b[tp2]], writes=[buf(f"outw{tp2}")],
                  tag=f"dma_out{cc}_{half}")
        each(snout)

    NG = NTOK // 256
    for grp in range(NG):
        if grp == 0:
            stageP1a(0)
            stageP1b(0)
        stageP2(grp)
        if grp + 1 < NG:
            stageP1a(grp + 1)
            stageP1b(grp + 1)
        stageP3(grp)

    fin_reads = [buf(f"outw{i}") for i in range(4)]
    fin_reads += [B[k] for k in B if k.startswith("dbg_")]
    p.emit("sync", lambda e: None, reads=fin_reads, tag="final")

    p.build(nc, verbose=verbose)
    stack.close()
    return nc


_CACHE = {}


def _get_program(debug=()):
    key = tuple(sorted(debug))
    if key not in _CACHE:
        _CACHE[key] = build_program(debug=key)
    return _CACHE[key]


def kernel(L0_attn_x, L0_attn_Wq, L0_attn_Wk, L0_attn_Wv, L0_attn_Wo,
           L0_attn_Ww, L0_attn_bw, L0_attn_We, L0_attn_be,
           L0_ffn_W_fc, L0_ffn_W_proj, _debug=(), _return_raw=False):
    x = np.ascontiguousarray(np.asarray(L0_attn_x, dtype=np.float32))
    com = {
        "wqt": np.ascontiguousarray(np.asarray(L0_attn_Wq, np.float32).T),
        "wkt": np.ascontiguousarray(np.asarray(L0_attn_Wk, np.float32).T),
        "wvt": np.ascontiguousarray(np.asarray(L0_attn_Wv, np.float32).T),
        "wot": np.ascontiguousarray(np.asarray(L0_attn_Wo, np.float32).T),
        "wwt": np.ascontiguousarray(np.asarray(L0_attn_Ww, np.float32).T),
        "wet": np.ascontiguousarray(np.asarray(L0_attn_We, np.float32).T),
        "bwc": np.ascontiguousarray(np.asarray(L0_attn_bw, np.float32)
                                    .reshape(H, 1)),
        "bec": np.ascontiguousarray(np.asarray(L0_attn_be, np.float32)
                                    .reshape(H, 1)),
        "fct": np.ascontiguousarray(np.asarray(L0_ffn_W_fc, np.float32).T),
        "pjt": np.ascontiguousarray(np.asarray(L0_ffn_W_proj, np.float32).T),
        "i128bf": np.eye(128, dtype=BF),
        "i128f": np.eye(128, dtype=np.float32),
        "i16f": np.eye(16, dtype=np.float32),
        "mpre": np.triu(np.full((128, 128), -1e30, np.float32), 1),
        "ones16": np.ones((16, 128), np.float32),
        "galg": np.tile(np.array([[np.float32(0.95) + np.float32(1e-8),
                                   1e-8]], np.float32), (16, 1)),
    }
    nc = _get_program(_debug)
    in_maps = []
    for k in range(N_CORES):
        m = dict(com)
        m["xin"] = np.ascontiguousarray(
            x[SEQS * k:SEQS * (k + 1)].reshape(NTOK, D))
        in_maps.append(m)
    res = run_bass_kernel_spmd(nc, in_maps, core_ids=list(range(N_CORES)))
    if _return_raw:
        return res
    out = np.concatenate(
        [r["out"].reshape(SEQS, L, D) for r in res.results], axis=0)
    return out


# version 35
# speedup vs baseline: 1.1445x; 1.0519x over previous
import contextlib
import numpy as np
import ml_dtypes
import concourse.bass as bass
import concourse.mybir as mybir
from concourse.bass_utils import run_bass_kernel_spmd

F32 = mybir.dt.float32
BF16 = mybir.dt.bfloat16
AF = mybir.ActivationFunctionType
ALU = mybir.AluOpType
BF = ml_dtypes.bfloat16

N_CORES = 8
SEQS = 2
L = 2048
D = 1024
H = 16
DH = 64
NCK = L // 128
NTOK = SEQS * L
DF = 4096

DEBUG = ()

ENGINE_NAMES = ("tensor", "vector", "scalar", "gpsimd", "sync")


class Buf:
    __slots__ = ("name", "group", "last_writer", "readers", "semkey")

    def __init__(self, name, group=None):
        self.name = name
        self.group = group
        self.last_writer = None
        self.readers = []
        self.semkey = None


class Node:
    __slots__ = ("idx", "eng", "key", "fn", "deps", "is_dma", "needs_inc",
                 "inc_value", "vclock", "tag")

    def __init__(self, idx, eng, fn, tag):
        self.idx = idx
        self.eng = eng
        self.key = eng
        self.fn = fn
        self.deps = set()
        self.is_dma = False
        self.needs_inc = False
        self.inc_value = None
        self.vclock = None
        self.tag = tag


class Prog:
    def __init__(self):
        self.nodes = []
        self.per_engine = {e: [] for e in ENGINE_NAMES}
        self.group_keys = set()
        self.last_by_key = {}
        self._pending = {e: [] for e in ENGINE_NAMES}
        self.n_waits = 0
        self.n_incs = 0

    def buf(self, name, group=None):
        return Buf(name, group)

    def emit(self, eng, fn, reads=(), writes=(), extra_deps=(), tag=""):
        n = Node(len(self.nodes), eng, fn, tag)
        if self._pending[eng]:
            for d in self._pending[eng]:
                if d is not None and d is not n:
                    n.deps.add(d)
            self._pending[eng] = []
        for b in reads:
            if b.last_writer is not None:
                n.deps.add(b.last_writer)
            b.readers.append(n)
        for b in writes:
            for r in b.readers:
                if r is not n:
                    n.deps.add(r)
            if b.last_writer is not None and b.last_writer is not n:
                n.deps.add(b.last_writer)
            b.last_writer = n
            b.readers = [n] if n in b.readers else []
        for d in extra_deps:
            if d is not None and d is not n:
                n.deps.add(d)
        self.nodes.append(n)
        self.per_engine[eng].append(n)
        self.last_by_key[n.key] = n
        return n

    def dma(self, eng, fn, reads=(), writes=(), extra_deps=(), tag="dma"):
        n = self.emit(eng, fn, reads, writes, extra_deps, tag)
        n.is_dma = True
        b = writes[0]
        if b.semkey is None:
            b.semkey = ("g:" + b.group) if b.group else ("b:" + b.name)
            if b.group:
                self.group_keys.add(b.semkey)
        n.key = b.semkey
        self.last_by_key[n.key] = n
        return n

    def barrier(self):
        deps = list(self.last_by_key.values())
        for e in ENGINE_NAMES:
            self._pending[e] = list(deps)

    def build(self, nc, verbose=False):
        for n in self.nodes:
            for d in n.deps:
                if d.key != n.key or d.is_dma:
                    d.needs_inc = True
                elif d.eng == n.eng and n.eng != "tensor":
                    d.needs_inc = True
        for n in self.nodes:
            if n.is_dma:
                n.needs_inc = True
        sem_total = {}
        for n in self.nodes:
            if n.needs_inc:
                step = 16 if n.is_dma else 1
                sem_total[n.key] = sem_total.get(n.key, 0) + step
                n.inc_value = sem_total[n.key]
        last_dma_idx = {}
        for n in self.nodes:
            if n.is_dma and n.key in self.group_keys:
                last_dma_idx[n.key] = n.idx
        clocks = {e: {} for e in ENGINE_NAMES}
        waits = {}
        predrain = set()
        drained_upto = {e: 0 for e in ENGINE_NAMES}
        for n in self.nodes:
            ww = {}
            myclock = clocks[n.eng]
            for d in n.deps:
                if (n.is_dma and d.is_dma and d.key == n.key
                        and d.key in self.group_keys):
                    continue
                if d.eng == n.eng and not d.is_dma and n.eng == "tensor":
                    continue
                v = d.inc_value
                if d.key in self.group_keys:
                    v = sem_total[d.key]
                    assert last_dma_idx[d.key] < n.idx, \
                        f"group {d.key} DMA after waiter {n.tag}"
                if myclock.get(d.key, 0) >= v:
                    continue
                if ww.get(d.key, 0) < v:
                    ww[d.key] = v
            if ww:
                for pk, v in ww.items():
                    myclock[pk] = max(myclock.get(pk, 0), v)
                for d in n.deps:
                    if d.vclock and ww.get(d.key, 0) >= (d.inc_value or 0):
                        for k, kv in d.vclock.items():
                            if myclock.get(k, 0) < kv:
                                myclock[k] = kv
                waits[n] = sorted(ww.items())
                self.n_waits += len(ww)
            if n.needs_inc:
                vc = dict(clocks[n.eng])
                vc[n.key] = n.inc_value
                n.vclock = vc
                self.n_incs += 1
        self._wmap = waits
        self._predrain = predrain

        used_keys = sorted(set(n.key for n in self.nodes if n.needs_inc))
        with contextlib.ExitStack() as stack:
            sem_handles = {k: stack.enter_context(nc.semaphore(f"s_{k}"))
                           for k in used_keys}
            block = stack.enter_context(nc.Block())
            prog = self

            def mk_runner(ename):
                def run(engobj):
                    for n in prog.per_engine[ename]:
                        if n in prog._predrain:
                            engobj.drain()
                        w = prog._wmap.get(n)
                        if w:
                            for pk, v in w:
                                engobj.wait_ge(sem_handles[pk], v)
                        inst = n.fn(engobj)
                        if n.needs_inc:
                            inst.then_inc(sem_handles[n.key],
                                          16 if n.is_dma else 1)
                return run

            for e in ENGINE_NAMES:
                if self.per_engine[e]:
                    getattr(block, e)(mk_runner(e))
        if verbose:
            print(f"[fw] nodes={len(self.nodes)} incs={self.n_incs} "
                  f"waits={self.n_waits} drains={len(predrain)} "
                  f"sems={len(used_keys)}")
        return sem_total


def build_program(debug=(), verbose=False):
    nc = bass.Bass("TRN2", target_bir_lowering=False, debug=False)
    p = Prog()

    xin = nc.dram_tensor("xin", [NTOK, D], F32, kind="ExternalInput").ap()
    wqt = nc.dram_tensor("wqt", [D, D], F32, kind="ExternalInput").ap()
    wkt = nc.dram_tensor("wkt", [D, D], F32, kind="ExternalInput").ap()
    wvt = nc.dram_tensor("wvt", [D, D], F32, kind="ExternalInput").ap()
    wot = nc.dram_tensor("wot", [D, D], F32, kind="ExternalInput").ap()
    wwt = nc.dram_tensor("wwt", [D, H], F32, kind="ExternalInput").ap()
    wet = nc.dram_tensor("wet", [D, H], F32, kind="ExternalInput").ap()
    bwc = nc.dram_tensor("bwc", [H, 1], F32, kind="ExternalInput").ap()
    bec = nc.dram_tensor("bec", [H, 1], F32, kind="ExternalInput").ap()
    fct = nc.dram_tensor("fct", [D, DF], F32, kind="ExternalInput").ap()
    pjt = nc.dram_tensor("pjt", [DF, D], F32, kind="ExternalInput").ap()
    i128bf = nc.dram_tensor("i128bf", [128, 128], BF16, kind="ExternalInput").ap()
    i128f = nc.dram_tensor("i128f", [128, 128], F32, kind="ExternalInput").ap()
    i16f = nc.dram_tensor("i16f", [16, 16], F32, kind="ExternalInput").ap()
    mpre = nc.dram_tensor("mpre", [128, 128], F32, kind="ExternalInput").ap()
    ones16 = nc.dram_tensor("ones16", [16, 128], F32, kind="ExternalInput").ap()
    galg = nc.dram_tensor("galg", [16, 2], F32, kind="ExternalInput").ap()
    out_t = nc.dram_tensor("out", [NTOK, D], F32, kind="ExternalOutput").ap()
    x1scr = nc.dram_tensor("x1scr", [NTOK, D], BF16, kind="Internal").ap()

    dbg = {}
    for name, shape, dt in [("xn", [NTOK, D], BF16), ("qt", [NTOK, D], BF16),
                            ("kt", [NTOK, D], BF16), ("vaug", [NTOK, 1040], BF16),
                            ("cc", [SEQS * NCK * 16, 128], F32),
                            ("bb", [SEQS * NCK * 16, 128], F32),
                            ("attn", [NTOK, D], BF16), ("x1", [NTOK, D], F32)]:
        if name in debug:
            dbg[name] = nc.dram_tensor("dbg_" + name, shape, dt,
                                       kind="ExternalOutput").ap()

    stack = contextlib.ExitStack()
    sb = lambda name, shape, dt: stack.enter_context(nc.sbuf_tensor(name, shape, dt))
    psalloc = lambda name: stack.enter_context(nc.psum_tensor(name, [128, 512], F32))

    c_i128bf = sb("c_i128bf", [128, 128], BF16)
    c_i128f = sb("c_i128f", [128, 128], F32)
    c_i16f = sb("c_i16f", [16, 16], F32)
    c_mpre = sb("c_mpre", [128, 128], F32)
    c_ones16 = sb("c_ones16", [16, 128], F32)
    c_galg = sb("c_galg", [16, 2], F32)
    c_bwc = sb("c_bwc", [16, 1], F32)
    c_nbw = sb("c_nbw", [16, 1], F32)
    c_bec = sb("c_bec", [16, 1], F32)
    pb = [psalloc(f"pb{i}") for i in range(8)]

    B = {}

    def buf(name, group=None):
        if name not in B:
            B[name] = p.buf(name, group)
        return B[name]

    pbb = [buf(f"pb{i}") for i in range(8)]

    c_m1 = sb("c_m1", [128, 1], F32)
    p.emit("vector", lambda e: e.memset(c_m1[:], -1.0),
           writes=[buf("m1w")], tag="m1set")
    m1b = buf("m1w")

    ln_cols = [sb(f"ln_cols{i}", [128, 8], F32) for i in range(2)]
    ln_cols_b = [buf("ln_cols0"), buf("ln_cols1")]
    l2inv = sb("l2inv", [128, 32], F32)
    l2nmu = sb("l2nmu", [128, 32], F32)
    l2invb = buf("l2inv")
    l2nmub = buf("l2nmu")

    def emit_ln(idx, x_ap, xbuf, out_ap, outbuf, tagp):
        par = idx % 2
        cl, clb = ln_cols[par], ln_cols_b[par]
        p.emit("scalar", lambda e, o=out_ap, i=x_ap, a=cl[:, 1:2]:
               e.activation(o, i, AF.Square, accum_out=a),
               reads=[xbuf], writes=[outbuf, clb], tag=f"{tagp}sq")
        p.emit("vector", lambda e, o=cl[:, 0:1], i=x_ap:
               e.reduce_sum(o, i, axis=mybir.AxisListType.X),
               reads=[xbuf], writes=[clb], tag=f"{tagp}sum")
        p.emit("vector", lambda e, o=cl[:, 2:3], i=cl[:, 0:1]:
               e.tensor_single_scalar(o, i, 1.0 / D, ALU.mult),
               reads=[clb], writes=[clb], tag=f"{tagp}mean")
        p.emit("vector", lambda e, o=cl[:, 3:4], i=cl[:, 2:3]:
               e.tensor_tensor(o, i, i, ALU.mult),
               reads=[clb], writes=[clb], tag=f"{tagp}msq")
        p.emit("vector", lambda e, o=cl[:, 4:5], i=cl[:, 1:2], m=cl[:, 3:4]:
               e.scalar_tensor_tensor(o, i, 1.0 / D, m, ALU.mult, ALU.subtract),
               reads=[clb], writes=[clb], tag=f"{tagp}var")
        p.emit("vector", lambda e, o=cl[:, 4:5], i=cl[:, 4:5]:
               e.tensor_single_scalar(o, i, 1e-5, ALU.add),
               reads=[clb], writes=[clb], tag=f"{tagp}vev")
        p.emit("scalar", lambda e, o=cl[:, 4:5], i=cl[:, 4:5]:
               e.activation(o, i, AF.Ln),
               reads=[clb], writes=[clb], tag=f"{tagp}ln")
        p.emit("vector", lambda e, o=cl[:, 4:5], i=cl[:, 4:5]:
               e.tensor_single_scalar(o, i, -0.5, ALU.mult),
               reads=[clb], writes=[clb], tag=f"{tagp}nh")
        p.emit("scalar", lambda e, o=cl[:, 4:5], i=cl[:, 4:5]:
               e.activation(o, i, AF.Exp),
               reads=[clb], writes=[clb], tag=f"{tagp}ex")
        p.emit("vector", lambda e, o=cl[:, 5:6], i=cl[:, 2:3], v=cl[:, 4:5]:
               e.tensor_scalar(o, i, v, -1.0, ALU.mult, ALU.mult),
               reads=[clb], writes=[clb], tag=f"{tagp}nmu")
        p.emit("scalar", lambda e, o=out_ap, i=x_ap, v=cl[:, 4:5],
               b2=cl[:, 5:6]:
               e.activation(o, i, AF.Identity, scale=v, bias=b2),
               reads=[xbuf, clb], writes=[outbuf], tag=f"{tagp}app")

    for cap, src in [(c_i128bf, i128bf), (c_i128f, i128f), (c_i16f, i16f),
                     (c_mpre, mpre), (c_ones16, ones16), (c_bwc, bwc),
                     (c_bec, bec), (c_galg, galg)]:
        p.dma("sync", lambda e, o=cap, s=src: e.dma_start(out=o[:], in_=s),
              writes=[buf("consts", group="consts")])
    cb = buf("consts")
    p.emit("vector", lambda e: e.tensor_scalar(c_nbw[:], c_bwc[:], -1.0, None,
                                               ALU.mult),
           reads=[cb], writes=[buf("nbwd")], tag="negbw")
    nbwb = buf("nbwd")

    with contextlib.ExitStack() as sA:
        sbA = lambda name, shape, dt: sA.enter_context(nc.sbuf_tensor(name, shape, dt))
        stg = [sbA(f"stg{i}", [128, D], F32) for i in range(4)]
        l2sm = sbA("l2sm", [128, 32], F32)
        l2ss = sbA("l2ss", [128, 32], F32)
        l2tmp = sbA("l2tmp", [128, 32], F32)
        l2smb, l2ssb, l2tmpb = buf("l2sm"), buf("l2ss"), buf("l2tmp")
        stgb = [buf(f"stg{i}") for i in range(4)]
        wq_sb = sbA("wq_sb", [128, 8 * D], BF16)
        wk_sb = sbA("wk_sb", [128, 8 * D], BF16)
        wv_sb = sbA("wv_sb", [128, 8 * D], BF16)
        wo_sb = sbA("wo_sb", [128, 8 * D], BF16)
        ww_sb = sbA("ww_sb", [128, 8 * H], BF16)
        we_sb = sbA("we_sb", [128, 8 * H], BF16)
        xn_bf = [sbA(f"xn_bf{i}", [128, D], BF16) for i in range(2)]
        xnT = [sbA(f"xnT{i}", [128, D], BF16) for i in range(2)]
        qt_c = [sbA(f"qt_c{i}", [128, D], BF16) for i in range(2)]
        kt_c = [sbA(f"kt_c{i}", [128, D], BF16) for i in range(2)]
        va_c = [sbA(f"va_c{i}", [128, H * 65], BF16) for i in range(2)]
        s_f32 = sbA("s_f32", [128, 8 * 65], F32)
        s_bf = sbA("s_bf", [128, 8 * 65], BF16)
        g_u = sbA("g_u", [16, 128], F32)
        g_lw1 = sbA("g_lw1", [16, 128], F32)
        g_t = sbA("g_t", [16, 128], F32)
        g_v = sbA("g_v", [16, 128], F32)
        g_t1 = sbA("g_t1", [16, 128], F32)
        g_logg = sbA("g_logg", [16, 128], F32)
        g_c = [sbA(f"g_c{i}", [16, 128], F32) for i in range(2)]
        g_b = sbA("g_b", [16, 128], F32)
        carry = sbA("carry", [16, 1], F32)
        cprev = sbA("cprev", [16, 1], F32)
        dcol = sbA("dcol", [16, 1], F32)
        asrc = sbA("asrc", [16, 128], F32)
        ct_c = [sbA(f"ct_c{i}", [128, 16], F32) for i in range(2)]
        bt_c = [sbA(f"bt_c{i}", [128, 16], F32) for i in range(2)]
        qd_c = [sbA(f"qd_c{i}", [128, 16], F32) for i in range(2)]
        kd_c = [sbA(f"kd_c{i}", [128, 16], F32) for i in range(2)]
        lam_c = [sbA(f"lam_c{i}", [128, 16], F32) for i in range(2)]
        qdt = sbA("qdt", [128, 16], F32)
        kdt = sbA("kdt", [128, 16], F32)
        scr_e = [sbA(f"scr_e{i}", [128, 512], F32) for i in range(2)]
        scr_m = [sbA(f"scr_m{i}", [128, 512], F32) for i in range(2)]
        tmpM = [sbA(f"tmpM{i}", [128, 128], F32) for i in range(8)]
        dec = [sbA(f"dec{i}", [128, 128], F32) for i in range(8)]
        effT = [sbA(f"effT{i}", [128, 128], BF16) for i in range(8)]
        kdb = [sbA(f"kdb{i}", [128, DH], BF16) for i in range(8)]
        r1s = [sbA(f"r1s{i}", [128, 65], F32) for i in range(8)]
        comb = [sbA(f"comb{i}", [128, 65], F32) for i in range(8)]
        recc = [sbA(f"recc{i}", [128, 1], F32) for i in range(8)]
        atm = [sbA(f"atm{i}", [128, D], BF16) for i in range(2)]
        att = [sbA(f"att{i}", [128, D], BF16) for i in range(2)]
        x1s = [sbA(f"x1s{i}", [128, D], F32) for i in range(2)]

        def wload(dst, src, cdim, fdim, name):
            inap = src.rearrange("(c p) f -> p c f", p=128)
            outap = dst[:].rearrange("p (c f) -> p c f", f=fdim)
            p.dma("gpsimd", lambda e, o=outap, i=inap: e.dma_start(out=o, in_=i),
                  writes=[buf(name)], tag="dma_w_" + name)
            return buf(name)

        wqb = wload(wq_sb, wqt, 8, D, "wq")
        wkb = wload(wk_sb, wkt, 8, D, "wk")
        wvb = wload(wv_sb, wvt, 8, D, "wv")
        wob = wload(wo_sb, wot, 8, D, "wo")
        wwb = wload(ww_sb, wwt, 8, H, "ww")
        web = wload(we_sb, wet, 8, H, "we")

        sbuf_bufs = {}

        def tb(obj, name):
            key = name
            if key not in sbuf_bufs:
                sbuf_bufs[key] = buf(key)
            return sbuf_bufs[key]

        stg_b = stgb
        xnbf_b = [buf("xnbf0"), buf("xnbf1")]
        xnT_b = [buf("xnT0"), buf("xnT1")]
        qt_b = [buf("qt0"), buf("qt1")]
        kt_b = [buf("kt0"), buf("kt1")]
        va_b = [buf("va0"), buf("va1")]
        sf_b = [buf(f"sf{h}") for h in range(H)]
        sbf_b = [buf(f"sbf{h}") for h in range(H)]
        gate_b = {nm: buf("g_" + nm) for nm in
                  ["u", "lw1", "t", "v", "t1", "logg", "b", "carry", "cprev",
                   "dcol", "asrc", "qdt", "kdt"]}
        gc_b = [buf("gc0"), buf("gc1")]
        ct_b = [buf("ct0"), buf("ct1")]
        bt_b = [buf("bt0"), buf("bt1")]
        qd_b = [buf("qd0"), buf("qd1")]
        kd_b = [buf("kd0"), buf("kd1")]
        lam_b = [buf("lam0"), buf("lam1")]
        scre_b = [buf("scre0"), buf("scre1")]
        scrm_b = [buf("scrm0"), buf("scrm1")]
        tmpM_b = [buf(f"tmpM{i}") for i in range(8)]
        dec_b = [buf(f"dec{i}") for i in range(8)]
        effT_b = [buf(f"effT{i}") for i in range(8)]
        kdb_b = [buf(f"kdb{i}") for i in range(8)]
        r1_b = [buf(f"r1{i}") for i in range(8)]
        comb_b = [buf(f"comb{i}") for i in range(8)]
        recc_b = [buf(f"recc{i}") for i in range(8)]
        atm_b = [buf("atm0"), buf("atm1")]
        att_b = [buf("att0"), buf("att1")]
        x1s_b = [buf("x1s0"), buf("x1s1")]

        def _ahead(gc):
            return (gc // NCK, gc % NCK, gc % 2, stg[gc % 4], stg_b[gc % 4])

        def stageA1a(gc):
            s, c, r, st, stb = _ahead(gc)
            p.dma("sync", lambda e, o=st, g=gc: e.dma_start(
                out=o[:], in_=xin[g * 128:(g + 1) * 128, :]),
                writes=[stb], tag=f"dma_x{gc}")
            emit_ln(gc, st[:], stb, xn_bf[r][:], xnbf_b[r], f"ln{gc}_")
            if "xn" in dbg:
                p.dma("sync", lambda e, i=xn_bf[r], g=gc: e.dma_start(
                    out=dbg["xn"][g * 128:(g + 1) * 128, :], in_=i[:]),
                    reads=[xnbf_b[r]], writes=[buf(f"dbg_xn_{r}")])
        def stageA1b(gc):
            s, c, r, st, stb = _ahead(gc)
            for d in range(8):
                tp = pb[2 + (d % 2)][:, 0:64].bitcast(BF16)
                tpb = pbb[2 + (d % 2)]
                p.emit("tensor", lambda e, o=tp, i=xn_bf[r][:, d * 128:(d + 1) * 128]:
                       e.transpose(o, i, c_i128bf[:]),
                       reads=[xnbf_b[r], cb], writes=[tpb], tag=f"trx{gc}_{d}")
                p.emit("vector", lambda e, o=xnT[r][:, d * 128:(d + 1) * 128], i=tp:
                       e.tensor_copy(o, i),
                       reads=[tpb], writes=[xnT_b[r]], tag=f"trxe{gc}_{d}")

        def stageA2(gc):
            s, c, r, st, stb = _ahead(gc)
            for (wsb_, wb_, bank, bankb, nm) in [
                    (ww_sb, wwb, pb[4], pbb[4], "zw"),
                    (we_sb, web, pb[5], pbb[5], "ze")]:
                for d in range(8):
                    p.emit("tensor", lambda e, o=bank[0:16, 0:128],
                           l=wsb_[:, d * H:(d + 1) * H],
                           rr=xnT[r][:, d * 128:(d + 1) * 128],
                           s0=(d == 0), s1=(d == 7):
                           e.matmul(o, l, rr, start=s0, stop=s1),
                           reads=[wb_, xnT_b[r]], writes=[bankb],
                           tag=f"mm{nm}{gc}_{d}")

            for (wsb, wb_, dst, dstb, nm) in [
                    (wq_sb, wqb, qt_c[r], qt_b[r], "q"),
                    (wk_sb, wkb, kt_c[r], kt_b[r], "k")]:
                for quad in range(2):
                    bank = pb[quad]
                    bankb = pbb[quad]
                    for fq in range(4):
                        fc = quad * 4 + fq
                        for d in range(8):
                            p.emit("tensor", lambda e,
                                   o=bank[:, fq * 128:(fq + 1) * 128],
                                   l=wsb[:, d * D + fc * 128: d * D + (fc + 1) * 128],
                                   rr=xnT[r][:, d * 128:(d + 1) * 128],
                                   s0=(d == 0), s1=(d == 7):
                                   e.matmul(o, l, rr, start=s0, stop=s1),
                                   reads=[wb_, xnT_b[r]], writes=[bankb],
                                   tag=f"mm{nm}{gc}_{fc}_{d}")
                    p.emit("scalar", lambda e, o=scr_e[quad], i=bank:
                           e.activation(o[:], i[:], AF.Exp),
                           reads=[bankb], writes=[scre_b[quad]],
                           tag=f"phiexp{nm}{gc}_{quad}")
                    p.emit("scalar", lambda e, o=scr_m[quad], i=bank:
                           e.activation(o[:], i[:], AF.Relu),
                           reads=[bankb], writes=[scrm_b[quad]],
                           tag=f"phirelu{nm}{gc}_{quad}")
                    p.emit("vector", lambda e, o=scr_e[quad], i=scr_e[quad]:
                           e.tensor_single_scalar(o[:], i[:], 1.0, ALU.min),
                           reads=[scre_b[quad]], writes=[scre_b[quad]],
                           tag=f"phimin{nm}{gc}_{quad}")
                    p.emit("gpsimd", lambda e,
                           o=dst[:, quad * 512:(quad + 1) * 512],
                           a=scr_m[quad], bb2=scr_e[quad]:
                           e.tensor_add(o, a[:], bb2[:]),
                           reads=[scrm_b[quad], scre_b[quad]], writes=[dstb],
                           tag=f"phiout{nm}{gc}_{quad}")
            if "qt" in dbg:
                p.dma("sync", lambda e, i=qt_c[r], g=gc: e.dma_start(
                    out=dbg["qt"][g * 128:(g + 1) * 128, :], in_=i[:]),
                    reads=[qt_b[r]], writes=[buf(f"dbg_qt_{r}")])
            if "kt" in dbg:
                p.dma("sync", lambda e, i=kt_c[r], g=gc: e.dma_start(
                    out=dbg["kt"][g * 128:(g + 1) * 128, :], in_=i[:]),
                    reads=[kt_b[r]], writes=[buf(f"dbg_kt_{r}")])

            for half in range(2):
                bank = pb[2 + half]
                bankb = pbb[2 + half]
                for d in range(8):
                    p.emit("tensor", lambda e, o=bank[:, 0:512],
                           l=xnT[r][:, d * 128:(d + 1) * 128],
                           rr=wv_sb[:, d * D + half * 512: d * D + half * 512 + 512],
                           s0=(d == 0), s1=(d == 7):
                           e.matmul(o, l, rr, start=s0, stop=s1),
                           reads=[wvb, xnT_b[r]], writes=[bankb],
                           tag=f"mmv{gc}_{half}_{d}")
                oap = va_c[r][:, half * 8 * 65:(half * 8 + 8) * 65] \
                    .rearrange("p (h v) -> p h v", v=65)[:, :, 0:64]
                p.emit("vector", lambda e, o=oap, i=bank[:, 0:512]:
                       e.tensor_copy(o, i),
                       reads=[bankb], writes=[va_b[r]], tag=f"vev{gc}_{half}")
            msap = va_c[r][:].rearrange("p (h v) -> p h v", v=65)[:, :, 64:65]
            p.emit("vector", lambda e, o=msap: e.memset(o, 1.0),
                   writes=[va_b[r]], tag=f"vones{gc}")
            if "vaug" in dbg:
                p.dma("sync", lambda e, i=va_c[r], g=gc: e.dma_start(
                    out=dbg["vaug"][g * 128:(g + 1) * 128, :], in_=i[:]),
                    reads=[va_b[r]], writes=[buf(f"dbg_va_{r}")])


        def stageA3(gc):
            s, c, r, st, stb = _ahead(gc)
            GB = gate_b
            p.emit("scalar", lambda e, o=g_u, i=pb[4][0:16, 0:128]:
                   e.activation(o[:], i, AF.Exp, bias=c_nbw[:], scale=-1.0),
                   reads=[pbb[4], nbwb], writes=[GB["u"]], tag=f"gu{gc}")
            p.emit("scalar", lambda e, o=g_lw1, i=g_u:
                   e.activation(o[:], i[:], AF.Ln, bias=1.0),
                   reads=[GB["u"]], writes=[GB["lw1"]], tag=f"glw{gc}")
            p.emit("vector", lambda e, o=g_t, i=pb[5][0:16, 0:128]:
                   e.tensor_scalar(o[:], i, c_bec[:], 80.0, ALU.add, ALU.min),
                   reads=[pbb[5], cb], writes=[GB["t"]], tag=f"gt{gc}")
            p.emit("scalar", lambda e, o=g_v, i=g_t:
                   e.activation(o[:], i[:], AF.Exp),
                   reads=[GB["t"]], writes=[GB["v"]], tag=f"gv{gc}")
            p.emit("scalar", lambda e, o=g_t1, i=g_v:
                   e.activation(o[:], i[:], AF.Ln, bias=1.0),
                   reads=[GB["v"]], writes=[GB["t1"]], tag=f"gt1{gc}")
            p.emit("scalar", lambda e, o=g_v, i=g_v:
                   e.activation(o[:], i[:], AF.Ln, bias=c_galg[:, 0:1],
                                scale=c_galg[:, 1:2]),
                   reads=[GB["v"], cb], writes=[GB["v"]], tag=f"gt2{gc}")
            p.emit("vector", lambda e, o=g_logg, a=g_v, b=g_t1:
                   e.tensor_sub(o[:], a[:], b[:]),
                   reads=[GB["v"], GB["t1"]], writes=[GB["logg"]],
                   tag=f"glogg{gc}")
            if c == 0:
                p.emit("vector", lambda e, o=cprev: e.memset(o[:], 0.0),
                       writes=[GB["cprev"]], tag=f"gcp{gc}")
            else:
                p.emit("vector", lambda e, o=cprev, i=carry:
                       e.tensor_copy(o[:], i[:]),
                       reads=[GB["carry"]], writes=[GB["cprev"]],
                       tag=f"gcp{gc}")
            p.emit("vector", lambda e, o=g_c[r], i=g_logg, ini=cprev:
                   e.tensor_tensor_scan(o[:], i[:], i[:], ini[:],
                                        ALU.add, ALU.bypass),
                   reads=[GB["logg"], GB["cprev"]], writes=[gc_b[r]],
                   tag=f"gscan{gc}")
            if "cc" in dbg:
                p.dma("sync", lambda e, i=g_c[r], g=gc: e.dma_start(
                    out=dbg["cc"][g * 16:(g + 1) * 16, :], in_=i[:]),
                    reads=[gc_b[r]], writes=[buf(f"dbg_cc_{r}")])
            p.emit("vector", lambda e, o=carry, i=g_c[r][:, 127:128]:
                   e.tensor_copy(o[:], i),
                   reads=[gc_b[r]], writes=[GB["carry"]], tag=f"gca{gc}")
            p.emit("vector", lambda e, o=g_b, a=g_lw1, bb_=g_c[r]:
                   e.scalar_tensor_tensor(o[:], a[:], -1.0, bb_[:],
                                          ALU.mult, ALU.subtract),
                   reads=[GB["lw1"], gc_b[r]], writes=[GB["b"]],
                   tag=f"gb{gc}")
            if "bb" in dbg:
                p.dma("sync", lambda e, i=g_b, g=gc: e.dma_start(
                    out=dbg["bb"][g * 16:(g + 1) * 16, :], in_=i[:]),
                    reads=[GB["b"]], writes=[buf(f"dbg_bb_{gc % 2}")])
            p.emit("tensor", lambda e, o=pb[4][:, 0:16], i=g_c[r]:
                   e.transpose(o, i[:], c_i16f[:]),
                   reads=[gc_b[r], cb], writes=[pbb[4]], tag=f"trc{gc}")
            p.emit("vector", lambda e, o=ct_c[r], i=pb[4][:, 0:16]:
                   e.tensor_copy(o[:], i),
                   reads=[pbb[4]], writes=[ct_b[r]], tag=f"trce{gc}")
            p.emit("tensor", lambda e, o=pb[5][:, 0:16], i=g_b:
                   e.transpose(o, i[:], c_i16f[:]),
                   reads=[GB["b"], cb], writes=[pbb[5]], tag=f"trb{gc}")
            p.emit("vector", lambda e, o=bt_c[r], i=pb[5][:, 0:16]:
                   e.tensor_copy(o[:], i),
                   reads=[pbb[5]], writes=[bt_b[r]], tag=f"trbe{gc}")
            p.emit("vector", lambda e, o=asrc, s1=cprev:
                   e.tensor_scalar(o[:], c_ones16[:], s1[:], None, ALU.mult),
                   reads=[cb, GB["cprev"]], writes=[GB["asrc"]],
                   tag=f"gas{gc}")
            p.emit("tensor", lambda e, o=pb[6][:, 0:16], i=asrc:
                   e.transpose(o, i[:], c_i16f[:]),
                   reads=[GB["asrc"], cb], writes=[pbb[6]], tag=f"trap{gc}")
            p.emit("vector", lambda e, o=qdt, a=ct_c[r], i=pb[6][:, 0:16]:
                   e.tensor_sub(o[:], a[:], i),
                   reads=[ct_b[r], pbb[6]], writes=[GB["qdt"]],
                   tag=f"qdt{gc}")
            p.emit("scalar", lambda e, o=qd_c[r], i=qdt:
                   e.activation(o[:], i[:], AF.Exp),
                   reads=[GB["qdt"]], writes=[qd_b[r]], tag=f"qde{gc}")
            p.emit("vector", lambda e, o=asrc, s1=carry:
                   e.tensor_scalar(o[:], c_ones16[:], s1[:], None, ALU.mult),
                   reads=[cb, GB["carry"]], writes=[GB["asrc"]],
                   tag=f"gas2{gc}")
            p.emit("tensor", lambda e, o=pb[7][:, 0:16], i=asrc:
                   e.transpose(o, i[:], c_i16f[:]),
                   reads=[GB["asrc"], cb], writes=[pbb[7]], tag=f"tran{gc}")
            p.emit("vector", lambda e, o=kdt, a=bt_c[r], i=pb[7][:, 0:16]:
                   e.tensor_add(o[:], a[:], i),
                   reads=[bt_b[r], pbb[7]], writes=[GB["kdt"]],
                   tag=f"kdt{gc}")
            p.emit("scalar", lambda e, o=kd_c[r], i=kdt:
                   e.activation(o[:], i[:], AF.Exp),
                   reads=[GB["kdt"]], writes=[kd_b[r]], tag=f"kde{gc}")
            p.emit("vector", lambda e, o=dcol, a=carry, bb_=cprev:
                   e.tensor_sub(o[:], a[:], bb_[:]),
                   reads=[GB["carry"], GB["cprev"]], writes=[GB["dcol"]],
                   tag=f"dcol{gc}")
            p.emit("vector", lambda e, o=asrc, s1=dcol:
                   e.tensor_scalar(o[:], c_ones16[:], s1[:], None,
                                   ALU.mult),
                   reads=[cb, GB["dcol"]], writes=[GB["asrc"]],
                   tag=f"lsrc{gc}")
            p.emit("tensor", lambda e, o=pb[6][:, 16:32], i=asrc:
                   e.transpose(o, i[:], c_i16f[:]),
                   reads=[GB["asrc"], cb], writes=[pbb[6]], tag=f"trl{gc}")
            p.emit("scalar", lambda e, o=lam_c[r], i=pb[6][:, 16:32]:
                   e.activation(o[:], i, AF.Exp),
                   reads=[pbb[6]], writes=[lam_b[r]], tag=f"lam{gc}")

        def stageA4(gc):
            s, c, r, st, stb = _ahead(gc)
            GSZ = 4
            KNOINT = False
            KNOU = False
            for hg in range(H // GSZ):
                heads = list(range(hg * GSZ, hg * GSZ + GSZ))

                def HH(fn):
                    for hh, h in enumerate(heads):
                        po = 64 * (h % 2)
                        fc = h // 2
                        fn(hh, h, po, fc,
                           kt_c[r][po:po + 64, fc * 128:(fc + 1) * 128],
                           qt_c[r][po:po + 64, fc * 128:(fc + 1) * 128],
                           va_c[r][:, h * 65:(h + 1) * 65])

                def sc_ap(hh):
                    return pb[hh % 2][:, (hh // 2) * 128:(hh // 2) * 128 + 128]

                def crm_ap(hh):
                    return pb[2 + hh % 2][:, (hh // 2) * 128:(hh // 2) * 128 + 128]

                def ktm_ap(hh):
                    c0 = (hh // 2) * 128
                    return pb[2 + hh % 2][:, c0:c0 + 32].bitcast(BF16)

                def po_ap(hh):
                    return pb[4 + hh % 2][:, (hh // 2) * 65:(hh // 2) * 65 + 65]

                def pi_ap(hh):
                    return pb[6 + hh % 2][:, (hh // 2) * 65:(hh // 2) * 65 + 65]


                def pu_ap(hh, po):
                    c0 = 260 + (hh // 2) * 65
                    return pb[6 + hh % 2][po:po + 64, c0:c0 + 65]

                def pub(hh):
                    return pbb[6 + hh % 2]

                def emit_tmpM(hg2):
                    for hh2, h2 in enumerate(range(hg2 * GSZ,
                                                   hg2 * GSZ + GSZ)):
                        sl2 = hh2 + 4 * (hg2 % 2)
                        p.emit("vector", lambda e, o=tmpM[sl2],
                               s1=ct_c[r][:, h2:h2 + 1]:
                               e.tensor_scalar(o[:], c_mpre[:], s1, None,
                                               ALU.add),
                               reads=[cb, ct_b[r]], writes=[tmpM_b[sl2]],
                               tag=f"tmpM{gc}_{h2}")

                if hg == 0:
                    emit_tmpM(0)
                HH(lambda hh, h, po, fc, ktsl, qtsl, vasl:
                   p.emit("tensor", lambda e, o=sc_ap(hh), l=ktsl, rr=qtsl:
                          e.matmul(o, l, rr, start=True, stop=True),
                          reads=[kt_b[r], qt_b[r]], writes=[pbb[hh % 2]],
                          tag=f"mmsc{gc}_{h}"))
                HH(lambda hh, h, po, fc, ktsl, qtsl, vasl:
                   p.emit("tensor", lambda e, o=ktm_ap(hh), i=ktsl,
                          idn=c_i128bf[po:po + 64, po:po + 64]:
                          e.transpose(o, i, idn),
                          reads=[kt_b[r], cb, dec_b[hh]],
                          writes=[pbb[2 + hh % 2]], tag=f"trktm{gc}_{h}"))
                HH(lambda hh, h, po, fc, ktsl, qtsl, vasl:
                   p.emit("vector", lambda e, o=kdb[hh], i=ktm_ap(hh),
                          sc=kd_c[r][:, h:h + 1]:
                          e.tensor_scalar(o[:], i, sc, None, ALU.mult),
                          reads=[pbb[2 + hh % 2], kd_b[r]], writes=[kdb_b[hh]],
                          tag=f"kdscale{gc}_{h}"))
                if c > 0 and not KNOINT:
                    HH(lambda hh, h, po, fc, ktsl, qtsl, vasl:
                       p.emit("tensor", lambda e, o=pi_ap(hh), l=qtsl,
                              rr=s_bf[po:po + 64, fc * 65:(fc + 1) * 65]:
                              e.matmul(o, l, rr, start=True, stop=True),
                              reads=[qt_b[r], sbf_b[h]],
                              writes=[pbb[6 + hh // 4]],
                              tag=f"mmint{gc}_{h}"))
                KNOU or HH(lambda hh, h, po, fc, ktsl, qtsl, vasl:
                   p.emit("tensor", lambda e, o=pu_ap(hh, po), l=kdb[hh],
                          rr=vasl: e.matmul(o, l[:], rr, start=True, stop=True),
                          reads=[kdb_b[hh], va_b[r]], writes=[pbb[6 + hh // 4]],
                          tag=f"mmu{gc}_{h}"))
                if c > 0 and not KNOINT:
                    HH(lambda hh, h, po, fc, ktsl, qtsl, vasl:
                       p.emit("vector", lambda e, o=r1s[hh], i=pi_ap(hh),
                              sc=qd_c[r][:, h:h + 1]:
                              e.tensor_scalar(o[:], i, sc, None, ALU.mult),
                              reads=[pbb[6 + hh // 4], qd_b[r]],
                              writes=[r1_b[hh]], tag=f"r1{gc}_{h}"))
                if hg + 1 < H // GSZ:
                    emit_tmpM(hg + 1)
                HH(lambda hh, h, po, fc, ktsl, qtsl, vasl:
                   p.emit("tensor", lambda e, o=crm_ap(hh),
                          i=tmpM[hh + 4 * (hg % 2)]:
                          e.transpose(o, i[:], c_i128f[:]),
                          reads=[tmpM_b[hh + 4 * (hg % 2)], cb],
                          writes=[pbb[2 + hh % 2]],
                          tag=f"trcrm{gc}_{h}"))
                HH(lambda hh, h, po, fc, ktsl, qtsl, vasl:
                   p.emit("scalar", lambda e, o=dec[hh], i=crm_ap(hh),
                          bi=bt_c[r][:, h:h + 1]:
                          e.activation(o[:], i, AF.Exp, bias=bi),
                          reads=[pbb[2 + hh % 2], bt_b[r]], writes=[dec_b[hh]],
                          tag=f"dec{gc}_{h}"))
                HH(lambda hh, h, po, fc, ktsl, qtsl, vasl:
                   p.emit("vector", lambda e, o=effT[hh], a=sc_ap(hh),
                          bb_=dec[hh]: e.tensor_mul(o[:], a, bb_[:]),
                          reads=[pbb[hh % 2], dec_b[hh]], writes=[effT_b[hh]],
                          tag=f"eff{gc}_{h}"))
                HH(lambda hh, h, po, fc, ktsl, qtsl, vasl:
                   p.emit("tensor", lambda e, o=po_ap(hh), l=effT[hh], rr=vasl:
                          e.matmul(o, l[:], rr, start=True, stop=True),
                          reads=[effT_b[hh], va_b[r]], writes=[pbb[4 + hh % 2]],
                          tag=f"mmdiag{gc}_{h}"))
                if c > 0 and not KNOINT:
                    HH(lambda hh, h, po, fc, ktsl, qtsl, vasl:
                       p.emit("vector", lambda e, o=comb[hh], a=po_ap(hh),
                              bb_=r1s[hh]: e.tensor_add(o[:], a, bb_[:]),
                              reads=[pbb[4 + hh % 2], r1_b[hh]],
                              writes=[comb_b[hh]], tag=f"comb{gc}_{h}"))
                if c == 0 or KNOINT:
                    HH(lambda hh, h, po, fc, ktsl, qtsl, vasl:
                       p.emit("vector", lambda e, o=comb[hh], i=po_ap(hh):
                              e.tensor_copy(o[:], i),
                              reads=[pbb[4 + hh % 2]], writes=[comb_b[hh]],
                              tag=f"comb{gc}_{h}"))
                if KNOU:
                    pass
                elif c == 0:
                    HH(lambda hh, h, po, fc, ktsl, qtsl, vasl:
                       p.emit("vector", lambda e,
                              o=s_f32[po:po + 64, fc * 65:(fc + 1) * 65],
                              i=pu_ap(hh, po): e.tensor_copy(o, i),
                              reads=[pbb[6 + hh // 4]], writes=[sf_b[h]],
                              tag=f"sini{gc}_{h}"))
                else:
                    HH(lambda hh, h, po, fc, ktsl, qtsl, vasl:
                       p.emit("vector", lambda e,
                              o=s_f32[po:po + 64, fc * 65:(fc + 1) * 65],
                              sc=lam_c[r][po:po + 64, h:h + 1]:
                              e.tensor_scalar(o, o, sc, None, ALU.mult),
                              reads=[lam_b[r], sf_b[h]], writes=[sf_b[h]],
                              tag=f"slam{gc}_{h}"))
                    KNOU or \
                    HH(lambda hh, h, po, fc, ktsl, qtsl, vasl:
                       p.emit("vector", lambda e,
                              o=s_f32[po:po + 64, fc * 65:(fc + 1) * 65],
                              u=pu_ap(hh, po): e.tensor_add(o, o, u),
                              reads=[pbb[6 + hh // 4], sf_b[h]],
                              writes=[sf_b[h]], tag=f"supd{gc}_{h}"))
                HH(lambda hh, h, po, fc, ktsl, qtsl, vasl:
                   p.emit("gpsimd", lambda e,
                          o=s_bf[po:po + 64, fc * 65:(fc + 1) * 65],
                          i=s_f32[po:po + 64, fc * 65:(fc + 1) * 65]:
                          e.tensor_copy(o, i),
                          reads=[sf_b[h]], writes=[sbf_b[h]],
                          tag=f"scast{gc}_{h}"))
                HH(lambda hh, h, po, fc, ktsl, qtsl, vasl:
                   p.emit("vector", lambda e, o=recc[hh], i=comb[hh][:, 64:65]:
                          e.tensor_single_scalar(o[:], i, 1e-6, ALU.add),
                          reads=[comb_b[hh]], writes=[recc_b[hh]],
                          tag=f"eps{gc}_{h}"))
                HH(lambda hh, h, po, fc, ktsl, qtsl, vasl:
                   p.emit("vector", lambda e, o=recc[hh]:
                          e.reciprocal(o[:], o[:]),
                          reads=[recc_b[hh]], writes=[recc_b[hh]],
                          tag=f"rec{gc}_{h}"))
                HH(lambda hh, h, po, fc, ktsl, qtsl, vasl:
                   p.emit("scalar", lambda e, o=atm[r][:, h * 64:(h + 1) * 64],
                          i=comb[hh][:, 0:64], sc=recc[hh]:
                          e.activation(o, i, AF.Identity, scale=sc[:]),
                          reads=[comb_b[hh], recc_b[hh]], writes=[atm_b[r]],
                          tag=f"osc{gc}_{h}"))

            if "attn" in dbg:
                p.dma("sync", lambda e, i=atm[r], g=gc: e.dma_start(
                    out=dbg["attn"][g * 128:(g + 1) * 128, :], in_=i[:]),
                    reads=[atm_b[r]], writes=[buf(f"dbg_at_{r}")])

        def stageA5(gc):
            s, c, r, st, stb = _ahead(gc)
            for d in range(8):
                tp = pb[2 + (d % 2)][:, 0:64].bitcast(BF16)
                tpb = pbb[2 + (d % 2)]
                p.emit("tensor", lambda e, o=tp,
                       i=atm[r][:, d * 128:(d + 1) * 128]:
                       e.transpose(o, i, c_i128bf[:]),
                       reads=[atm_b[r], cb], writes=[tpb],
                       tag=f"trat{gc}_{d}")
                p.emit("vector", lambda e, o=att[r][:, d * 128:(d + 1) * 128],
                       i=tp: e.tensor_copy(o, i),
                       reads=[tpb], writes=[att_b[r]], tag=f"trate{gc}_{d}")
            for half in range(2):
                bank = pb[half]
                bankb = pbb[half]
                for d in range(8):
                    p.emit("tensor", lambda e, o=bank[:, 0:512],
                           l=att[r][:, d * 128:(d + 1) * 128],
                           rr=wo_sb[:, d * D + half * 512: d * D + half * 512 + 512],
                           s0=(d == 0), s1=(d == 7):
                           e.matmul(o, l, rr, start=s0, stop=s1),
                           reads=[att_b[r], wob], writes=[bankb],
                           tag=f"mmwo{gc}_{half}_{d}")
                p.emit("vector", lambda e,
                       o=x1s[r][:, half * 512:half * 512 + 512],
                       a=bank[:, 0:512],
                       bb_=st[:, half * 512:half * 512 + 512]:
                       e.tensor_add(o, a, bb_),
                       reads=[bankb, stb], writes=[x1s_b[r]],
                       tag=f"x1add{gc}_{half}")
            p.dma("gpsimd", lambda e, i=x1s[r], g=gc: e.dma_start(
                out=x1scr[g * 128:(g + 1) * 128, :], in_=i[:]),
                reads=[x1s_b[r]], writes=[buf(f"x1w{r}")],
                tag=f"dma_x1w{gc}")
            p.emit("scalar", lambda e, o=atm[r], i=x1s[r],
                   a=l2ss[:, gc:gc + 1]:
                   e.activation(o[:], i[:], AF.Square, accum_out=a),
                   reads=[x1s_b[r]], writes=[atm_b[r], l2ssb],
                   tag=f"l2sq{gc}")
            p.emit("vector", lambda e, o=l2sm[:, gc:gc + 1], i=x1s[r]:
                   e.reduce_sum(o, i[:], axis=mybir.AxisListType.X),
                   reads=[x1s_b[r]], writes=[l2smb], tag=f"l2sum{gc}")
            if "x1" in dbg:
                p.dma("sync", lambda e, i=x1s[r], g=gc: e.dma_start(
                    out=dbg["x1"][g * 128:(g + 1) * 128, :], in_=i[:]),
                    reads=[x1s_b[r]], writes=[buf(f"dbg_x1_{r}")])

        NCA = NCK * SEQS
        for gc in range(NCA):
            if gc == 0:
                stageA1a(0)
                stageA1b(0)
            if gc + 1 < NCA:
                stageA1a(gc + 1)
            stageA2(gc)
            if gc > 0:
                stageA5(gc - 1)
            stageA3(gc)
            if gc + 1 < NCA:
                stageA1b(gc + 1)
            stageA4(gc)
        stageA5(NCA - 1)

        p.emit("vector", lambda e: e.tensor_single_scalar(
               l2sm[:], l2sm[:], 1.0 / D, ALU.mult),
               reads=[l2smb], writes=[l2smb], tag="l2mean")
        p.emit("vector", lambda e: e.tensor_tensor(
               l2tmp[:], l2sm[:], l2sm[:], ALU.mult),
               reads=[l2smb], writes=[l2tmpb], tag="l2msq")
        p.emit("vector", lambda e: e.scalar_tensor_tensor(
               l2ss[:], l2ss[:], 1.0 / D, l2tmp[:], ALU.mult, ALU.subtract),
               reads=[l2ssb, l2tmpb], writes=[l2ssb], tag="l2var")
        p.emit("vector", lambda e: e.tensor_single_scalar(
               l2ss[:], l2ss[:], 1e-5, ALU.add),
               reads=[l2ssb], writes=[l2ssb], tag="l2vev")
        p.emit("scalar", lambda e: e.activation(l2ss[:], l2ss[:], AF.Ln),
               reads=[l2ssb], writes=[l2ssb], tag="l2ln")
        p.emit("vector", lambda e: e.tensor_single_scalar(
               l2ss[:], l2ss[:], -0.5, ALU.mult),
               reads=[l2ssb], writes=[l2ssb], tag="l2nh")
        p.emit("scalar", lambda e: e.activation(l2inv[:], l2ss[:], AF.Exp),
               reads=[l2ssb], writes=[l2invb], tag="l2ex")
        p.emit("vector", lambda e: e.scalar_tensor_tensor(
               l2nmu[:], l2sm[:], -1.0, l2inv[:], ALU.mult, ALU.mult),
               reads=[l2smb, l2invb], writes=[l2nmub], tag="l2nmu")

        p.barrier()
        sA.close()

    fc_sb = sb("fc_sb", [128, 8 * DF], BF16)
    pj_sb = sb("pj_sb", [128, 32 * D], BF16)
    ht_sb = sb("ht_sb", [128, 32 * 256], BF16)
    xn2T = [sb(f"xn2T{i}", [128, 8 * 256], BF16) for i in range(2)]
    x1st = [sb(f"x1st{i}", [128, D], BF16) for i in range(6)]
    xn2b = [sb(f"xn2b{i}", [128, D], BF16) for i in range(4)]
    y_sb = [sb(f"y_sb{i}", [128, 512], F32) for i in range(4)]
    sn_s = [sb(f"sn_s{i}", [128, 512], F32) for i in range(4)]
    sn_a = [sb(f"sn_a{i}", [128, 512], BF16) for i in range(4)]
    sn_m = [sb(f"sn_m{i}", [128, 512], F32) for i in range(4)]
    rh_sb = [sb(f"rh_sb{i}", [128, 256], F32) for i in range(4)]

    fcb, pjb = [], []
    for i in range(8):
        inap = fct.rearrange("(c p) f -> p c f", p=128)[:, :, i * 512:(i + 1) * 512]
        outap = fc_sb[:].rearrange("p (c f) -> p c f", f=DF)[:, :, i * 512:(i + 1) * 512]
        p.dma("gpsimd", lambda e, o=outap, s=inap: e.dma_start(out=o, in_=s),
              writes=[buf(f"fc{i}")], tag=f"dma_fc{i}")
        fcb.append(buf(f"fc{i}"))
    for i in range(8):
        inap = pjt.rearrange("(c p) f -> p c f", p=128)[:, i * 4:(i + 1) * 4, :]
        outap = pj_sb[:].rearrange("p (c f) -> p c f", f=D)[:, i * 4:(i + 1) * 4, :]
        p.dma("gpsimd", lambda e, o=outap, s=inap: e.dma_start(out=o, in_=s),
              writes=[buf(f"pj{i}")], tag=f"dma_pj{i}")
        pjb.append(buf(f"pj{i}"))

    htb = buf("ht")
    xn2T_b = [buf("xn2T0"), buf("xn2T1")]
    x1st_b = [buf(f"x1st{i}") for i in range(6)]
    xn2b_b = [buf(f"xn2b{i}") for i in range(4)]
    y_b = [buf(f"y{i}") for i in range(4)]
    sns_b = [buf(f"sns{i}") for i in range(4)]
    sna_b = [buf(f"sna{i}") for i in range(4)]
    snm_b = [buf(f"snm{i}") for i in range(4)]
    rh_b = [buf(f"rh{i}") for i in range(4)]
    x1r_deps = [buf("x1w0"), buf("x1w1")]

    def stageP1a(grp):
        gr = grp % 2
        for j in range(2):
            cc = grp * 2 + j
            stb_ = x1st_b[cc % 6]
            st_ = x1st[cc % 6]
            p.dma("scalar", lambda e, o=st_, g=cc: e.dma_start(
                out=o[:], in_=x1scr[g * 128:(g + 1) * 128, :]),
                reads=x1r_deps, writes=[stb_], tag=f"dma_x1r{cc}")
            p.emit("scalar", lambda e, o=xn2b[cc % 4][:], i=st_[:],
                   sc=l2inv[:, cc:cc + 1], b2=l2nmu[:, cc:cc + 1]:
                   e.activation(o, i, AF.Identity, scale=sc, bias=b2),
                   reads=[stb_, l2invb, l2nmub], writes=[xn2b_b[cc % 4]],
                   tag=f"ln2app{cc}")

    def stageP1b(grp):
        gr = grp % 2
        for j in range(2):
            cc = grp * 2 + j
            for d in range(8):
                tp = pb[2 + (d % 2)][:, 0:64].bitcast(BF16)
                tpb = pbb[2 + (d % 2)]
                p.emit("tensor", lambda e, o=tp,
                       i=xn2b[cc % 4][:, d * 128:(d + 1) * 128]:
                       e.transpose(o, i, c_i128bf[:]),
                       reads=[xn2b_b[cc % 4], cb], writes=[tpb],
                       tag=f"tr2_{cc}_{d}")
                p.emit("scalar", lambda e,
                       o=xn2T[gr][:, d * 256 + j * 128: d * 256 + j * 128 + 128],
                       i=tp: e.activation(o, i, AF.Copy),
                       reads=[tpb], writes=[xn2T_b[gr]], tag=f"tr2e{cc}_{d}")

    def stageP2(grp):
        gr = grp % 2
        for fcx in range(32):
            bank = pb[fcx % 4]
            bankb = pbb[fcx % 4]
            for d in range(8):
                p.emit("tensor", lambda e, o=bank[:, 0:256],
                       l=fc_sb[:, d * DF + fcx * 128: d * DF + (fcx + 1) * 128],
                       rr=xn2T[gr][:, d * 256:(d + 1) * 256],
                       s0=(d == 0), s1=(d == 7):
                       e.matmul(o, l, rr, start=s0, stop=s1),
                       reads=[fcb[fcx // 4], xn2T_b[gr]], writes=[bankb],
                       tag=f"mmh{grp}_{fcx}_{d}")
            p.emit("scalar", lambda e, o=rh_sb[fcx % 4], i=bank[:, 0:256]:
                   e.activation(o[:], i, AF.Relu),
                   reads=[bankb], writes=[rh_b[fcx % 4]], tag=f"relu{grp}_{fcx}")
            engm = "gpsimd"
            p.emit(engm, lambda e, o=ht_sb[:, fcx * 256:(fcx + 1) * 256],
                   i=rh_sb[fcx % 4]: e.tensor_mul(o, i[:], i[:]),
                   reads=[rh_b[fcx % 4]], writes=[htb], tag=f"sq{grp}_{fcx}")

    def stageP3(grp):
        gr = grp % 2
        tiles = []
        for j in range(2):
            cc = grp * 2 + j
            for half in range(2):
                t = 2 * j + half
                bank = pb[4 + t]
                bankb = pbb[4 + t]
                for fcx in range(32):
                    p.emit("tensor", lambda e, o=bank[:, 0:512],
                           l=ht_sb[:, fcx * 256 + j * 128: fcx * 256 + j * 128 + 128],
                           rr=pj_sb[:, fcx * D + half * 512: fcx * D + half * 512 + 512],
                           s0=(fcx == 0), s1=(fcx == 31):
                           e.matmul(o, l, rr, start=s0, stop=s1),
                           reads=[htb, pjb[fcx // 4]],
                           writes=[bankb], tag=f"mmy{cc}_{half}_{fcx}")
                tiles.append((t, cc, j, half, bank, bankb))

        def each(fn):
            for (tp2, cc, j, half, bank, bankb) in tiles:
                fn(tp2, cc, j, half, bank, bankb)

        each(lambda tp2, cc, j, half, bank, bankb:
             p.emit("vector", lambda e, o=y_sb[tp2], i=bank[:, 0:512]:
                    e.tensor_copy(o[:], i),
                    reads=[bankb], writes=[y_b[tp2]], tag=f"yev{cc}_{half}"))

        def sg(src_list):
            each(lambda tp2, cc, j, half, bank, bankb:
                 p.emit("vector", lambda e, o=sn_s[tp2], i=src_list[tp2]:
                        e.tensor_single_scalar(o[:], i[:], 1.0, ALU.is_gt),
                        reads=[(y_b if src_list is y_sb else snm_b)[tp2]],
                        writes=[sns_b[tp2]], tag=f"sg{cc}_{half}"))

        sg(y_sb)
        each(lambda tp2, cc, j, half, bank, bankb:
             p.emit("vector", lambda e, o=sn_a[tp2], i=sn_s[tp2]:
                    e.tensor_copy(o[:], i[:]),
                    reads=[sns_b[tp2]], writes=[sna_b[tp2]], tag=f"sac{cc}_{half}"))
        each(lambda tp2, cc, j, half, bank, bankb:
             p.emit("vector", lambda e, o=sn_m[tp2], i=y_sb[tp2],
                    s2=sn_s[tp2]:
                    e.scalar_tensor_tensor(o[:], i[:], 1.9, s2[:],
                                           ALU.mult, ALU.subtract),
                    reads=[y_b[tp2], sns_b[tp2]], writes=[snm_b[tp2]],
                    tag=f"sn3{cc}_{half}"))
        for k2 in range(3):
            sg(sn_m)
            each(lambda tp2, cc, j, half, bank, bankb:
                 p.emit("vector", lambda e, o=sn_a[tp2], i=sn_s[tp2]:
                        e.tensor_add(o[:], o[:], i[:]),
                        reads=[sns_b[tp2], sna_b[tp2]], writes=[sna_b[tp2]],
                        tag=f"sad{k2}_{cc}_{half}"))
            each(lambda tp2, cc, j, half, bank, bankb:
                 p.emit("vector", lambda e, o=sn_m[tp2], s2=sn_s[tp2]:
                        e.scalar_tensor_tensor(o[:], o[:], 0.9, s2[:],
                                               ALU.mult, ALU.subtract),
                        reads=[sns_b[tp2], snm_b[tp2]], writes=[snm_b[tp2]],
                        tag=f"sw{k2}_{cc}_{half}"))
            each(lambda tp2, cc, j, half, bank, bankb:
                 p.emit("vector", lambda e, o=sn_m[tp2], y2=y_sb[tp2]:
                        e.tensor_add(o[:], o[:], y2[:]),
                        reads=[y_b[tp2], snm_b[tp2]], writes=[snm_b[tp2]],
                        tag=f"swy{k2}_{cc}_{half}"))
        sg(sn_m)
        each(lambda tp2, cc, j, half, bank, bankb:
             p.emit("vector", lambda e, o=sn_a[tp2], i=sn_s[tp2]:
                    e.tensor_add(o[:], o[:], i[:]),
                    reads=[sns_b[tp2], sna_b[tp2]], writes=[sna_b[tp2]],
                    tag=f"sa9{cc}_{half}"))
        each(lambda tp2, cc, j, half, bank, bankb:
             p.emit("vector", lambda e, o=sn_m[tp2], i=sn_a[tp2]:
                    e.tensor_single_scalar(o[:], i[:], 0.2, ALU.mult),
                    reads=[sna_b[tp2]], writes=[snm_b[tp2]], tag=f"so1{cc}_{half}"))
        def snout(tp2, cc, j, half, bank, bankb):
            p.emit("vector", lambda e, o=y_sb[tp2][:, 0:512], a=sn_m[tp2],
                   x1=x1st[cc % 6][:, half * 512:half * 512 + 512]:
                   e.tensor_add(o, a[:], x1),
                   reads=[snm_b[tp2], x1st_b[cc % 6]],
                   writes=[y_b[tp2]], tag=f"snout{cc}_{half}")
            p.dma("sync", lambda e, i=y_sb[tp2], g=cc, hf=half:
                  e.dma_start(out=out_t[g * 128:(g + 1) * 128,
                                        hf * 512:hf * 512 + 512], in_=i[:]),
                  reads=[y_b[tp2]], writes=[buf(f"outw{tp2}")],
                  tag=f"dma_out{cc}_{half}")
        each(snout)

    NG = NTOK // 256
    for grp in range(NG):
        if grp == 0:
            stageP1a(0)
            stageP1b(0)
        stageP2(grp)
        if grp + 1 < NG:
            stageP1a(grp + 1)
            stageP1b(grp + 1)
        stageP3(grp)

    fin_reads = [buf(f"outw{i}") for i in range(4)]
    fin_reads += [B[k] for k in B if k.startswith("dbg_")]
    p.emit("sync", lambda e: None, reads=fin_reads, tag="final")

    p.build(nc, verbose=verbose)
    stack.close()
    return nc


_CACHE = {}


def _get_program(debug=()):
    key = tuple(sorted(debug))
    if key not in _CACHE:
        _CACHE[key] = build_program(debug=key)
    return _CACHE[key]


def kernel(L0_attn_x, L0_attn_Wq, L0_attn_Wk, L0_attn_Wv, L0_attn_Wo,
           L0_attn_Ww, L0_attn_bw, L0_attn_We, L0_attn_be,
           L0_ffn_W_fc, L0_ffn_W_proj, _debug=(), _return_raw=False):
    x = np.ascontiguousarray(np.asarray(L0_attn_x, dtype=np.float32))
    com = {
        "wqt": np.ascontiguousarray(np.asarray(L0_attn_Wq, np.float32).T),
        "wkt": np.ascontiguousarray(np.asarray(L0_attn_Wk, np.float32).T),
        "wvt": np.ascontiguousarray(np.asarray(L0_attn_Wv, np.float32).T),
        "wot": np.ascontiguousarray(np.asarray(L0_attn_Wo, np.float32).T),
        "wwt": np.ascontiguousarray(np.asarray(L0_attn_Ww, np.float32).T),
        "wet": np.ascontiguousarray(np.asarray(L0_attn_We, np.float32).T),
        "bwc": np.ascontiguousarray(np.asarray(L0_attn_bw, np.float32)
                                    .reshape(H, 1)),
        "bec": np.ascontiguousarray(np.asarray(L0_attn_be, np.float32)
                                    .reshape(H, 1)),
        "fct": np.ascontiguousarray(np.asarray(L0_ffn_W_fc, np.float32).T),
        "pjt": np.ascontiguousarray(np.asarray(L0_ffn_W_proj, np.float32).T),
        "i128bf": np.eye(128, dtype=BF),
        "i128f": np.eye(128, dtype=np.float32),
        "i16f": np.eye(16, dtype=np.float32),
        "mpre": np.triu(np.full((128, 128), -1e30, np.float32), 1),
        "ones16": np.ones((16, 128), np.float32),
        "galg": np.tile(np.array([[np.float32(0.95) + np.float32(1e-8),
                                   1e-8]], np.float32), (16, 1)),
    }
    nc = _get_program(_debug)
    in_maps = []
    for k in range(N_CORES):
        m = dict(com)
        m["xin"] = np.ascontiguousarray(
            x[SEQS * k:SEQS * (k + 1)].reshape(NTOK, D))
        in_maps.append(m)
    res = run_bass_kernel_spmd(nc, in_maps, core_ids=list(range(N_CORES)))
    if _return_raw:
        return res
    out = np.concatenate(
        [r["out"].reshape(SEQS, L, D) for r in res.results], axis=0)
    return out


# version 36 (final)
# speedup vs baseline: 1.1968x; 1.0457x over previous
import contextlib
import numpy as np
import ml_dtypes
import concourse.bass as bass
import concourse.mybir as mybir
from concourse.bass_utils import run_bass_kernel_spmd

F32 = mybir.dt.float32
BF16 = mybir.dt.bfloat16
AF = mybir.ActivationFunctionType
ALU = mybir.AluOpType
BF = ml_dtypes.bfloat16

N_CORES = 8
SEQS = 2
L = 2048
D = 1024
H = 16
DH = 64
NCK = L // 128
NTOK = SEQS * L
DF = 4096

DEBUG = ()

ENGINE_NAMES = ("tensor", "vector", "scalar", "gpsimd", "sync")


class Buf:
    __slots__ = ("name", "group", "last_writer", "readers", "semkey")

    def __init__(self, name, group=None):
        self.name = name
        self.group = group
        self.last_writer = None
        self.readers = []
        self.semkey = None


class Node:
    __slots__ = ("idx", "eng", "key", "fn", "deps", "is_dma", "needs_inc",
                 "inc_value", "vclock", "tag")

    def __init__(self, idx, eng, fn, tag):
        self.idx = idx
        self.eng = eng
        self.key = eng
        self.fn = fn
        self.deps = set()
        self.is_dma = False
        self.needs_inc = False
        self.inc_value = None
        self.vclock = None
        self.tag = tag


class Prog:
    def __init__(self):
        self.nodes = []
        self.per_engine = {e: [] for e in ENGINE_NAMES}
        self.group_keys = set()
        self.last_by_key = {}
        self._pending = {e: [] for e in ENGINE_NAMES}
        self.n_waits = 0
        self.n_incs = 0

    def buf(self, name, group=None):
        return Buf(name, group)

    def emit(self, eng, fn, reads=(), writes=(), extra_deps=(), tag=""):
        n = Node(len(self.nodes), eng, fn, tag)
        if self._pending[eng]:
            for d in self._pending[eng]:
                if d is not None and d is not n:
                    n.deps.add(d)
            self._pending[eng] = []
        for b in reads:
            if b.last_writer is not None:
                n.deps.add(b.last_writer)
            b.readers.append(n)
        for b in writes:
            for r in b.readers:
                if r is not n:
                    n.deps.add(r)
            if b.last_writer is not None and b.last_writer is not n:
                n.deps.add(b.last_writer)
            b.last_writer = n
            b.readers = [n] if n in b.readers else []
        for d in extra_deps:
            if d is not None and d is not n:
                n.deps.add(d)
        self.nodes.append(n)
        self.per_engine[eng].append(n)
        self.last_by_key[n.key] = n
        return n

    def dma(self, eng, fn, reads=(), writes=(), extra_deps=(), tag="dma"):
        n = self.emit(eng, fn, reads, writes, extra_deps, tag)
        n.is_dma = True
        b = writes[0]
        if b.semkey is None:
            b.semkey = ("g:" + b.group) if b.group else ("b:" + b.name)
            if b.group:
                self.group_keys.add(b.semkey)
        n.key = b.semkey
        self.last_by_key[n.key] = n
        return n

    def barrier(self):
        deps = list(self.last_by_key.values())
        for e in ENGINE_NAMES:
            self._pending[e] = list(deps)

    def build(self, nc, verbose=False):
        for n in self.nodes:
            for d in n.deps:
                if d.key != n.key or d.is_dma:
                    d.needs_inc = True
                elif d.eng == n.eng and n.eng != "tensor":
                    d.needs_inc = True
        for n in self.nodes:
            if n.is_dma:
                n.needs_inc = True
        sem_total = {}
        for n in self.nodes:
            if n.needs_inc:
                step = 16 if n.is_dma else 1
                sem_total[n.key] = sem_total.get(n.key, 0) + step
                n.inc_value = sem_total[n.key]
        last_dma_idx = {}
        for n in self.nodes:
            if n.is_dma and n.key in self.group_keys:
                last_dma_idx[n.key] = n.idx
        clocks = {e: {} for e in ENGINE_NAMES}
        waits = {}
        predrain = set()
        drained_upto = {e: 0 for e in ENGINE_NAMES}
        for n in self.nodes:
            ww = {}
            myclock = clocks[n.eng]
            for d in n.deps:
                if (n.is_dma and d.is_dma and d.key == n.key
                        and d.key in self.group_keys):
                    continue
                if d.eng == n.eng and not d.is_dma and n.eng == "tensor":
                    continue
                v = d.inc_value
                if d.key in self.group_keys:
                    v = sem_total[d.key]
                    assert last_dma_idx[d.key] < n.idx, \
                        f"group {d.key} DMA after waiter {n.tag}"
                if myclock.get(d.key, 0) >= v:
                    continue
                if ww.get(d.key, 0) < v:
                    ww[d.key] = v
            if ww:
                for pk, v in ww.items():
                    myclock[pk] = max(myclock.get(pk, 0), v)
                for d in n.deps:
                    if d.vclock and ww.get(d.key, 0) >= (d.inc_value or 0):
                        for k, kv in d.vclock.items():
                            if myclock.get(k, 0) < kv:
                                myclock[k] = kv
                waits[n] = sorted(ww.items())
                self.n_waits += len(ww)
            if n.needs_inc:
                vc = dict(clocks[n.eng])
                vc[n.key] = n.inc_value
                n.vclock = vc
                self.n_incs += 1
        self._wmap = waits
        self._predrain = predrain

        used_keys = sorted(set(n.key for n in self.nodes if n.needs_inc))
        with contextlib.ExitStack() as stack:
            sem_handles = {k: stack.enter_context(nc.semaphore(f"s_{k}"))
                           for k in used_keys}
            block = stack.enter_context(nc.Block())
            prog = self

            def mk_runner(ename):
                def run(engobj):
                    for n in prog.per_engine[ename]:
                        if n in prog._predrain:
                            engobj.drain()
                        w = prog._wmap.get(n)
                        if w:
                            for pk, v in w:
                                engobj.wait_ge(sem_handles[pk], v)
                        inst = n.fn(engobj)
                        if n.needs_inc:
                            inst.then_inc(sem_handles[n.key],
                                          16 if n.is_dma else 1)
                return run

            for e in ENGINE_NAMES:
                if self.per_engine[e]:
                    getattr(block, e)(mk_runner(e))
        if verbose:
            print(f"[fw] nodes={len(self.nodes)} incs={self.n_incs} "
                  f"waits={self.n_waits} drains={len(predrain)} "
                  f"sems={len(used_keys)}")
        return sem_total


def build_program(debug=(), verbose=False):
    nc = bass.Bass("TRN2", target_bir_lowering=False, debug=False)
    p = Prog()

    xin = nc.dram_tensor("xin", [NTOK, D], F32, kind="ExternalInput").ap()
    wqt = nc.dram_tensor("wqt", [D, D], F32, kind="ExternalInput").ap()
    wkt = nc.dram_tensor("wkt", [D, D], F32, kind="ExternalInput").ap()
    wvt = nc.dram_tensor("wvt", [D, D], F32, kind="ExternalInput").ap()
    wot = nc.dram_tensor("wot", [D, D], F32, kind="ExternalInput").ap()
    wwt = nc.dram_tensor("wwt", [D, H], F32, kind="ExternalInput").ap()
    wet = nc.dram_tensor("wet", [D, H], F32, kind="ExternalInput").ap()
    bwc = nc.dram_tensor("bwc", [H, 1], F32, kind="ExternalInput").ap()
    bec = nc.dram_tensor("bec", [H, 1], F32, kind="ExternalInput").ap()
    fct = nc.dram_tensor("fct", [D, DF], F32, kind="ExternalInput").ap()
    pjt = nc.dram_tensor("pjt", [DF, D], F32, kind="ExternalInput").ap()
    i128bf = nc.dram_tensor("i128bf", [128, 128], BF16, kind="ExternalInput").ap()
    i128f = nc.dram_tensor("i128f", [128, 128], F32, kind="ExternalInput").ap()
    i16f = nc.dram_tensor("i16f", [16, 16], F32, kind="ExternalInput").ap()
    mpre = nc.dram_tensor("mpre", [128, 128], F32, kind="ExternalInput").ap()
    ones16 = nc.dram_tensor("ones16", [16, 128], F32, kind="ExternalInput").ap()
    galg = nc.dram_tensor("galg", [16, 2], F32, kind="ExternalInput").ap()
    out_t = nc.dram_tensor("out", [NTOK, D], F32, kind="ExternalOutput").ap()
    x1scr = nc.dram_tensor("x1scr", [NTOK, D], BF16, kind="Internal").ap()

    dbg = {}
    for name, shape, dt in [("xn", [NTOK, D], BF16), ("qt", [NTOK, D], BF16),
                            ("kt", [NTOK, D], BF16), ("vaug", [NTOK, 1040], BF16),
                            ("cc", [SEQS * NCK * 16, 128], F32),
                            ("bb", [SEQS * NCK * 16, 128], F32),
                            ("attn", [NTOK, D], BF16), ("x1", [NTOK, D], F32)]:
        if name in debug:
            dbg[name] = nc.dram_tensor("dbg_" + name, shape, dt,
                                       kind="ExternalOutput").ap()

    stack = contextlib.ExitStack()
    sb = lambda name, shape, dt: stack.enter_context(nc.sbuf_tensor(name, shape, dt))
    psalloc = lambda name: stack.enter_context(nc.psum_tensor(name, [128, 512], F32))

    c_i128bf = sb("c_i128bf", [128, 128], BF16)
    c_i128f = sb("c_i128f", [128, 128], F32)
    c_i16f = sb("c_i16f", [16, 16], F32)
    c_mpre = sb("c_mpre", [128, 128], F32)
    c_ones16 = sb("c_ones16", [16, 128], F32)
    c_galg = sb("c_galg", [16, 2], F32)
    c_bwc = sb("c_bwc", [16, 1], F32)
    c_nbw = sb("c_nbw", [16, 1], F32)
    c_bec = sb("c_bec", [16, 1], F32)
    pb = [psalloc(f"pb{i}") for i in range(8)]

    B = {}

    def buf(name, group=None):
        if name not in B:
            B[name] = p.buf(name, group)
        return B[name]

    pbb = [buf(f"pb{i}") for i in range(8)]

    c_m1 = sb("c_m1", [128, 1], F32)
    p.emit("vector", lambda e: e.memset(c_m1[:], -1.0),
           writes=[buf("m1w")], tag="m1set")
    m1b = buf("m1w")

    ln_cols = [sb(f"ln_cols{i}", [128, 8], F32) for i in range(2)]
    ln_cols_b = [buf("ln_cols0"), buf("ln_cols1")]
    l2inv = sb("l2inv", [128, 32], F32)
    l2nmu = sb("l2nmu", [128, 32], F32)
    l2invb = buf("l2inv")
    l2nmub = buf("l2nmu")

    def emit_ln(idx, x_ap, xbuf, out_ap, outbuf, tagp):
        par = idx % 2
        cl, clb = ln_cols[par], ln_cols_b[par]
        p.emit("scalar", lambda e, o=out_ap, i=x_ap, a=cl[:, 1:2]:
               e.activation(o, i, AF.Square, accum_out=a),
               reads=[xbuf], writes=[outbuf, clb], tag=f"{tagp}sq")
        p.emit("vector", lambda e, o=cl[:, 0:1], i=x_ap:
               e.reduce_sum(o, i, axis=mybir.AxisListType.X),
               reads=[xbuf], writes=[clb], tag=f"{tagp}sum")
        p.emit("vector", lambda e, o=cl[:, 2:3], i=cl[:, 0:1]:
               e.tensor_single_scalar(o, i, 1.0 / D, ALU.mult),
               reads=[clb], writes=[clb], tag=f"{tagp}mean")
        p.emit("vector", lambda e, o=cl[:, 3:4], i=cl[:, 2:3]:
               e.tensor_tensor(o, i, i, ALU.mult),
               reads=[clb], writes=[clb], tag=f"{tagp}msq")
        p.emit("vector", lambda e, o=cl[:, 4:5], i=cl[:, 1:2], m=cl[:, 3:4]:
               e.scalar_tensor_tensor(o, i, 1.0 / D, m, ALU.mult, ALU.subtract),
               reads=[clb], writes=[clb], tag=f"{tagp}var")
        p.emit("vector", lambda e, o=cl[:, 4:5], i=cl[:, 4:5]:
               e.tensor_single_scalar(o, i, 1e-5, ALU.add),
               reads=[clb], writes=[clb], tag=f"{tagp}vev")
        p.emit("scalar", lambda e, o=cl[:, 4:5], i=cl[:, 4:5]:
               e.activation(o, i, AF.Ln),
               reads=[clb], writes=[clb], tag=f"{tagp}ln")
        p.emit("vector", lambda e, o=cl[:, 4:5], i=cl[:, 4:5]:
               e.tensor_single_scalar(o, i, -0.5, ALU.mult),
               reads=[clb], writes=[clb], tag=f"{tagp}nh")
        p.emit("scalar", lambda e, o=cl[:, 4:5], i=cl[:, 4:5]:
               e.activation(o, i, AF.Exp),
               reads=[clb], writes=[clb], tag=f"{tagp}ex")
        p.emit("vector", lambda e, o=cl[:, 5:6], i=cl[:, 2:3], v=cl[:, 4:5]:
               e.tensor_scalar(o, i, v, -1.0, ALU.mult, ALU.mult),
               reads=[clb], writes=[clb], tag=f"{tagp}nmu")
        p.emit("scalar", lambda e, o=out_ap, i=x_ap, v=cl[:, 4:5],
               b2=cl[:, 5:6]:
               e.activation(o, i, AF.Identity, scale=v, bias=b2),
               reads=[xbuf, clb], writes=[outbuf], tag=f"{tagp}app")

    for cap, src in [(c_i128bf, i128bf), (c_i128f, i128f), (c_i16f, i16f),
                     (c_mpre, mpre), (c_ones16, ones16), (c_bwc, bwc),
                     (c_bec, bec), (c_galg, galg)]:
        p.dma("sync", lambda e, o=cap, s=src: e.dma_start(out=o[:], in_=s),
              writes=[buf("consts", group="consts")])
    cb = buf("consts")
    p.emit("vector", lambda e: e.tensor_scalar(c_nbw[:], c_bwc[:], -1.0, None,
                                               ALU.mult),
           reads=[cb], writes=[buf("nbwd")], tag="negbw")
    nbwb = buf("nbwd")

    with contextlib.ExitStack() as sA:
        sbA = lambda name, shape, dt: sA.enter_context(nc.sbuf_tensor(name, shape, dt))
        stg = [sbA(f"stg{i}", [128, D], F32) for i in range(4)]
        l2sm = sbA("l2sm", [128, 32], F32)
        l2ss = sbA("l2ss", [128, 32], F32)
        l2tmp = sbA("l2tmp", [128, 32], F32)
        l2smb, l2ssb, l2tmpb = buf("l2sm"), buf("l2ss"), buf("l2tmp")
        stgb = [buf(f"stg{i}") for i in range(4)]
        wq_sb = sbA("wq_sb", [128, 8 * D], BF16)
        wk_sb = sbA("wk_sb", [128, 8 * D], BF16)
        wv_sb = sbA("wv_sb", [128, 8 * D], BF16)
        wo_sb = sbA("wo_sb", [128, 8 * D], BF16)
        ww_sb = sbA("ww_sb", [128, 8 * H], BF16)
        we_sb = sbA("we_sb", [128, 8 * H], BF16)
        xn_bf = [sbA(f"xn_bf{i}", [128, D], BF16) for i in range(2)]
        xnT = [sbA(f"xnT{i}", [128, D], BF16) for i in range(2)]
        qt_c = [sbA(f"qt_c{i}", [128, D], BF16) for i in range(2)]
        kt_c = [sbA(f"kt_c{i}", [128, D], BF16) for i in range(2)]
        va_c = [sbA(f"va_c{i}", [128, H * 65], BF16) for i in range(2)]
        s_f32 = sbA("s_f32", [128, 8 * 65], F32)
        s_bf = sbA("s_bf", [128, 8 * 65], BF16)
        g_u = sbA("g_u", [16, 128], F32)
        g_lw1 = sbA("g_lw1", [16, 128], F32)
        g_t = sbA("g_t", [16, 128], F32)
        g_v = sbA("g_v", [16, 128], F32)
        g_t1 = sbA("g_t1", [16, 128], F32)
        g_logg = sbA("g_logg", [16, 128], F32)
        g_c = [sbA(f"g_c{i}", [16, 128], F32) for i in range(2)]
        g_b = sbA("g_b", [16, 128], F32)
        carry = sbA("carry", [16, 1], F32)
        cprev = sbA("cprev", [16, 1], F32)
        dcol = sbA("dcol", [16, 1], F32)
        asrc = sbA("asrc", [16, 128], F32)
        ct_c = [sbA(f"ct_c{i}", [128, 16], F32) for i in range(2)]
        bt_c = [sbA(f"bt_c{i}", [128, 16], F32) for i in range(2)]
        qd_c = [sbA(f"qd_c{i}", [128, 16], F32) for i in range(2)]
        kd_c = [sbA(f"kd_c{i}", [128, 16], F32) for i in range(2)]
        lam_c = [sbA(f"lam_c{i}", [128, 16], F32) for i in range(2)]
        qdt = sbA("qdt", [128, 16], F32)
        kdt = sbA("kdt", [128, 16], F32)
        scr_e = [sbA(f"scr_e{i}", [128, 512], F32) for i in range(2)]
        scr_m = [sbA(f"scr_m{i}", [128, 512], F32) for i in range(2)]
        tmpM = [sbA(f"tmpM{i}", [128, 128], F32) for i in range(8)]
        dec = [sbA(f"dec{i}", [128, 128], F32) for i in range(8)]
        effT = [sbA(f"effT{i}", [128, 128], BF16) for i in range(8)]
        kdb = [sbA(f"kdb{i}", [128, DH], BF16) for i in range(8)]
        r1s = [sbA(f"r1s{i}", [128, 65], F32) for i in range(8)]
        comb = [sbA(f"comb{i}", [128, 65], F32) for i in range(8)]
        recc = [sbA(f"recc{i}", [128, 1], F32) for i in range(8)]
        atm = [sbA(f"atm{i}", [128, D], BF16) for i in range(2)]
        att = [sbA(f"att{i}", [128, D], BF16) for i in range(2)]
        x1s = [sbA(f"x1s{i}", [128, D], F32) for i in range(2)]

        def wload(dst, src, cdim, fdim, name):
            inap = src.rearrange("(c p) f -> p c f", p=128)
            outap = dst[:].rearrange("p (c f) -> p c f", f=fdim)
            p.dma("gpsimd", lambda e, o=outap, i=inap: e.dma_start(out=o, in_=i),
                  writes=[buf(name)], tag="dma_w_" + name)
            return buf(name)

        wqb = wload(wq_sb, wqt, 8, D, "wq")
        wkb = wload(wk_sb, wkt, 8, D, "wk")
        wvb = wload(wv_sb, wvt, 8, D, "wv")
        wob = wload(wo_sb, wot, 8, D, "wo")
        wwb = wload(ww_sb, wwt, 8, H, "ww")
        web = wload(we_sb, wet, 8, H, "we")

        sbuf_bufs = {}

        def tb(obj, name):
            key = name
            if key not in sbuf_bufs:
                sbuf_bufs[key] = buf(key)
            return sbuf_bufs[key]

        stg_b = stgb
        xnbf_b = [buf("xnbf0"), buf("xnbf1")]
        xnT_b = [buf("xnT0"), buf("xnT1")]
        qt_b = [buf("qt0"), buf("qt1")]
        kt_b = [buf("kt0"), buf("kt1")]
        va_b = [buf("va0"), buf("va1")]
        sf_b = [buf(f"sf{h}") for h in range(H)]
        sbf_b = [buf(f"sbf{h}") for h in range(H)]
        gate_b = {nm: buf("g_" + nm) for nm in
                  ["u", "lw1", "t", "v", "t1", "logg", "b", "carry", "cprev",
                   "dcol", "asrc", "qdt", "kdt"]}
        gc_b = [buf("gc0"), buf("gc1")]
        ct_b = [buf("ct0"), buf("ct1")]
        bt_b = [buf("bt0"), buf("bt1")]
        qd_b = [buf("qd0"), buf("qd1")]
        kd_b = [buf("kd0"), buf("kd1")]
        lam_b = [buf("lam0"), buf("lam1")]
        scre_b = [buf("scre0"), buf("scre1")]
        scrm_b = [buf("scrm0"), buf("scrm1")]
        tmpM_b = [buf(f"tmpM{i}") for i in range(8)]
        dec_b = [buf(f"dec{i}") for i in range(8)]
        effT_b = [buf(f"effT{i}") for i in range(8)]
        kdb_b = [buf(f"kdb{i}") for i in range(8)]
        r1_b = [buf(f"r1{i}") for i in range(8)]
        comb_b = [buf(f"comb{i}") for i in range(8)]
        recc_b = [buf(f"recc{i}") for i in range(8)]
        atm_b = [buf("atm0"), buf("atm1")]
        att_b = [buf("att0"), buf("att1")]
        x1s_b = [buf("x1s0"), buf("x1s1")]

        def _ahead(gc):
            return (gc // NCK, gc % NCK, gc % 2, stg[gc % 4], stg_b[gc % 4])

        def stageA1a(gc):
            s, c, r, st, stb = _ahead(gc)
            p.dma("sync", lambda e, o=st, g=gc: e.dma_start(
                out=o[:], in_=xin[g * 128:(g + 1) * 128, :]),
                writes=[stb], tag=f"dma_x{gc}")
            emit_ln(gc, st[:], stb, xn_bf[r][:], xnbf_b[r], f"ln{gc}_")
            if "xn" in dbg:
                p.dma("sync", lambda e, i=xn_bf[r], g=gc: e.dma_start(
                    out=dbg["xn"][g * 128:(g + 1) * 128, :], in_=i[:]),
                    reads=[xnbf_b[r]], writes=[buf(f"dbg_xn_{r}")])
        def stageA1b(gc):
            s, c, r, st, stb = _ahead(gc)
            for d in range(8):
                tp = pb[2 + (d % 2)][:, 0:64].bitcast(BF16)
                tpb = pbb[2 + (d % 2)]
                p.emit("tensor", lambda e, o=tp, i=xn_bf[r][:, d * 128:(d + 1) * 128]:
                       e.transpose(o, i, c_i128bf[:]),
                       reads=[xnbf_b[r], cb], writes=[tpb], tag=f"trx{gc}_{d}")
                p.emit("vector", lambda e, o=xnT[r][:, d * 128:(d + 1) * 128], i=tp:
                       e.tensor_copy(o, i),
                       reads=[tpb], writes=[xnT_b[r]], tag=f"trxe{gc}_{d}")

        def stageA2(gc):
            s, c, r, st, stb = _ahead(gc)
            for (wsb_, wb_, bank, bankb, nm) in [
                    (ww_sb, wwb, pb[4], pbb[4], "zw"),
                    (we_sb, web, pb[5], pbb[5], "ze")]:
                for d in range(8):
                    p.emit("tensor", lambda e, o=bank[0:16, 0:128],
                           l=wsb_[:, d * H:(d + 1) * H],
                           rr=xnT[r][:, d * 128:(d + 1) * 128],
                           s0=(d == 0), s1=(d == 7):
                           e.matmul(o, l, rr, start=s0, stop=s1),
                           reads=[wb_, xnT_b[r]], writes=[bankb],
                           tag=f"mm{nm}{gc}_{d}")

            for (wsb, wb_, dst, dstb, nm) in [
                    (wq_sb, wqb, qt_c[r], qt_b[r], "q"),
                    (wk_sb, wkb, kt_c[r], kt_b[r], "k")]:
                for quad in range(2):
                    bank = pb[quad]
                    bankb = pbb[quad]
                    for fq in range(4):
                        fc = quad * 4 + fq
                        for d in range(8):
                            p.emit("tensor", lambda e,
                                   o=bank[:, fq * 128:(fq + 1) * 128],
                                   l=wsb[:, d * D + fc * 128: d * D + (fc + 1) * 128],
                                   rr=xnT[r][:, d * 128:(d + 1) * 128],
                                   s0=(d == 0), s1=(d == 7):
                                   e.matmul(o, l, rr, start=s0, stop=s1),
                                   reads=[wb_, xnT_b[r]], writes=[bankb],
                                   tag=f"mm{nm}{gc}_{fc}_{d}")
                    p.emit("scalar", lambda e, o=scr_e[quad], i=bank:
                           e.activation(o[:], i[:], AF.Exp),
                           reads=[bankb], writes=[scre_b[quad]],
                           tag=f"phiexp{nm}{gc}_{quad}")
                    p.emit("scalar", lambda e, o=scr_m[quad], i=bank:
                           e.activation(o[:], i[:], AF.Relu),
                           reads=[bankb], writes=[scrm_b[quad]],
                           tag=f"phirelu{nm}{gc}_{quad}")
                    p.emit("vector", lambda e, o=scr_e[quad], i=scr_e[quad]:
                           e.tensor_single_scalar(o[:], i[:], 1.0, ALU.min),
                           reads=[scre_b[quad]], writes=[scre_b[quad]],
                           tag=f"phimin{nm}{gc}_{quad}")
                    p.emit("gpsimd", lambda e,
                           o=dst[:, quad * 512:(quad + 1) * 512],
                           a=scr_m[quad], bb2=scr_e[quad]:
                           e.tensor_add(o, a[:], bb2[:]),
                           reads=[scrm_b[quad], scre_b[quad]], writes=[dstb],
                           tag=f"phiout{nm}{gc}_{quad}")
            if "qt" in dbg:
                p.dma("sync", lambda e, i=qt_c[r], g=gc: e.dma_start(
                    out=dbg["qt"][g * 128:(g + 1) * 128, :], in_=i[:]),
                    reads=[qt_b[r]], writes=[buf(f"dbg_qt_{r}")])
            if "kt" in dbg:
                p.dma("sync", lambda e, i=kt_c[r], g=gc: e.dma_start(
                    out=dbg["kt"][g * 128:(g + 1) * 128, :], in_=i[:]),
                    reads=[kt_b[r]], writes=[buf(f"dbg_kt_{r}")])

            for half in range(2):
                bank = pb[2 + half]
                bankb = pbb[2 + half]
                for d in range(8):
                    p.emit("tensor", lambda e, o=bank[:, 0:512],
                           l=xnT[r][:, d * 128:(d + 1) * 128],
                           rr=wv_sb[:, d * D + half * 512: d * D + half * 512 + 512],
                           s0=(d == 0), s1=(d == 7):
                           e.matmul(o, l, rr, start=s0, stop=s1),
                           reads=[wvb, xnT_b[r]], writes=[bankb],
                           tag=f"mmv{gc}_{half}_{d}")
                oap = va_c[r][:, half * 8 * 65:(half * 8 + 8) * 65] \
                    .rearrange("p (h v) -> p h v", v=65)[:, :, 0:64]
                p.emit("vector", lambda e, o=oap, i=bank[:, 0:512]:
                       e.tensor_copy(o, i),
                       reads=[bankb], writes=[va_b[r]], tag=f"vev{gc}_{half}")
            msap = va_c[r][:].rearrange("p (h v) -> p h v", v=65)[:, :, 64:65]
            p.emit("vector", lambda e, o=msap: e.memset(o, 1.0),
                   writes=[va_b[r]], tag=f"vones{gc}")
            if "vaug" in dbg:
                p.dma("sync", lambda e, i=va_c[r], g=gc: e.dma_start(
                    out=dbg["vaug"][g * 128:(g + 1) * 128, :], in_=i[:]),
                    reads=[va_b[r]], writes=[buf(f"dbg_va_{r}")])


        def stageA3(gc):
            s, c, r, st, stb = _ahead(gc)
            GB = gate_b
            p.emit("scalar", lambda e, o=g_u, i=pb[4][0:16, 0:128]:
                   e.activation(o[:], i, AF.Exp, bias=c_nbw[:], scale=-1.0),
                   reads=[pbb[4], nbwb], writes=[GB["u"]], tag=f"gu{gc}")
            p.emit("scalar", lambda e, o=g_lw1, i=g_u:
                   e.activation(o[:], i[:], AF.Ln, bias=1.0),
                   reads=[GB["u"]], writes=[GB["lw1"]], tag=f"glw{gc}")
            p.emit("vector", lambda e, o=g_t, i=pb[5][0:16, 0:128]:
                   e.tensor_scalar(o[:], i, c_bec[:], 80.0, ALU.add, ALU.min),
                   reads=[pbb[5], cb], writes=[GB["t"]], tag=f"gt{gc}")
            p.emit("scalar", lambda e, o=g_v, i=g_t:
                   e.activation(o[:], i[:], AF.Exp),
                   reads=[GB["t"]], writes=[GB["v"]], tag=f"gv{gc}")
            p.emit("scalar", lambda e, o=g_t1, i=g_v:
                   e.activation(o[:], i[:], AF.Ln, bias=1.0),
                   reads=[GB["v"]], writes=[GB["t1"]], tag=f"gt1{gc}")
            p.emit("scalar", lambda e, o=g_v, i=g_v:
                   e.activation(o[:], i[:], AF.Ln, bias=c_galg[:, 0:1],
                                scale=c_galg[:, 1:2]),
                   reads=[GB["v"], cb], writes=[GB["v"]], tag=f"gt2{gc}")
            p.emit("vector", lambda e, o=g_logg, a=g_v, b=g_t1:
                   e.tensor_sub(o[:], a[:], b[:]),
                   reads=[GB["v"], GB["t1"]], writes=[GB["logg"]],
                   tag=f"glogg{gc}")
            if c == 0:
                p.emit("vector", lambda e, o=cprev: e.memset(o[:], 0.0),
                       writes=[GB["cprev"]], tag=f"gcp{gc}")
            else:
                p.emit("vector", lambda e, o=cprev, i=carry:
                       e.tensor_copy(o[:], i[:]),
                       reads=[GB["carry"]], writes=[GB["cprev"]],
                       tag=f"gcp{gc}")
            p.emit("vector", lambda e, o=g_c[r], i=g_logg, ini=cprev:
                   e.tensor_tensor_scan(o[:], i[:], i[:], ini[:],
                                        ALU.add, ALU.bypass),
                   reads=[GB["logg"], GB["cprev"]], writes=[gc_b[r]],
                   tag=f"gscan{gc}")
            if "cc" in dbg:
                p.dma("sync", lambda e, i=g_c[r], g=gc: e.dma_start(
                    out=dbg["cc"][g * 16:(g + 1) * 16, :], in_=i[:]),
                    reads=[gc_b[r]], writes=[buf(f"dbg_cc_{r}")])
            p.emit("vector", lambda e, o=carry, i=g_c[r][:, 127:128]:
                   e.tensor_copy(o[:], i),
                   reads=[gc_b[r]], writes=[GB["carry"]], tag=f"gca{gc}")
            p.emit("vector", lambda e, o=g_b, a=g_lw1, bb_=g_c[r]:
                   e.scalar_tensor_tensor(o[:], a[:], -1.0, bb_[:],
                                          ALU.mult, ALU.subtract),
                   reads=[GB["lw1"], gc_b[r]], writes=[GB["b"]],
                   tag=f"gb{gc}")
            if "bb" in dbg:
                p.dma("sync", lambda e, i=g_b, g=gc: e.dma_start(
                    out=dbg["bb"][g * 16:(g + 1) * 16, :], in_=i[:]),
                    reads=[GB["b"]], writes=[buf(f"dbg_bb_{gc % 2}")])
            p.emit("tensor", lambda e, o=pb[4][:, 0:16], i=g_c[r]:
                   e.transpose(o, i[:], c_i16f[:]),
                   reads=[gc_b[r], cb], writes=[pbb[4]], tag=f"trc{gc}")
            p.emit("vector", lambda e, o=ct_c[r], i=pb[4][:, 0:16]:
                   e.tensor_copy(o[:], i),
                   reads=[pbb[4]], writes=[ct_b[r]], tag=f"trce{gc}")
            p.emit("tensor", lambda e, o=pb[5][:, 0:16], i=g_b:
                   e.transpose(o, i[:], c_i16f[:]),
                   reads=[GB["b"], cb], writes=[pbb[5]], tag=f"trb{gc}")
            p.emit("vector", lambda e, o=bt_c[r], i=pb[5][:, 0:16]:
                   e.tensor_copy(o[:], i),
                   reads=[pbb[5]], writes=[bt_b[r]], tag=f"trbe{gc}")
            p.emit("vector", lambda e, o=asrc, s1=cprev:
                   e.tensor_scalar(o[:], c_ones16[:], s1[:], None, ALU.mult),
                   reads=[cb, GB["cprev"]], writes=[GB["asrc"]],
                   tag=f"gas{gc}")
            p.emit("tensor", lambda e, o=pb[6][:, 0:16], i=asrc:
                   e.transpose(o, i[:], c_i16f[:]),
                   reads=[GB["asrc"], cb], writes=[pbb[6]], tag=f"trap{gc}")
            p.emit("vector", lambda e, o=qdt, a=ct_c[r], i=pb[6][:, 0:16]:
                   e.tensor_sub(o[:], a[:], i),
                   reads=[ct_b[r], pbb[6]], writes=[GB["qdt"]],
                   tag=f"qdt{gc}")
            p.emit("scalar", lambda e, o=qd_c[r], i=qdt:
                   e.activation(o[:], i[:], AF.Exp),
                   reads=[GB["qdt"]], writes=[qd_b[r]], tag=f"qde{gc}")
            p.emit("vector", lambda e, o=asrc, s1=carry:
                   e.tensor_scalar(o[:], c_ones16[:], s1[:], None, ALU.mult),
                   reads=[cb, GB["carry"]], writes=[GB["asrc"]],
                   tag=f"gas2{gc}")
            p.emit("tensor", lambda e, o=pb[7][:, 0:16], i=asrc:
                   e.transpose(o, i[:], c_i16f[:]),
                   reads=[GB["asrc"], cb], writes=[pbb[7]], tag=f"tran{gc}")
            p.emit("vector", lambda e, o=kdt, a=bt_c[r], i=pb[7][:, 0:16]:
                   e.tensor_add(o[:], a[:], i),
                   reads=[bt_b[r], pbb[7]], writes=[GB["kdt"]],
                   tag=f"kdt{gc}")
            p.emit("scalar", lambda e, o=kd_c[r], i=kdt:
                   e.activation(o[:], i[:], AF.Exp),
                   reads=[GB["kdt"]], writes=[kd_b[r]], tag=f"kde{gc}")
            p.emit("vector", lambda e, o=dcol, a=carry, bb_=cprev:
                   e.tensor_sub(o[:], a[:], bb_[:]),
                   reads=[GB["carry"], GB["cprev"]], writes=[GB["dcol"]],
                   tag=f"dcol{gc}")
            p.emit("vector", lambda e, o=asrc, s1=dcol:
                   e.tensor_scalar(o[:], c_ones16[:], s1[:], None,
                                   ALU.mult),
                   reads=[cb, GB["dcol"]], writes=[GB["asrc"]],
                   tag=f"lsrc{gc}")
            p.emit("tensor", lambda e, o=pb[6][:, 16:32], i=asrc:
                   e.transpose(o, i[:], c_i16f[:]),
                   reads=[GB["asrc"], cb], writes=[pbb[6]], tag=f"trl{gc}")
            p.emit("scalar", lambda e, o=lam_c[r], i=pb[6][:, 16:32]:
                   e.activation(o[:], i, AF.Exp),
                   reads=[pbb[6]], writes=[lam_b[r]], tag=f"lam{gc}")

        def stageA4(gc):
            s, c, r, st, stb = _ahead(gc)
            GSZ = 4
            KNOINT = False
            KNOU = False
            for hg in range(H // GSZ):
                heads = list(range(hg * GSZ, hg * GSZ + GSZ))

                def HH(fn):
                    for hh, h in enumerate(heads):
                        po = 64 * (h % 2)
                        fc = h // 2
                        fn(hh, h, po, fc,
                           kt_c[r][po:po + 64, fc * 128:(fc + 1) * 128],
                           qt_c[r][po:po + 64, fc * 128:(fc + 1) * 128],
                           va_c[r][:, h * 65:(h + 1) * 65])

                def sc_ap(hh):
                    return pb[hh % 2][:, (hh // 2) * 128:(hh // 2) * 128 + 128]

                def crm_ap(hh):
                    return pb[2 + hh % 2][:, (hh // 2) * 128:(hh // 2) * 128 + 128]

                def ktm_ap(hh):
                    c0 = (hh // 2) * 128
                    return pb[2 + hh % 2][:, c0:c0 + 32].bitcast(BF16)

                def po_ap(hh):
                    return pb[4 + hh % 2][:, (hh // 2) * 65:(hh // 2) * 65 + 65]

                def pi_ap(hh):
                    return pb[6 + hh % 2][:, (hh // 2) * 65:(hh // 2) * 65 + 65]


                def pu_ap(hh, po):
                    c0 = 260 + (hh // 2) * 65
                    return pb[6 + hh % 2][po:po + 64, c0:c0 + 65]

                def pub(hh):
                    return pbb[6 + hh % 2]

                def emit_tmpM(hg2):
                    for hh2, h2 in enumerate(range(hg2 * GSZ,
                                                   hg2 * GSZ + GSZ)):
                        sl2 = hh2 + 4 * (hg2 % 2)
                        p.emit("vector", lambda e, o=tmpM[sl2],
                               s1=ct_c[r][:, h2:h2 + 1]:
                               e.tensor_scalar(o[:], c_mpre[:], s1, None,
                                               ALU.add),
                               reads=[cb, ct_b[r]], writes=[tmpM_b[sl2]],
                               tag=f"tmpM{gc}_{h2}")

                if hg == 0:
                    emit_tmpM(0)
                HH(lambda hh, h, po, fc, ktsl, qtsl, vasl:
                   p.emit("tensor", lambda e, o=sc_ap(hh), l=ktsl, rr=qtsl:
                          e.matmul(o, l, rr, start=True, stop=True),
                          reads=[kt_b[r], qt_b[r]], writes=[pbb[hh % 2]],
                          tag=f"mmsc{gc}_{h}"))
                HH(lambda hh, h, po, fc, ktsl, qtsl, vasl:
                   p.emit("tensor", lambda e, o=ktm_ap(hh), i=ktsl,
                          idn=c_i128bf[po:po + 64, po:po + 64]:
                          e.transpose(o, i, idn),
                          reads=[kt_b[r], cb, dec_b[hh]],
                          writes=[pbb[2 + hh % 2]], tag=f"trktm{gc}_{h}"))
                HH(lambda hh, h, po, fc, ktsl, qtsl, vasl:
                   p.emit("vector", lambda e, o=kdb[hh], i=ktm_ap(hh),
                          sc=kd_c[r][:, h:h + 1]:
                          e.tensor_scalar(o[:], i, sc, None, ALU.mult),
                          reads=[pbb[2 + hh % 2], kd_b[r]], writes=[kdb_b[hh]],
                          tag=f"kdscale{gc}_{h}"))
                if c > 0 and not KNOINT:
                    HH(lambda hh, h, po, fc, ktsl, qtsl, vasl:
                       p.emit("tensor", lambda e, o=pi_ap(hh), l=qtsl,
                              rr=s_bf[po:po + 64, fc * 65:(fc + 1) * 65]:
                              e.matmul(o, l, rr, start=True, stop=True),
                              reads=[qt_b[r], sbf_b[h]],
                              writes=[pbb[6 + hh // 4]],
                              tag=f"mmint{gc}_{h}"))
                KNOU or HH(lambda hh, h, po, fc, ktsl, qtsl, vasl:
                   p.emit("tensor", lambda e, o=pu_ap(hh, po), l=kdb[hh],
                          rr=vasl: e.matmul(o, l[:], rr, start=True, stop=True),
                          reads=[kdb_b[hh], va_b[r]], writes=[pbb[6 + hh // 4]],
                          tag=f"mmu{gc}_{h}"))
                if c > 0 and not KNOINT:
                    HH(lambda hh, h, po, fc, ktsl, qtsl, vasl:
                       p.emit("vector", lambda e, o=r1s[hh], i=pi_ap(hh),
                              sc=qd_c[r][:, h:h + 1]:
                              e.tensor_scalar(o[:], i, sc, None, ALU.mult),
                              reads=[pbb[6 + hh // 4], qd_b[r]],
                              writes=[r1_b[hh]], tag=f"r1{gc}_{h}"))
                if hg + 1 < H // GSZ:
                    emit_tmpM(hg + 1)
                HH(lambda hh, h, po, fc, ktsl, qtsl, vasl:
                   p.emit("tensor", lambda e, o=crm_ap(hh),
                          i=tmpM[hh + 4 * (hg % 2)]:
                          e.transpose(o, i[:], c_i128f[:]),
                          reads=[tmpM_b[hh + 4 * (hg % 2)], cb],
                          writes=[pbb[2 + hh % 2]],
                          tag=f"trcrm{gc}_{h}"))
                HH(lambda hh, h, po, fc, ktsl, qtsl, vasl:
                   p.emit("scalar", lambda e, o=dec[hh], i=crm_ap(hh),
                          bi=bt_c[r][:, h:h + 1]:
                          e.activation(o[:], i, AF.Exp, bias=bi),
                          reads=[pbb[2 + hh % 2], bt_b[r]], writes=[dec_b[hh]],
                          tag=f"dec{gc}_{h}"))
                HH(lambda hh, h, po, fc, ktsl, qtsl, vasl:
                   p.emit("vector", lambda e, o=effT[hh], a=sc_ap(hh),
                          bb_=dec[hh]: e.tensor_mul(o[:], a, bb_[:]),
                          reads=[pbb[hh % 2], dec_b[hh]], writes=[effT_b[hh]],
                          tag=f"eff{gc}_{h}"))
                HH(lambda hh, h, po, fc, ktsl, qtsl, vasl:
                   p.emit("tensor", lambda e, o=po_ap(hh), l=effT[hh], rr=vasl:
                          e.matmul(o, l[:], rr, start=True, stop=True),
                          reads=[effT_b[hh], va_b[r]], writes=[pbb[4 + hh % 2]],
                          tag=f"mmdiag{gc}_{h}"))
                if c > 0 and not KNOINT:
                    HH(lambda hh, h, po, fc, ktsl, qtsl, vasl:
                       p.emit("vector", lambda e, o=comb[hh], a=po_ap(hh),
                              bb_=r1s[hh]: e.tensor_add(o[:], a, bb_[:]),
                              reads=[pbb[4 + hh % 2], r1_b[hh]],
                              writes=[comb_b[hh]], tag=f"comb{gc}_{h}"))
                if c == 0 or KNOINT:
                    HH(lambda hh, h, po, fc, ktsl, qtsl, vasl:
                       p.emit("vector", lambda e, o=comb[hh], i=po_ap(hh):
                              e.tensor_copy(o[:], i),
                              reads=[pbb[4 + hh % 2]], writes=[comb_b[hh]],
                              tag=f"comb{gc}_{h}"))
                if KNOU:
                    pass
                elif c == 0:
                    HH(lambda hh, h, po, fc, ktsl, qtsl, vasl:
                       p.emit("vector", lambda e,
                              o=s_f32[po:po + 64, fc * 65:(fc + 1) * 65],
                              i=pu_ap(hh, po): e.tensor_copy(o, i),
                              reads=[pbb[6 + hh // 4]], writes=[sf_b[h]],
                              tag=f"sini{gc}_{h}"))
                else:
                    HH(lambda hh, h, po, fc, ktsl, qtsl, vasl:
                       p.emit("vector", lambda e,
                              o=s_f32[po:po + 64, fc * 65:(fc + 1) * 65],
                              sc=lam_c[r][po:po + 64, h:h + 1]:
                              e.tensor_scalar(o, o, sc, None, ALU.mult),
                              reads=[lam_b[r], sf_b[h]], writes=[sf_b[h]],
                              tag=f"slam{gc}_{h}"))
                    KNOU or \
                    HH(lambda hh, h, po, fc, ktsl, qtsl, vasl:
                       p.emit("vector", lambda e,
                              o=s_f32[po:po + 64, fc * 65:(fc + 1) * 65],
                              u=pu_ap(hh, po): e.tensor_add(o, o, u),
                              reads=[pbb[6 + hh // 4], sf_b[h]],
                              writes=[sf_b[h]], tag=f"supd{gc}_{h}"))
                HH(lambda hh, h, po, fc, ktsl, qtsl, vasl:
                   p.emit("gpsimd", lambda e,
                          o=s_bf[po:po + 64, fc * 65:(fc + 1) * 65],
                          i=s_f32[po:po + 64, fc * 65:(fc + 1) * 65]:
                          e.tensor_copy(o, i),
                          reads=[sf_b[h]], writes=[sbf_b[h]],
                          tag=f"scast{gc}_{h}"))
                HH(lambda hh, h, po, fc, ktsl, qtsl, vasl:
                   p.emit("vector", lambda e, o=recc[hh], i=comb[hh][:, 64:65]:
                          e.tensor_single_scalar(o[:], i, 1e-6, ALU.add),
                          reads=[comb_b[hh]], writes=[recc_b[hh]],
                          tag=f"eps{gc}_{h}"))
                HH(lambda hh, h, po, fc, ktsl, qtsl, vasl:
                   p.emit("vector", lambda e, o=recc[hh]:
                          e.reciprocal(o[:], o[:]),
                          reads=[recc_b[hh]], writes=[recc_b[hh]],
                          tag=f"rec{gc}_{h}"))
                HH(lambda hh, h, po, fc, ktsl, qtsl, vasl:
                   p.emit("scalar", lambda e, o=atm[r][:, h * 64:(h + 1) * 64],
                          i=comb[hh][:, 0:64], sc=recc[hh]:
                          e.activation(o, i, AF.Identity, scale=sc[:]),
                          reads=[comb_b[hh], recc_b[hh]], writes=[atm_b[r]],
                          tag=f"osc{gc}_{h}"))

            if "attn" in dbg:
                p.dma("sync", lambda e, i=atm[r], g=gc: e.dma_start(
                    out=dbg["attn"][g * 128:(g + 1) * 128, :], in_=i[:]),
                    reads=[atm_b[r]], writes=[buf(f"dbg_at_{r}")])

        def stageA5(gc):
            s, c, r, st, stb = _ahead(gc)
            for d in range(8):
                tp = pb[2 + (d % 2)][:, 0:64].bitcast(BF16)
                tpb = pbb[2 + (d % 2)]
                p.emit("tensor", lambda e, o=tp,
                       i=atm[r][:, d * 128:(d + 1) * 128]:
                       e.transpose(o, i, c_i128bf[:]),
                       reads=[atm_b[r], cb], writes=[tpb],
                       tag=f"trat{gc}_{d}")
                p.emit("vector", lambda e, o=att[r][:, d * 128:(d + 1) * 128],
                       i=tp: e.tensor_copy(o, i),
                       reads=[tpb], writes=[att_b[r]], tag=f"trate{gc}_{d}")
            for half in range(2):
                bank = pb[half]
                bankb = pbb[half]
                for d in range(8):
                    p.emit("tensor", lambda e, o=bank[:, 0:512],
                           l=att[r][:, d * 128:(d + 1) * 128],
                           rr=wo_sb[:, d * D + half * 512: d * D + half * 512 + 512],
                           s0=(d == 0), s1=(d == 7):
                           e.matmul(o, l, rr, start=s0, stop=s1),
                           reads=[att_b[r], wob], writes=[bankb],
                           tag=f"mmwo{gc}_{half}_{d}")


        def stageA5b(gc):
            s, c, r, st, stb = _ahead(gc)
            for half in range(2):
                bank = pb[half]
                bankb = pbb[half]
                p.emit("vector", lambda e,
                       o=x1s[r][:, half * 512:half * 512 + 512],
                       a=bank[:, 0:512],
                       bb_=st[:, half * 512:half * 512 + 512]:
                       e.tensor_add(o, a, bb_),
                       reads=[bankb, stb], writes=[x1s_b[r]],
                       tag=f"x1add{gc}_{half}")
            p.dma("gpsimd", lambda e, i=x1s[r], g=gc: e.dma_start(
                out=x1scr[g * 128:(g + 1) * 128, :], in_=i[:]),
                reads=[x1s_b[r]], writes=[buf(f"x1w{r}")],
                tag=f"dma_x1w{gc}")
            p.emit("scalar", lambda e, o=atm[r], i=x1s[r],
                   a=l2ss[:, gc:gc + 1]:
                   e.activation(o[:], i[:], AF.Square, accum_out=a),
                   reads=[x1s_b[r]], writes=[atm_b[r], l2ssb],
                   tag=f"l2sq{gc}")
            p.emit("vector", lambda e, o=l2sm[:, gc:gc + 1], i=x1s[r]:
                   e.reduce_sum(o, i[:], axis=mybir.AxisListType.X),
                   reads=[x1s_b[r]], writes=[l2smb], tag=f"l2sum{gc}")
            if "x1" in dbg:
                p.dma("sync", lambda e, i=x1s[r], g=gc: e.dma_start(
                    out=dbg["x1"][g * 128:(g + 1) * 128, :], in_=i[:]),
                    reads=[x1s_b[r]], writes=[buf(f"dbg_x1_{r}")])

        NCA = NCK * SEQS
        for gc in range(NCA):
            if gc == 0:
                stageA1a(0)
                stageA1b(0)
            if gc + 1 < NCA:
                stageA1a(gc + 1)
            stageA2(gc)
            if gc > 0:
                stageA5(gc - 1)
            stageA3(gc)
            if gc > 0:
                stageA5b(gc - 1)
            if gc + 1 < NCA:
                stageA1b(gc + 1)
            stageA4(gc)
        stageA5(NCA - 1)
        stageA5b(NCA - 1)

        p.emit("vector", lambda e: e.tensor_single_scalar(
               l2sm[:], l2sm[:], 1.0 / D, ALU.mult),
               reads=[l2smb], writes=[l2smb], tag="l2mean")
        p.emit("vector", lambda e: e.tensor_tensor(
               l2tmp[:], l2sm[:], l2sm[:], ALU.mult),
               reads=[l2smb], writes=[l2tmpb], tag="l2msq")
        p.emit("vector", lambda e: e.scalar_tensor_tensor(
               l2ss[:], l2ss[:], 1.0 / D, l2tmp[:], ALU.mult, ALU.subtract),
               reads=[l2ssb, l2tmpb], writes=[l2ssb], tag="l2var")
        p.emit("vector", lambda e: e.tensor_single_scalar(
               l2ss[:], l2ss[:], 1e-5, ALU.add),
               reads=[l2ssb], writes=[l2ssb], tag="l2vev")
        p.emit("scalar", lambda e: e.activation(l2ss[:], l2ss[:], AF.Ln),
               reads=[l2ssb], writes=[l2ssb], tag="l2ln")
        p.emit("vector", lambda e: e.tensor_single_scalar(
               l2ss[:], l2ss[:], -0.5, ALU.mult),
               reads=[l2ssb], writes=[l2ssb], tag="l2nh")
        p.emit("scalar", lambda e: e.activation(l2inv[:], l2ss[:], AF.Exp),
               reads=[l2ssb], writes=[l2invb], tag="l2ex")
        p.emit("vector", lambda e: e.scalar_tensor_tensor(
               l2nmu[:], l2sm[:], -1.0, l2inv[:], ALU.mult, ALU.mult),
               reads=[l2smb, l2invb], writes=[l2nmub], tag="l2nmu")

        p.barrier()
        sA.close()

    fc_sb = sb("fc_sb", [128, 8 * DF], BF16)
    pj_sb = sb("pj_sb", [128, 32 * D], BF16)
    ht_sb = sb("ht_sb", [128, 32 * 256], BF16)
    xn2T = [sb(f"xn2T{i}", [128, 8 * 256], BF16) for i in range(2)]
    x1st = [sb(f"x1st{i}", [128, D], BF16) for i in range(6)]
    xn2b = [sb(f"xn2b{i}", [128, D], BF16) for i in range(4)]
    y_sb = [sb(f"y_sb{i}", [128, 512], F32) for i in range(4)]
    sn_s = [sb(f"sn_s{i}", [128, 512], F32) for i in range(4)]
    sn_a = [sb(f"sn_a{i}", [128, 512], BF16) for i in range(4)]
    sn_m = [sb(f"sn_m{i}", [128, 512], F32) for i in range(4)]
    rh_sb = [sb(f"rh_sb{i}", [128, 256], F32) for i in range(4)]

    fcb, pjb = [], []
    for i in range(8):
        inap = fct.rearrange("(c p) f -> p c f", p=128)[:, :, i * 512:(i + 1) * 512]
        outap = fc_sb[:].rearrange("p (c f) -> p c f", f=DF)[:, :, i * 512:(i + 1) * 512]
        p.dma("gpsimd", lambda e, o=outap, s=inap: e.dma_start(out=o, in_=s),
              writes=[buf(f"fc{i}")], tag=f"dma_fc{i}")
        fcb.append(buf(f"fc{i}"))
    for i in range(8):
        inap = pjt.rearrange("(c p) f -> p c f", p=128)[:, i * 4:(i + 1) * 4, :]
        outap = pj_sb[:].rearrange("p (c f) -> p c f", f=D)[:, i * 4:(i + 1) * 4, :]
        p.dma("gpsimd", lambda e, o=outap, s=inap: e.dma_start(out=o, in_=s),
              writes=[buf(f"pj{i}")], tag=f"dma_pj{i}")
        pjb.append(buf(f"pj{i}"))

    htb = buf("ht")
    xn2T_b = [buf("xn2T0"), buf("xn2T1")]
    x1st_b = [buf(f"x1st{i}") for i in range(6)]
    xn2b_b = [buf(f"xn2b{i}") for i in range(4)]
    y_b = [buf(f"y{i}") for i in range(4)]
    sns_b = [buf(f"sns{i}") for i in range(4)]
    sna_b = [buf(f"sna{i}") for i in range(4)]
    snm_b = [buf(f"snm{i}") for i in range(4)]
    rh_b = [buf(f"rh{i}") for i in range(4)]
    x1r_deps = [buf("x1w0"), buf("x1w1")]

    def stageP1a(grp):
        gr = grp % 2
        for j in range(2):
            cc = grp * 2 + j
            stb_ = x1st_b[cc % 6]
            st_ = x1st[cc % 6]
            p.dma("scalar", lambda e, o=st_, g=cc: e.dma_start(
                out=o[:], in_=x1scr[g * 128:(g + 1) * 128, :]),
                reads=x1r_deps, writes=[stb_], tag=f"dma_x1r{cc}")
            p.emit("scalar", lambda e, o=xn2b[cc % 4][:], i=st_[:],
                   sc=l2inv[:, cc:cc + 1], b2=l2nmu[:, cc:cc + 1]:
                   e.activation(o, i, AF.Identity, scale=sc, bias=b2),
                   reads=[stb_, l2invb, l2nmub], writes=[xn2b_b[cc % 4]],
                   tag=f"ln2app{cc}")

    def stageP1b(grp):
        gr = grp % 2
        for j in range(2):
            cc = grp * 2 + j
            for d in range(8):
                tp = pb[2 + (d % 2)][:, 0:64].bitcast(BF16)
                tpb = pbb[2 + (d % 2)]
                p.emit("tensor", lambda e, o=tp,
                       i=xn2b[cc % 4][:, d * 128:(d + 1) * 128]:
                       e.transpose(o, i, c_i128bf[:]),
                       reads=[xn2b_b[cc % 4], cb], writes=[tpb],
                       tag=f"tr2_{cc}_{d}")
                p.emit("scalar", lambda e,
                       o=xn2T[gr][:, d * 256 + j * 128: d * 256 + j * 128 + 128],
                       i=tp: e.activation(o, i, AF.Copy),
                       reads=[tpb], writes=[xn2T_b[gr]], tag=f"tr2e{cc}_{d}")

    def stageP2(grp):
        gr = grp % 2
        for fcx in range(32):
            bank = pb[fcx % 4]
            bankb = pbb[fcx % 4]
            for d in range(8):
                p.emit("tensor", lambda e, o=bank[:, 0:256],
                       l=fc_sb[:, d * DF + fcx * 128: d * DF + (fcx + 1) * 128],
                       rr=xn2T[gr][:, d * 256:(d + 1) * 256],
                       s0=(d == 0), s1=(d == 7):
                       e.matmul(o, l, rr, start=s0, stop=s1),
                       reads=[fcb[fcx // 4], xn2T_b[gr]], writes=[bankb],
                       tag=f"mmh{grp}_{fcx}_{d}")
            p.emit("scalar", lambda e, o=rh_sb[fcx % 4], i=bank[:, 0:256]:
                   e.activation(o[:], i, AF.Relu),
                   reads=[bankb], writes=[rh_b[fcx % 4]], tag=f"relu{grp}_{fcx}")
            engm = "gpsimd"
            p.emit(engm, lambda e, o=ht_sb[:, fcx * 256:(fcx + 1) * 256],
                   i=rh_sb[fcx % 4]: e.tensor_mul(o, i[:], i[:]),
                   reads=[rh_b[fcx % 4]], writes=[htb], tag=f"sq{grp}_{fcx}")

    def stageP3(grp):
        gr = grp % 2
        tiles = []
        for j in range(2):
            cc = grp * 2 + j
            for half in range(2):
                t = 2 * j + half
                bank = pb[4 + t]
                bankb = pbb[4 + t]
                for fcx in range(32):
                    p.emit("tensor", lambda e, o=bank[:, 0:512],
                           l=ht_sb[:, fcx * 256 + j * 128: fcx * 256 + j * 128 + 128],
                           rr=pj_sb[:, fcx * D + half * 512: fcx * D + half * 512 + 512],
                           s0=(fcx == 0), s1=(fcx == 31):
                           e.matmul(o, l, rr, start=s0, stop=s1),
                           reads=[htb, pjb[fcx // 4]],
                           writes=[bankb], tag=f"mmy{cc}_{half}_{fcx}")
                tiles.append((t, cc, j, half, bank, bankb))

        def each(fn):
            for (tp2, cc, j, half, bank, bankb) in tiles:
                fn(tp2, cc, j, half, bank, bankb)

        each(lambda tp2, cc, j, half, bank, bankb:
             p.emit("vector", lambda e, o=y_sb[tp2], i=bank[:, 0:512]:
                    e.tensor_copy(o[:], i),
                    reads=[bankb], writes=[y_b[tp2]], tag=f"yev{cc}_{half}"))

        def sg(src_list):
            each(lambda tp2, cc, j, half, bank, bankb:
                 p.emit("vector", lambda e, o=sn_s[tp2], i=src_list[tp2]:
                        e.tensor_single_scalar(o[:], i[:], 1.0, ALU.is_gt),
                        reads=[(y_b if src_list is y_sb else snm_b)[tp2]],
                        writes=[sns_b[tp2]], tag=f"sg{cc}_{half}"))

        sg(y_sb)
        each(lambda tp2, cc, j, half, bank, bankb:
             p.emit("vector", lambda e, o=sn_a[tp2], i=sn_s[tp2]:
                    e.tensor_copy(o[:], i[:]),
                    reads=[sns_b[tp2]], writes=[sna_b[tp2]], tag=f"sac{cc}_{half}"))
        each(lambda tp2, cc, j, half, bank, bankb:
             p.emit("vector", lambda e, o=sn_m[tp2], i=y_sb[tp2],
                    s2=sn_s[tp2]:
                    e.scalar_tensor_tensor(o[:], i[:], 1.9, s2[:],
                                           ALU.mult, ALU.subtract),
                    reads=[y_b[tp2], sns_b[tp2]], writes=[snm_b[tp2]],
                    tag=f"sn3{cc}_{half}"))
        for k2 in range(3):
            sg(sn_m)
            each(lambda tp2, cc, j, half, bank, bankb:
                 p.emit("vector", lambda e, o=sn_a[tp2], i=sn_s[tp2]:
                        e.tensor_add(o[:], o[:], i[:]),
                        reads=[sns_b[tp2], sna_b[tp2]], writes=[sna_b[tp2]],
                        tag=f"sad{k2}_{cc}_{half}"))
            each(lambda tp2, cc, j, half, bank, bankb:
                 p.emit("vector", lambda e, o=sn_m[tp2], s2=sn_s[tp2]:
                        e.scalar_tensor_tensor(o[:], o[:], 0.9, s2[:],
                                               ALU.mult, ALU.subtract),
                        reads=[sns_b[tp2], snm_b[tp2]], writes=[snm_b[tp2]],
                        tag=f"sw{k2}_{cc}_{half}"))
            each(lambda tp2, cc, j, half, bank, bankb:
                 p.emit("vector", lambda e, o=sn_m[tp2], y2=y_sb[tp2]:
                        e.tensor_add(o[:], o[:], y2[:]),
                        reads=[y_b[tp2], snm_b[tp2]], writes=[snm_b[tp2]],
                        tag=f"swy{k2}_{cc}_{half}"))
        sg(sn_m)
        each(lambda tp2, cc, j, half, bank, bankb:
             p.emit("vector", lambda e, o=sn_a[tp2], i=sn_s[tp2]:
                    e.tensor_add(o[:], o[:], i[:]),
                    reads=[sns_b[tp2], sna_b[tp2]], writes=[sna_b[tp2]],
                    tag=f"sa9{cc}_{half}"))
        each(lambda tp2, cc, j, half, bank, bankb:
             p.emit("vector", lambda e, o=sn_m[tp2], i=sn_a[tp2]:
                    e.tensor_single_scalar(o[:], i[:], 0.2, ALU.mult),
                    reads=[sna_b[tp2]], writes=[snm_b[tp2]], tag=f"so1{cc}_{half}"))
        def snout(tp2, cc, j, half, bank, bankb):
            p.emit("vector", lambda e, o=y_sb[tp2][:, 0:512], a=sn_m[tp2],
                   x1=x1st[cc % 6][:, half * 512:half * 512 + 512]:
                   e.tensor_add(o, a[:], x1),
                   reads=[snm_b[tp2], x1st_b[cc % 6]],
                   writes=[y_b[tp2]], tag=f"snout{cc}_{half}")
            p.dma("sync", lambda e, i=y_sb[tp2], g=cc, hf=half:
                  e.dma_start(out=out_t[g * 128:(g + 1) * 128,
                                        hf * 512:hf * 512 + 512], in_=i[:]),
                  reads=[y_b[tp2]], writes=[buf(f"outw{tp2}")],
                  tag=f"dma_out{cc}_{half}")
        each(snout)

    NG = NTOK // 256
    for grp in range(NG):
        if grp == 0:
            stageP1a(0)
            stageP1b(0)
        stageP2(grp)
        if grp + 1 < NG:
            stageP1a(grp + 1)
            stageP1b(grp + 1)
        stageP3(grp)

    fin_reads = [buf(f"outw{i}") for i in range(4)]
    fin_reads += [B[k] for k in B if k.startswith("dbg_")]
    p.emit("sync", lambda e: None, reads=fin_reads, tag="final")

    p.build(nc, verbose=verbose)
    stack.close()
    return nc


_CACHE = {}


def _get_program(debug=()):
    key = tuple(sorted(debug))
    if key not in _CACHE:
        _CACHE[key] = build_program(debug=key)
    return _CACHE[key]


def kernel(L0_attn_x, L0_attn_Wq, L0_attn_Wk, L0_attn_Wv, L0_attn_Wo,
           L0_attn_Ww, L0_attn_bw, L0_attn_We, L0_attn_be,
           L0_ffn_W_fc, L0_ffn_W_proj, _debug=(), _return_raw=False):
    x = np.ascontiguousarray(np.asarray(L0_attn_x, dtype=np.float32))
    com = {
        "wqt": np.ascontiguousarray(np.asarray(L0_attn_Wq, np.float32).T),
        "wkt": np.ascontiguousarray(np.asarray(L0_attn_Wk, np.float32).T),
        "wvt": np.ascontiguousarray(np.asarray(L0_attn_Wv, np.float32).T),
        "wot": np.ascontiguousarray(np.asarray(L0_attn_Wo, np.float32).T),
        "wwt": np.ascontiguousarray(np.asarray(L0_attn_Ww, np.float32).T),
        "wet": np.ascontiguousarray(np.asarray(L0_attn_We, np.float32).T),
        "bwc": np.ascontiguousarray(np.asarray(L0_attn_bw, np.float32)
                                    .reshape(H, 1)),
        "bec": np.ascontiguousarray(np.asarray(L0_attn_be, np.float32)
                                    .reshape(H, 1)),
        "fct": np.ascontiguousarray(np.asarray(L0_ffn_W_fc, np.float32).T),
        "pjt": np.ascontiguousarray(np.asarray(L0_ffn_W_proj, np.float32).T),
        "i128bf": np.eye(128, dtype=BF),
        "i128f": np.eye(128, dtype=np.float32),
        "i16f": np.eye(16, dtype=np.float32),
        "mpre": np.triu(np.full((128, 128), -1e30, np.float32), 1),
        "ones16": np.ones((16, 128), np.float32),
        "galg": np.tile(np.array([[np.float32(0.95) + np.float32(1e-8),
                                   1e-8]], np.float32), (16, 1)),
    }
    nc = _get_program(_debug)
    in_maps = []
    for k in range(N_CORES):
        m = dict(com)
        m["xin"] = np.ascontiguousarray(
            x[SEQS * k:SEQS * (k + 1)].reshape(NTOK, D))
        in_maps.append(m)
    res = run_bass_kernel_spmd(nc, in_maps, core_ids=list(range(N_CORES)))
    if _return_raw:
        return res
    out = np.concatenate(
        [r["out"].reshape(SEQS, L, D) for r in res.results], axis=0)
    return out
